# version 06
# speedup vs baseline: 1.0439x; 1.0255x over previous
import contextlib
import dataclasses
import numpy as np
import ml_dtypes

import concourse.bass as bass
import concourse.mybir as mybir
from concourse.bass_utils import run_bass_kernel_spmd

F32 = mybir.dt.float32
BF16 = mybir.dt.bfloat16
I32 = mybir.dt.int32
AF = mybir.ActivationFunctionType
PM2 = mybir.MatmulPerfMode.DoubleRow
ALU = mybir.AluOpType
AX = mybir.AxisListType
DS = bass.ds

L_FULL = 16384
V = 4096
DIM = 512
DFF = 2048
DEPTH = 4
EPS = 1e-5
NIN = 8192
NOUT = 2048
CC = 126
NCH = 66
GRID = NCH * CC
ATILE = 256
NAT = NIN // ATILE
NVC = V // 128
NDC = DIM // 128
NFC = DFF // 128
MODP = 2 * CC - 1
NREP = 33
BIGN = NREP * MODP
TGRP = 32
NTG = DIM // TGRP
NE = 28
RK = 8
ER = NE * RK
SCALING = 2.0


class Sem:
    def __init__(self, nc, name):
        self.h = nc.alloc_semaphore(name=name)
        self.n = 0

    def inc(self, instr, k=1):
        instr.then_inc(self.h, k)
        self.n += k
        return self.n

    def dma(self, instr):
        return self.inc(instr, 16)


class SemRing:

    def __init__(self, nc, name, k):
        self.sem = Sem(nc, name)
        self.k = k
        self.idx = 0

    def issue(self, eng, mk):
        self.sem.dma(mk())
        self.idx += 1
        return self.idx - 1

    def wait_issue(self, eng, j):
        eng.wait_ge(self.sem.h, 16 * (j + 1))

    def wait_all(self, eng):
        if self.idx:
            eng.wait_ge(self.sem.h, 16 * self.idx)


RACE_CHECK = False


def build_program(n_layers=DEPTH, stop_after=None, dumps=(), nat=NAT):
    nc = bass.Bass("TRN2", target_bir_lowering=False, debug=False,
                   detect_race_conditions=RACE_CHECK,
                   dynamic_dma_scratch_size=32768)
    es = contextlib.ExitStack()

    def din(name, shape, dtype=F32):
        return nc.dram_tensor(name, shape, dtype, kind="ExternalInput").ap()

    x_inner = din("x_inner", [NIN, V])
    x_outer = din("x_outer", [NOUT, V])
    Win_i = din("Win_i", [DIM, V])
    bin_i = din("bin_i", [1, DIM])
    ln1w_i = din("ln1w_i", [DEPTH, DIM]); ln1b_i = din("ln1b_i", [DEPTH, DIM])
    phzr_i = din("phzr_i", [DEPTH, DIM]); phzi_i = din("phzi_i", [DEPTH, DIM])
    lcr_i = din("lcr_i", [DEPTH, DIM]); lci_i = din("lci_i", [DEPTH, DIM])
    scl_i = din("scl_i", [DEPTH, DIM])
    ln2w_i = din("ln2w_i", [DEPTH, DIM]); ln2b_i = din("ln2b_i", [DEPTH, DIM])
    W1_i = din("W1_i", [DEPTH, DFF, DIM]); b1_i = din("b1_i", [DEPTH, DFF])
    W2_i = din("W2_i", [DEPTH, DIM, DFF]); b2_i = din("b2_i", [DEPTH, DIM])
    lnlw_i = din("lnlw_i", [1, DIM]); lnlb_i = din("lnlb_i", [1, DIM])
    Wout_i = din("Wout_i", [V, DIM]); bout_i = din("bout_i", [1, V])
    baseW_i = din("baseW_i", [V, V])
    gateW_i = din("gateW_i", [32, V])
    loraA_i = din("loraA_i", [ER, V])
    loraB_i = din("loraB_i", [NE, V, RK])
    rolef_i = din("rolef_i", [1, 8])
    flags_i = din("flags_i", [1, 16], I32)
    myflag_i = din("myflag_i", [1, 16], I32)

    out_t = nc.dram_tensor("out", [NOUT, V], F32, kind="ExternalOutput").ap()
    dump_t = {}
    for nm, shape in (("h", [NIN, DIM]), ("y", [GRID, DIM]), ("yf", [GRID, DIM]),
                      ("si", [128, 8 * NCH]), ("x1", [NOUT, V]), ("z2", [NOUT, V]),
                      ("ffn", [NOUT, V])):
        if nm in dumps:
            dump_t[nm] = nc.dram_tensor("dbg_" + nm, shape, F32,
                                        kind="ExternalOutput").ap()

    h_dram = nc.dram_tensor("h_dram", [NIN, DIM], F32, kind="Internal").ap()
    y_dram = nc.dram_tensor("y_dram", [GRID, DIM], BF16, kind="Internal").ap()
    zzbf_dram = nc.dram_tensor("zzbf_dram", [NIN, DIM], BF16, kind="Internal").ap()
    yf_dram = nc.dram_tensor("yf_dram", [GRID, DIM], F32, kind="Internal").ap()
    winbf_dram = nc.dram_tensor("winbf_dram", [DIM, V], BF16, kind="Internal").ap()
    xbf_dram = nc.dram_tensor("xbf_dram", [NIN, V], BF16, kind="Internal").ap()
    w1bf_dram = nc.dram_tensor("w1bf_dram", [DEPTH, DFF, DIM], BF16, kind="Internal").ap()
    w2bf_dram = nc.dram_tensor("w2bf_dram", [DEPTH, DIM, DFF], BF16, kind="Internal").ap()
    woutbf_dram = nc.dram_tensor("woutbf_dram", [V, DIM], BF16, kind="Internal").ap()
    basebf_dram = nc.dram_tensor("basebf_dram", [V, V], BF16, kind="Internal").ap()
    gatebf_dram = nc.dram_tensor("gatebf_dram", [32, V], BF16, kind="Internal").ap()
    lorabf_dram = nc.dram_tensor("lorabf_dram", [ER, V], BF16, kind="Internal").ap()
    powrow_dram = nc.dram_tensor("powrow_dram", [DIM, MODP], BF16, kind="Internal").ap()
    big_dram = nc.dram_tensor("big_dram", [DIM, BIGN], BF16, kind="Internal").ap()
    misc_dram = nc.dram_tensor("misc_dram", [128, 64], F32, kind="Internal").ap()

    sh_h = nc.dram_tensor("sh_h", [L_FULL, DIM], F32, kind="Internal",
                          addr_space="Shared").ap()
    sh_carry = nc.dram_tensor("sh_carry", [DEPTH, 2, 2, DIM], F32, kind="Internal",
                              addr_space="Shared").ap()
    sh_flags = nc.dram_tensor("sh_flags", [1, 16], I32, kind="Internal",
                              addr_space="Shared").ap()

    def sbuf(name, shape, dtype):
        return es.enter_context(nc.sbuf_tensor(name, shape, dtype))

    ident = sbuf("ident", [128, 128], BF16)
    onesf = sbuf("onesf", [65, 128], F32)
    onesbf = sbuf("onesbf", [1, 128], BF16)
    lnlw_row = sbuf("lnlw_row", [1, DIM], F32)
    lnlb_row = sbuf("lnlb_row", [1, DIM], F32)
    b1col = sbuf("b1col", [128, NFC], F32)
    b2_row = sbuf("b2_row", [1, DIM], BF16)
    rolef_sb = sbuf("rolef_sb", [1, 8], F32)
    flagv_sb = sbuf("flagv_sb", [1, 16], I32)
    myflag_sb = sbuf("myflag_sb", [1, 16], I32)
    polltmp = sbuf("polltmp", [1, 2], I32)
    wsel = sbuf("wsel", [128, 4], F32)
    epscol = sbuf("epscol", [128, 1], F32)

    NBC = 5
    NBT = NIN // 128
    NST = NBT // 4
    bcast_pool = sbuf("bcast_pool", [128, NBC * DIM], F32)
    BC_SCL, BC_L1W, BC_L1B, BC_L2W, BC_L2B = range(5)

    def bc(i):
        return bcast_pool[:, i * DIM:(i + 1) * DIM]

    stat = sbuf("stat", [128, 160], F32)

    esA = contextlib.ExitStack()

    def sbufA(name, shape, dtype):
        return esA.enter_context(nc.sbuf_tensor(name, shape, dtype))

    bin_row = sbufA("bin_row", [1, DIM], F32)
    wcs_row = sbufA("wcs_row", [1, DIM], F32)
    abc = sbufA("abc", [128, 2 * DIM], F32)
    xa = sbufA("xa", [128, 2 * 2 * V], F32)
    xsq = sbufA("xsq", [128, V], F32)
    xaT = sbufA("xaT", [128, 2 * NVC * ATILE], BF16)
    winT = sbufA("winT", [128, NVC * DIM], BF16)
    habuf = sbufA("habuf", [128, 2 * 2 * DIM], F32)

    ptA = es.enter_context(nc.psum_tensor("ptA", [128, 1024], BF16))
    ptB = es.enter_context(nc.psum_tensor("ptB", [128, 1024], BF16))
    pmA = es.enter_context(nc.psum_tensor("pmA", [128, 512], F32))
    pmB = es.enter_context(nc.psum_tensor("pmB", [128, 512], F32))
    pmC = es.enter_context(nc.psum_tensor("pmC", [128, 512], F32))
    pmD = es.enter_context(nc.psum_tensor("pmD", [128, 512], F32))
    pmE = es.enter_context(nc.psum_tensor("pmE", [128, 512], F32))
    pmF = es.enter_context(nc.psum_tensor("pmF", [128, 512], F32))

    dS = Sem(nc, "dS")
    tE = Sem(nc, "tE"); vE = Sem(nc, "vE"); aE = Sem(nc, "aE"); pE = Sem(nc, "pE")
    bar = Sem(nc, "bar")
    nbar = [0]

    blk = es.enter_context(nc.Block())

    def barrier():
        nbar[0] += 1
        tgt = 5 * nbar[0]
        for en in ("sync", "tensor", "vector", "scalar", "gpsimd"):
            def body(eng, tgt=tgt):
                eng.sem_inc(bar.h, 1)
                eng.wait_ge(bar.h, tgt)
            getattr(blk, en)(body)
        bar.n = tgt

    dw_win = Sem(nc, "dw_win")
    dw_w1 = [Sem(nc, f"dw_w1_{l}") for l in range(DEPTH)]
    dw_w2 = [Sem(nc, f"dw_w2_{l}") for l in range(DEPTH)]
    dw_wout = Sem(nc, "dw_wout")
    dw_gate = Sem(nc, "dw_gate")
    dw_lora = Sem(nc, "dw_lora")
    dw_bout = Sem(nc, "dw_bout")
    dw_base = Sem(nc, "dw_base")
    d0 = SemRing(nc, "d0", 4)
    dmisc = Sem(nc, "dmisc")
    dwT = SemRing(nc, "dwT", 4)

    dw_xbf = Sem(nc, "dw_xbf")

    @blk.gpsimd
    def _(g):
        for xc in range(8):
            dw_xbf.dma(g.dma_start(out=xbf_dram[xc * (NIN // 8):(xc + 1) * (NIN // 8)],
                                   in_=x_inner[xc * (NIN // 8):(xc + 1) * (NIN // 8)]))
        dw_win.dma(g.dma_start(out=winbf_dram, in_=Win_i))
        for l in range(n_layers):
            dw_w1[l].dma(g.dma_start(out=w1bf_dram[l], in_=W1_i[l]))
            dw_w2[l].dma(g.dma_start(out=w2bf_dram[l], in_=W2_i[l]))
        dw_wout.dma(g.dma_start(out=woutbf_dram, in_=Wout_i))
        dw_gate.dma(g.dma_start(out=gatebf_dram, in_=gateW_i))
        dw_lora.dma(g.dma_start(out=lorabf_dram, in_=loraA_i))
        dw_base.dma(g.dma_start(out=basebf_dram, in_=baseW_i))

    @blk.sync
    def _(s):
        for dst, srcp in ((bin_row[:], bin_i), (lnlw_row[:], lnlw_i),
                          (lnlb_row[:], lnlb_i), (rolef_sb[:], rolef_i),
                          (flagv_sb[:], flags_i), (myflag_sb[:], myflag_i)):
            d0.issue(s, lambda dst=dst, srcp=srcp: s.dma_start(out=dst, in_=srcp))
        for c in range(4):
            d0.issue(s, lambda c=c: s.dma_start(out=xa[:, c * V:(c + 1) * V],
                                                in_=Win_i[c * 128:(c + 1) * 128]))

    @blk.gpsimd
    def _(g):
        g.memset(ident[:], 0.0)
        g.drain()
        g.affine_select(out=ident[:], in_=ident[:], compare_op=ALU.not_equal,
                        fill=1.0, base=0, pattern=[[-1, 128]], channel_multiplier=1)
        ins = g.memset(onesbf[:], 1.0)
        pE.inc(ins)

    @blk.vector
    def _(v):
        d0.wait_all(v)
        v.memset(onesf[:], 1.0)
        v.memset(epscol[:], EPS)
        for c in range(4):
            v.reduce_sum(stat[:, c:c + 1], xa[:, c * V:(c + 1) * V], axis=AX.X)
        v.drain()
        vE.inc(v.tensor_copy(stat[:, 8:9], stat[:, 0:1]))

    @blk.sync
    def _(s):
        s.wait_ge(vE.h, vE.n)
        dmisc.dma(s.dma_start(out=misc_dram[:, 0:4], in_=stat[:, 0:4]))
        s.wait_ge(dmisc.h, dmisc.n)
        srcw = dataclasses.replace(misc_dram.flatten(), offset=0,
                                   ap=[[1, 1], [1, 4], [64, 128]])
        dstw = dataclasses.replace(wcs_row[:], offset=0, ap=[[DIM, 1], [1, DIM]])
        dmisc.dma(s.dma_start(out=dstw, in_=srcw, allow_slow_non_contiguous=True))
        s.wait_ge(dmisc.h, dmisc.n)

    @blk.tensor
    def _(t):
        t.wait_ge(dmisc.h, dmisc.n)
        t.wait_ge(pE.h, pE.n)
        t.wait_ge(vE.h, vE.n)
        t.matmul(pmA[:, 0:DIM], onesf[0:1], wcs_row[:], start=True, stop=True)
        t.matmul(pmB[:, 0:DIM], onesf[0:1], bin_row[:], start=True, stop=True)
        ins = t.matmul(pmC[:, 0:8], onesf[0:1], rolef_sb[:], start=True, stop=True)
        tE.inc(ins)

    @blk.vector
    def _(v):
        v.wait_ge(tE.h, tE.n)
        v.tensor_copy(abc[:, 0:DIM], pmA[:, 0:DIM])
        v.tensor_copy(abc[:, DIM:2 * DIM], pmB[:, 0:DIM])
        v.tensor_copy(wsel[:, 0:1], pmC[:, 0:1])
        v.tensor_scalar(wsel[:, 1:2], pmC[:, 0:1], -1.0, 1.0,
                        op0=ALU.mult, op1=ALU.add)
        v.drain()
        vE.inc(v.tensor_copy(stat[:, 9:10], stat[:, 0:1]))

    @blk.sync
    def _(s):
        s.wait_ge(dw_win.h, 16)
        for vck in range(NVC):
            dwT.issue(s, lambda vck=vck: s.dma_start(
                out=winT[:, vck * DIM:(vck + 1) * DIM],
                in_=winbf_dram[:, vck * 128:(vck + 1) * 128], transpose=True))

    barrier()

    AC = ATILE // 128
    NGR = AC * NVC // 4

    dX2 = SemRing(nc, "dX2", 2)
    dH2 = SemRing(nc, "dH2", 2)
    eT = Sem(nc, "eT")

    xaT_base = {}
    mmA_done = {}
    evacA_done = {}

    @blk.sync
    def _(s):
        dwT.wait_all(s)
        s.wait_ge(dw_xbf.h, 16)
        for c in range(AC):
            dX2.issue(s, lambda c=c: s.dma_start(
                out=xa[:, c * V:(c + 1) * V], in_=x_inner[c * 128:(c + 1) * 128]))
        xaT_base[0] = dwT.idx
        for vck in range(NVC):
            dwT.issue(s, lambda vck=vck: s.dma_start(
                out=xaT[:, vck * ATILE:(vck + 1) * ATILE],
                in_=xbf_dram[0:ATILE, vck * 128:(vck + 1) * 128], transpose=True))

    for ti in range(nat):
        par = ti % 2
        xoff = par * AC * V
        sbase = 16 + par * 40

        @blk.sync
        def _(s, ti=ti):
            if ti + 1 < nat:
                nb = (ti + 1) % 2
                if ti >= 1:
                    s.wait_ge(vE.h, vE.n)
                    s.wait_ge(tE.h, mmA_done[ti - 1])
                for c in range(AC):
                    dX2.issue(s, lambda ti=ti, nb=nb, c=c: s.dma_start(
                        out=xa[:, (nb * AC + c) * V:(nb * AC + c + 1) * V],
                        in_=x_inner[(ti + 1) * ATILE + c * 128:
                                    (ti + 1) * ATILE + (c + 1) * 128]))
                need = ((ti + 2) * ATILE - 1) // (NIN // 8) + 1
                s.wait_ge(dw_xbf.h, 16 * min(need, 8))
                xaT_base[ti + 1] = dwT.idx
                toff = nb * NVC * ATILE
                for vck in range(NVC):
                    dwT.issue(s, lambda ti=ti, vck=vck, toff=toff: s.dma_start(
                        out=xaT[:, toff + vck * ATILE: toff + (vck + 1) * ATILE],
                        in_=xbf_dram[(ti + 1) * ATILE:(ti + 2) * ATILE,
                                     vck * 128:(vck + 1) * 128], transpose=True))

        @blk.vector
        def _(v, ti=ti, par=par, xoff=xoff, sbase=sbase):
            dX2.wait_issue(v, 2 * ti)
            dX2.wait_issue(v, 2 * ti + 1)
            for c in range(AC):
                xv = xa[:, xoff + c * V: xoff + (c + 1) * V]
                v.reduce_sum(stat[:, sbase + c: sbase + c + 1], xv, axis=AX.X)
                v.scalar_tensor_tensor(
                    xsq[:], xv, 1.0, xv, op0=ALU.mult, op1=ALU.mult,
                    accum_out=stat[:, sbase + 2 + c: sbase + 3 + c])
            v.drain()
            v.tensor_scalar(stat[:, sbase + 4: sbase + 6], stat[:, sbase: sbase + 2],
                            1.0 / V, None, op0=ALU.mult)
            v.drain()
            v.scalar_tensor_tensor(stat[:, sbase + 6: sbase + 8],
                                   stat[:, sbase + 4: sbase + 6], -1.0,
                                   stat[:, sbase + 4: sbase + 6],
                                   op0=ALU.mult, op1=ALU.mult)
            v.drain()
            vE.inc(v.scalar_tensor_tensor(stat[:, sbase + 6: sbase + 8],
                                          stat[:, sbase + 2: sbase + 4], 1.0 / V,
                                          stat[:, sbase + 6: sbase + 8],
                                          op0=ALU.mult, op1=ALU.add))

        @blk.scalar
        def _(sc, ti=ti, sbase=sbase):
            sc.wait_ge(vE.h, vE.n)
            sc.activation(stat[:, sbase + 8: sbase + 10],
                          stat[:, sbase + 6: sbase + 8], AF.Ln, bias=epscol[:, 0:1])
            sc.drain()
            ins = sc.activation(stat[:, sbase + 10: sbase + 12],
                                stat[:, sbase + 8: sbase + 10], AF.Exp, scale=-0.5)
            sc.drain()
            aE.inc(ins)

        @blk.tensor
        def _(t, ti=ti, par=par):
            dwT.wait_issue(t, xaT_base[ti] + NVC - 1)
            if ti >= 1:
                t.wait_ge(aE.h, evacA_done[ti - 1])
            toff = par * NVC * ATILE
            for c in range(AC):
                ps = pmA if c == 0 else pmB
                for kc in range(NVC):
                    xt = xaT[:, toff + kc * ATILE + c * 128:
                             toff + kc * ATILE + c * 128 + 128]
                    ins = t.matmul(ps[:, 0:DIM], xt, winT[:, kc * DIM:(kc + 1) * DIM],
                                   start=(kc == 0), stop=(kc == NVC - 1))
            tE.inc(ins)
            mmA_done[ti] = tE.n

        @blk.scalar
        def _(sc, ti=ti, par=par, sbase=sbase):
            sc.wait_ge(tE.h, mmA_done[ti])
            sc.wait_ge(aE.h, aE.n)
            if ti >= 2:
                dH2.wait_issue(sc, 2 * (ti - 2))
                dH2.wait_issue(sc, 2 * (ti - 2) + 1)
            for c in range(AC):
                ps = pmA if c == 0 else pmB
                rs = stat[:, sbase + 10 + c: sbase + 11 + c]
                sc.activation(habuf[:, (par * AC + c) * DIM:(par * AC + c + 1) * DIM],
                              ps[:, 0:DIM], AF.Copy, scale=rs)
            sc.drain()
            aE.inc(sc.copy(stat[:, sbase + 14:sbase + 15], stat[:, sbase:sbase + 1]))
            evacA_done[ti] = aE.n

        @blk.vector
        def _(v, ti=ti, par=par, sbase=sbase):
            v.wait_ge(aE.h, aE.n)
            v.scalar_tensor_tensor(stat[:, sbase + 12: sbase + 14],
                                   stat[:, sbase + 4: sbase + 6], -1.0,
                                   stat[:, sbase + 10: sbase + 12],
                                   op0=ALU.mult, op1=ALU.mult)
            v.drain()
            for c in range(AC):
                hb = habuf[:, (par * AC + c) * DIM:(par * AC + c + 1) * DIM]
                v.scalar_tensor_tensor(hb, abc[:, 0:DIM],
                                       stat[:, sbase + 12 + c: sbase + 13 + c],
                                       hb, op0=ALU.mult, op1=ALU.add)
            v.drain()
            for c in range(AC):
                hb = habuf[:, (par * AC + c) * DIM:(par * AC + c + 1) * DIM]
                ins = v.tensor_add(hb, hb, abc[:, DIM:2 * DIM])
            vE.inc(ins)

        @blk.sync
        def _(s, ti=ti, par=par):
            s.wait_ge(vE.h, vE.n)
            for c in range(AC):
                dH2.issue(s, lambda ti=ti, par=par, c=c: s.dma_start(
                    out=h_dram[ti * ATILE + c * 128: ti * ATILE + (c + 1) * 128],
                    in_=habuf[:, (par * AC + c) * DIM:(par * AC + c + 1) * DIM]))

    barrier()

    if "h" in dumps:
        @blk.sync
        def _(s):
            nrow = nat * ATILE
            dH2.wait_all(s)
            dmisc.dma(s.dma_start(out=dump_t["h"][0:nrow], in_=h_dram[0:nrow]))
            s.wait_ge(dmisc.h, dmisc.n)
        barrier()

    if stop_after == "A":
        _emit_dummy_out(blk, nc, out_t, bcast_pool, dmisc, vE)
        esA.close()
        es.close()
        return nc

    @blk.sync
    def _(s):
        dH2.wait_all(s)
        dX2.wait_all(s)
    barrier()
    esA.close()

    esB = contextlib.ExitStack()

    def sbufB(name, shape, dtype):
        return esB.enter_context(nc.sbuf_tensor(name, shape, dtype))

    CCP = CC + 1
    hbt = sbufB("hbt", [128, 4 * DIM], F32)
    bscr = sbufB("bscr", [128, 4 * DIM], F32)
    y126 = sbufB("y126", [128, NCH * DIM], BF16)
    yfbt = sbufB("yfbt", [128, 4 * DIM], F32)
    zbf = sbufB("zbf", [128, 4 * DIM], BF16)
    zT = sbufB("zT", [128, 2 * 4 * DIM], BF16)
    pch = sbufB("pch", [128, 16], F32)
    powc = sbufB("powc", [128, 2 * 4 * CCP], F32)
    apowv = sbufB("apowv", [128, 2 * 4 * NCH], F32)
    lcvv = sbufB("lcvv", [128, 8], F32)
    carryv = sbufB("carryv", [128, 8], F32)
    recvv = sbufB("recvv", [128, 8], F32)
    statev = sbufB("statev", [128, 16], F32)
    sire = sbufB("sire", [128, 4 * NCH], F32)
    siim = sbufB("siim", [128, 4 * NCH], F32)
    si_inr = sbufB("si_inr", [128, 4 * NCH], F32)
    si_ini = sbufB("si_ini", [128, 4 * NCH], F32)
    sc1 = sbufB("sc1", [128, 4 * NCH], F32)
    sc2 = sbufB("sc2", [128, 4 * NCH], F32)


    tl2 = sbufB("tl2", [128, 8], F32)
    tl2bf = sbufB("tl2bf", [128, 8], BF16)
    zrow = sbufB("zrow", [128, DIM], BF16)

    dHB = SemRing(nc, "dHB", 4)
    dY = SemRing(nc, "dY", 4)
    dCk = SemRing(nc, "dCk", 4)
    dTh = SemRing(nc, "dTh", 4)
    dT = SemRing(nc, "dT", 2)
    dYW = SemRing(nc, "dYW", 2)
    dYF = SemRing(nc, "dYF", 4)
    dHW = SemRing(nc, "dHW", 4)
    dPr = SemRing(nc, "dPr", 4)
    dSt = SemRing(nc, "dSt", 2)
    eC = Sem(nc, "eC")
    sE = Sem(nc, "sE")
    eTH = Sem(nc, "eTH")
    eZ = Sem(nc, "eZ")
    thin_dram = nc.dram_tensor("thin_dram", [2, DIM * NCH], F32, kind="Internal").ap()
    si_dram = nc.dram_tensor("si_dram", [2, NCH * DIM], BF16, kind="Internal").ap()
    misc2_dram = nc.dram_tensor("misc2_dram", [2, DIM], BF16, kind="Internal").ap()

    sreg_slot = nc.alloc_register(mybir.EngineType.SP, "sreg_slot")
    sreg_flagv = nc.alloc_register(mybir.EngineType.SP, "sreg_flagv")
    sreg_tgt = nc.alloc_register(mybir.EngineType.SP, "sreg_tgt")
    sreg_cond = nc.alloc_register(mybir.EngineType.SP, "sreg_cond")
    sreg_dmc = nc.alloc_register(mybir.EngineType.SP, "sreg_dmc")
    sreg_mycol = nc.alloc_register(mybir.EngineType.SP, "sreg_mycol")
    sreg_pollc = nc.alloc_register(mybir.EngineType.SP, "sreg_pollc")
    sreg_hoff = nc.alloc_register(mybir.EngineType.SP, "sreg_hoff")
    sreg_pub = nc.alloc_register(mybir.EngineType.SP, "sreg_pub")
    dm = Sem(nc, "dm")

    d_b2 = SemRing(nc, "d_b2", 2)
    dZB = SemRing(nc, "dZB", 4)
    dZT = SemRing(nc, "dZT", 4)
    dHB2 = SemRing(nc, "dHB2", 4)
    fE = Sem(nc, "fE")
    hE = Sem(nc, "hE")
    zt_bases = {}
    f1_done = {}
    zbA = {}
    h2base = {}
    hb2base = {}
    hb4 = {}
    yf4 = {}
    he_done = {}
    b1v2 = {}
    dyb1 = {}

    def snap_pub():
        return nc.sync.snap(sreg_pub, min_val=0, max_val=NIN)

    def snap_hoff():
        return nc.sync.snap(sreg_hoff, min_val=0, max_val=L_FULL - NOUT)

    def snap_mycol():
        return nc.sync.snap(sreg_mycol, min_val=8, max_val=9)

    def snap_pollc():
        return nc.sync.snap(sreg_pollc, min_val=8, max_val=9)

    @blk.sync
    def _(s):
        nc.sync.reg_load(sreg_slot, flagv_sb[0:1, 12:13])
        nc.sync.reg_load(sreg_mycol, flagv_sb[0:1, 10:11])
        nc.sync.reg_load(sreg_pollc, flagv_sb[0:1, 11:12])
        nc.sync.reg_load(sreg_hoff, flagv_sb[0:1, 14:15])
        nc.sync.reg_load(sreg_pub, myflag_sb[0:1, 15:16])
        nc.sync.reg_mov(sreg_dmc, 0)

    @blk.vector
    def _(v):
        vE.inc(v.memset(recvv[:], 0.0))

    for l in range(n_layers):
        es23 = contextlib.ExitStack()
        tbuf = es23.enter_context(nc.sbuf_tensor(f"tbuf{l}", [128, 2 * TGRP * 128], BF16))
        yfst = es23.enter_context(nc.sbuf_tensor(f"yfst{l}", [128, TGRP * NCH], F32))
        thst = es23.enter_context(nc.sbuf_tensor(f"thst{l}", [2, 2 * 462], F32))
        swpair = es23.enter_context(nc.sbuf_tensor(f"swpair{l}", [128, 2 * DIM], BF16))
        swbf = es23.enter_context(nc.sbuf_tensor(f"swbf{l}", [128, 2 * DIM], BF16))
        crre_bf = es23.enter_context(nc.sbuf_tensor(f"crre{l}", [128, 4 * CC], BF16))
        powrow_sb = es23.enter_context(nc.sbuf_tensor(f"powrow{l}", [128, 4 * MODP], BF16))
        sirow = es23.enter_context(nc.sbuf_tensor(f"sirow{l}", [128, 2 * DIM], F32))
        crimn_bf = es23.enter_context(nc.sbuf_tensor(f"crimn{l}", [128, 4 * CC], BF16))
        sibf = es23.enter_context(nc.sbuf_tensor(f"sibf{l}", [128, 2 * DIM], BF16))

        @blk.sync
        def _(s, l=l):
            for (k, srcp, dstt, dcol) in ((0, phzr_i, pch, 0), (1, phzi_i, pch, 4),
                                          (2, lcr_i, lcvv, 0), (3, lci_i, lcvv, 4)):
                srcv = dataclasses.replace(srcp.flatten(), offset=l * DIM,
                                           ap=[[1, 128], [128, 4]])
                dstv = dataclasses.replace(
                    dstt[:, dcol:dcol + 4], ap=[[dstt[:].ap[0][0], 128], [1, 4]])
                dPr.issue(s, lambda dstv=dstv, srcv=srcv: s.dma_start(
                    out=dstv, in_=srcv, allow_slow_non_contiguous=True))
            dPr.issue(s, lambda l=l: s.dma_start(out=sirow[0:1, 0:DIM], in_=ln1w_i[l:l + 1]))
            dPr.issue(s, lambda l=l: s.dma_start(out=sirow[0:1, DIM:2 * DIM], in_=ln1b_i[l:l + 1]))
            dPr.issue(s, lambda l=l: s.dma_start(out=sirow[32:33, 0:DIM], in_=ln2w_i[l:l + 1]))
            dPr.issue(s, lambda l=l: s.dma_start(out=sirow[32:33, DIM:2 * DIM], in_=ln2b_i[l:l + 1]))
            dPr.issue(s, lambda l=l: s.dma_start(out=sirow[64:65, 0:DIM], in_=scl_i[l:l + 1]))
            srcb1 = dataclasses.replace(b1_i.flatten(), offset=l * DFF,
                                        ap=[[1, 128], [128, NFC]])
            dstb1 = dataclasses.replace(b1col[:], ap=[[NFC, 128], [1, NFC]])
            dPr.issue(s, lambda srcb1=srcb1, dstb1=dstb1: s.dma_start(
                out=dstb1, in_=srcb1, allow_slow_non_contiguous=True))

        @blk.gpsimd
        def _(g_, l=l):
            d_b2.issue(g_, lambda l=l: g_.dma_start(out=b2_row[:], in_=b2_i[l:l + 1]))

        @blk.tensor
        def _(t, l=l):
            dPr.wait_all(t)
            t.matmul(pmA[:, 0:DIM], onesf[0:1], sirow[0:1, 0:DIM], start=True, stop=True)
            t.matmul(pmB[:, 0:DIM], onesf[0:1], sirow[0:1, DIM:2 * DIM], start=True, stop=True)
            t.matmul(pmC[:, 0:DIM], onesf[32:33], sirow[32:33, 0:DIM], start=True, stop=True)
            t.matmul(pmD[:, 0:DIM], onesf[32:33], sirow[32:33, DIM:2 * DIM], start=True, stop=True)
            ins = t.matmul(pmE[:, 0:DIM], onesf[64:65], sirow[64:65, 0:DIM], start=True, stop=True)
            tE.inc(ins)

        @blk.vector
        def _(v, l=l):
            v.wait_ge(tE.h, tE.n)
            v.tensor_copy(bc(BC_L1W), pmA[:, 0:DIM])
            v.tensor_copy(bc(BC_L1B), pmB[:, 0:DIM])
            v.tensor_copy(bc(BC_L2W), pmC[:, 0:DIM])
            v.tensor_copy(bc(BC_L2B), pmD[:, 0:DIM])
            v.tensor_copy(bc(BC_SCL), pmE[:, 0:DIM])
            v.memset(zrow[:], 0.0)
            re_ = pch[:, 0:4]; im_ = pch[:, 4:8]
            r2 = pch[:, 8:12]
            v.drain()
            v.scalar_tensor_tensor(r2, re_, 1.0, re_, op0=ALU.mult, op1=ALU.mult)
            v.scalar_tensor_tensor(sc1[:, 0:4], im_, 1.0, im_, op0=ALU.mult, op1=ALU.mult)
            v.drain()
            vE.inc(v.tensor_add(r2, r2, sc1[:, 0:4]))

        @blk.scalar
        def _(sc, l=l):
            sc.wait_ge(vE.h, vE.n)
            sc.activation(pch[:, 12:16], pch[:, 8:12], AF.Ln)
            sc.drain()
            sc.activation(pch[:, 8:12], pch[:, 12:16], AF.Exp, scale=0.5)
            sc.drain()
            aE.inc(sc.activation(pch[:, 12:16], pch[:, 8:12], AF.Tanh))

        @blk.vector
        def _(v, l=l):
            v.wait_ge(aE.h, aE.n)
            rinv = sc1[:, 0:4]
            v.reciprocal(rinv, pch[:, 8:12])
            v.drain()
            tor = sc1[:, 4:8]
            v.tensor_mul(tor, pch[:, 12:16], rinv)
            v.drain()
            v.tensor_mul(pch[:, 0:4], pch[:, 0:4], tor)
            v.tensor_mul(pch[:, 4:8], pch[:, 4:8], tor)
            v.drain()
            PR, PI = 0, 4 * CCP
            for g in range(4):
                v.memset(powc[:, PR + g * CCP: PR + g * CCP + 1], 1.0)
                v.memset(powc[:, PI + g * CCP: PI + g * CCP + 1], 0.0)
            v.drain()
            for g in range(4):
                mr = pch[:, g:g + 1]; mi = pch[:, 4 + g:5 + g]
                br = powc[:, PR + g * CCP: PR + g * CCP + 1]
                bi = powc[:, PI + g * CCP: PI + g * CCP + 1]
                v.tensor_scalar(sc1[:, g * NCH: g * NCH + 1], br, mr, None, op0=ALU.mult)
                v.tensor_scalar(sc2[:, g * NCH: g * NCH + 1], bi, mi, None, op0=ALU.mult)
                v.tensor_scalar(si_inr[:, g * NCH: g * NCH + 1], bi, mr, None, op0=ALU.mult)
                v.tensor_scalar(si_ini[:, g * NCH: g * NCH + 1], br, mi, None, op0=ALU.mult)
            v.drain()
            for g in range(4):
                v.tensor_sub(powc[:, PR + g * CCP + 1: PR + g * CCP + 2],
                             sc1[:, g * NCH: g * NCH + 1],
                             sc2[:, g * NCH: g * NCH + 1])
                v.tensor_add(powc[:, PI + g * CCP + 1: PI + g * CCP + 2],
                             si_inr[:, g * NCH: g * NCH + 1],
                             si_ini[:, g * NCH: g * NCH + 1])
            v.drain()
            K = 2
            while K < CCP:
                w = min(K - 1, CCP - K)
                for g in range(4):
                    mr = powc[:, PR + g * CCP + K - 1: PR + g * CCP + K]
                    mi = powc[:, PI + g * CCP + K - 1: PI + g * CCP + K]
                    br = powc[:, PR + g * CCP + 1: PR + g * CCP + 1 + w]
                    bi = powc[:, PI + g * CCP + 1: PI + g * CCP + 1 + w]
                    v.tensor_scalar(sc1[:, g * NCH: g * NCH + w], br, mr, None, op0=ALU.mult)
                    v.tensor_scalar(sc2[:, g * NCH: g * NCH + w], bi, mi, None, op0=ALU.mult)
                    v.tensor_scalar(si_inr[:, g * NCH: g * NCH + w], bi, mr, None, op0=ALU.mult)
                    v.tensor_scalar(si_ini[:, g * NCH: g * NCH + w], br, mi, None, op0=ALU.mult)
                v.drain()
                for g in range(4):
                    v.tensor_sub(powc[:, PR + g * CCP + K: PR + g * CCP + K + w],
                                 sc1[:, g * NCH: g * NCH + w],
                                 sc2[:, g * NCH: g * NCH + w])
                    v.tensor_add(powc[:, PI + g * CCP + K: PI + g * CCP + K + w],
                                 si_inr[:, g * NCH: g * NCH + w],
                                 si_ini[:, g * NCH: g * NCH + w])
                v.drain()
                K += w
            for g in range(4):
                v.tensor_copy(crre_bf[:, g * CC:(g + 1) * CC],
                              powc[:, PR + g * CCP + 1: PR + g * CCP + 1 + CC])
                v.tensor_scalar(crimn_bf[:, g * CC:(g + 1) * CC],
                                powc[:, PI + g * CCP + 1: PI + g * CCP + 1 + CC],
                                -1.0, None, op0=ALU.mult)
            v.drain()
            AR, AI = 0, 4 * NCH
            for g in range(4):
                v.memset(apowv[:, AR + g * NCH: AR + g * NCH + 1], 1.0)
                v.memset(apowv[:, AI + g * NCH: AI + g * NCH + 1], 0.0)
            v.drain()
            for g in range(4):
                mr = powc[:, PR + g * CCP + CC: PR + g * CCP + CCP]
                mi = powc[:, PI + g * CCP + CC: PI + g * CCP + CCP]
                br = apowv[:, AR + g * NCH: AR + g * NCH + 1]
                bi = apowv[:, AI + g * NCH: AI + g * NCH + 1]
                v.tensor_scalar(sc1[:, g * NCH: g * NCH + 1], br, mr, None, op0=ALU.mult)
                v.tensor_scalar(sc2[:, g * NCH: g * NCH + 1], bi, mi, None, op0=ALU.mult)
                v.tensor_scalar(si_inr[:, g * NCH: g * NCH + 1], bi, mr, None, op0=ALU.mult)
                v.tensor_scalar(si_ini[:, g * NCH: g * NCH + 1], br, mi, None, op0=ALU.mult)
            v.drain()
            for g in range(4):
                v.tensor_sub(apowv[:, AR + g * NCH + 1: AR + g * NCH + 2],
                             sc1[:, g * NCH: g * NCH + 1],
                             sc2[:, g * NCH: g * NCH + 1])
                v.tensor_add(apowv[:, AI + g * NCH + 1: AI + g * NCH + 2],
                             si_inr[:, g * NCH: g * NCH + 1],
                             si_ini[:, g * NCH: g * NCH + 1])
            v.drain()
            K = 2
            while K < NCH:
                w = min(K - 1, NCH - K)
                for g in range(4):
                    mr = apowv[:, AR + g * NCH + K - 1: AR + g * NCH + K]
                    mi = apowv[:, AI + g * NCH + K - 1: AI + g * NCH + K]
                    br = apowv[:, AR + g * NCH + 1: AR + g * NCH + 1 + w]
                    bi = apowv[:, AI + g * NCH + 1: AI + g * NCH + 1 + w]
                    v.tensor_scalar(sc1[:, g * NCH: g * NCH + w], br, mr, None, op0=ALU.mult)
                    v.tensor_scalar(sc2[:, g * NCH: g * NCH + w], bi, mi, None, op0=ALU.mult)
                    v.tensor_scalar(si_inr[:, g * NCH: g * NCH + w], bi, mr, None, op0=ALU.mult)
                    v.tensor_scalar(si_ini[:, g * NCH: g * NCH + w], br, mi, None, op0=ALU.mult)
                v.drain()
                for g in range(4):
                    v.tensor_sub(apowv[:, AR + g * NCH + K: AR + g * NCH + K + w],
                                 sc1[:, g * NCH: g * NCH + w],
                                 sc2[:, g * NCH: g * NCH + w])
                    v.tensor_add(apowv[:, AI + g * NCH + K: AI + g * NCH + K + w],
                                 si_inr[:, g * NCH: g * NCH + w],
                                 si_ini[:, g * NCH: g * NCH + w])
                v.drain()
                K += w
            v.memset(powrow_sb[:], 0.0)
            v.drain()
            for g in range(4):
                v.tensor_copy(powrow_sb[:, g * MODP + CC - 1: g * MODP + CC - 1 + CC],
                              powc[:, PR + g * CCP: PR + g * CCP + CC])
            v.memset(swbf[:], 0.0)
            v.drain()
            for g in range(4):
                rsrc_r = dataclasses.replace(
                    powc[:], offset=PR + g * CCP + CC - 1,
                    ap=[list(powc[:].ap[0]), [-1, CC]])
                rsrc_i = dataclasses.replace(
                    powc[:], offset=PI + g * CCP + CC - 1,
                    ap=[list(powc[:].ap[0]), [-1, CC]])
                v.tensor_copy(swbf[:, g * 128: g * 128 + CC], rsrc_r)
                v.tensor_copy(swbf[:, DIM + g * 128: DIM + g * 128 + CC], rsrc_i)
            v.drain()
            vE.inc(v.tensor_copy(sc1[:, 0:1], sc1[:, 0:1]))

        @blk.gpsimd
        def _(g_, l=l, tbuf=tbuf, swpair=swpair):
            g_.memset(tbuf[:], 0.0)
            g_.drain()
            pE.inc(g_.memset(swpair[:], 0.0))

        eZ_base = eZ.n
        tE_base = tE.n

        @blk.tensor
        def _(t, l=l, eZ_base=eZ_base):
            t.wait_ge(vE.h, vE.n)
            t.wait_ge(pE.h, pE.n)
            for k in range(8):
                comp, g = k % 2, k // 2
                if k >= 2:
                    t.wait_ge(eZ.h, eZ_base + k - 1)
                pt = ptA if k % 2 == 0 else ptB
                ins = t.transpose(pt[:, 0:128],
                                  swbf[:, comp * DIM + g * 128: comp * DIM + g * 128 + 128],
                                  ident[:])
                tE.inc(ins)

        @blk.vector
        def _(v, l=l, tE_base=tE_base):
            for k in range(8):
                comp, g = k % 2, k // 2
                v.wait_ge(tE.h, tE_base + k + 1)
                pt = ptA if k % 2 == 0 else ptB
                dst = dataclasses.replace(
                    swpair[:], offset=g * 256 + comp,
                    ap=[list(swpair[:].ap[0]), [2, 128]])
                ins = v.tensor_copy(dst, pt[:, 0:128])
                eZ.inc(ins)

        @blk.sync
        def _(s, l=l):
            s.wait_ge(vE.h, vE.n)
            srcp = dataclasses.replace(
                powrow_sb[:], ap=[[4 * MODP, 128], [MODP, 4], [1, MODP]])
            dstp = dataclasses.replace(
                powrow_dram.flatten(), offset=0,
                ap=[[MODP, 128], [128 * MODP, 4], [1, MODP]])
            dmisc.dma(s.dma_start(out=dstp, in_=srcp))
            s.wait_ge(dmisc.h, dmisc.n)
            dmisc.dma(s.dma_start(out=big_dram[:, 0:MODP], in_=powrow_dram))
            s.wait_ge(dmisc.h, dmisc.n)
            n = MODP
            while n < BIGN:
                m_ = min(n, BIGN - n)
                dmisc.dma(s.dma_start(out=big_dram[:, n:n + m_],
                                      in_=big_dram[:, 0:m_]))
                s.wait_ge(dmisc.h, dmisc.n)
                n += m_

        barrier()

        if l == 0:
            @blk.sync
            def _(s, l=l):
                dmisc.dma(s.dma_start(out=y_dram[NIN:GRID], in_=zrow[0:GRID - NIN, :]))
                s.wait_ge(dmisc.h, dmisc.n)
                for ti in range(4):
                    dHB.issue(s, lambda ti=ti: s.dma_start(
                        out=hbt[:, ti * DIM:(ti + 1) * DIM],
                        in_=h_dram[ti * 128:(ti + 1) * 128]))

        for st_ in (range(NST) if l == 0 else ()):
            sb_ = 16 + (st_ % 2) * 40

            @blk.vector
            def _(v, l=l, st_=st_, sb_=sb_):
                dHB.wait_issue(v, 128 * l + st_ * 4 + 3)
                for q in range(4):
                    hv = hbt[:, q * DIM:(q + 1) * DIM]
                    v.reduce_sum(stat[:, sb_ + q:sb_ + q + 1], hv, axis=AX.X)
                    v.scalar_tensor_tensor(bscr[:, q * DIM:(q + 1) * DIM], hv, 1.0,
                                           hv, op0=ALU.mult, op1=ALU.mult,
                                           accum_out=stat[:, sb_ + 4 + q:sb_ + 5 + q])
                v.drain()
                v.tensor_scalar(stat[:, sb_ + 8:sb_ + 12], stat[:, sb_:sb_ + 4],
                                1.0 / DIM, None, op0=ALU.mult)
                v.drain()
                v.scalar_tensor_tensor(stat[:, sb_ + 12:sb_ + 16],
                                       stat[:, sb_ + 8:sb_ + 12], -1.0,
                                       stat[:, sb_ + 8:sb_ + 12],
                                       op0=ALU.mult, op1=ALU.mult)
                v.drain()
                vE.inc(v.scalar_tensor_tensor(stat[:, sb_ + 12:sb_ + 16],
                                              stat[:, sb_ + 4:sb_ + 8], 1.0 / DIM,
                                              stat[:, sb_ + 12:sb_ + 16],
                                              op0=ALU.mult, op1=ALU.add))

            @blk.scalar
            def _(sc, l=l, st_=st_, sb_=sb_):
                sc.wait_ge(vE.h, vE.n)
                sc.activation(stat[:, sb_ + 16:sb_ + 20], stat[:, sb_ + 12:sb_ + 16],
                              AF.Ln, bias=epscol[:, 0:1])
                sc.drain()
                ins = sc.activation(stat[:, sb_ + 20:sb_ + 24],
                                    stat[:, sb_ + 16:sb_ + 20], AF.Exp, scale=-0.5)
                sc.drain()
                aE.inc(ins)

            @blk.vector
            def _(v, l=l, st_=st_, sb_=sb_):
                v.wait_ge(aE.h, aE.n)
                v.scalar_tensor_tensor(stat[:, sb_ + 24:sb_ + 28],
                                       stat[:, sb_ + 8:sb_ + 12], -1.0,
                                       stat[:, sb_ + 20:sb_ + 24],
                                       op0=ALU.mult, op1=ALU.mult)
                v.drain()
                for q in range(4):
                    v.tensor_scalar(bscr[:, q * DIM:(q + 1) * DIM],
                                    hbt[:, q * DIM:(q + 1) * DIM],
                                    stat[:, sb_ + 20 + q:sb_ + 21 + q],
                                    stat[:, sb_ + 24 + q:sb_ + 25 + q],
                                    op0=ALU.mult, op1=ALU.add)
                v.drain()
                for q in range(4):
                    v.tensor_mul(bscr[:, q * DIM:(q + 1) * DIM],
                                 bscr[:, q * DIM:(q + 1) * DIM], bc(BC_L1W))
                v.drain()
                for q in range(4):
                    ins = v.tensor_add(bscr[:, q * DIM:(q + 1) * DIM],
                                       bscr[:, q * DIM:(q + 1) * DIM], bc(BC_L1B))
                vE.inc(ins)

            @blk.scalar
            def _(sc, l=l, st_=st_):
                sc.wait_ge(vE.h, vE.n)
                for q in range(4):
                    ti = st_ * 4 + q
                    if l * NBT + ti >= 4:
                        dY.wait_issue(sc, l * NBT + ti - 4)
                    ins = sc.activation(zbf[:, q * DIM:(q + 1) * DIM],
                                        bscr[:, q * DIM:(q + 1) * DIM], AF.Silu)
                    aE.inc(ins)

            @blk.sync
            def _(s, l=l, st_=st_):
                base_aE = aE.n - 4
                for q in range(4):
                    ti = st_ * 4 + q
                    s.wait_ge(aE.h, base_aE + q + 1)
                    dY.issue(s, lambda ti=ti, q=q: s.dma_start(
                        out=y_dram[ti * 128:(ti + 1) * 128],
                        in_=zbf[:, q * DIM:(q + 1) * DIM]))
                    if ti + 4 < NBT:
                        dHB.issue(s, lambda ti=ti, q=q: s.dma_start(
                            out=hbt[:, q * DIM:(q + 1) * DIM],
                            in_=h_dram[(ti + 4) * 128:(ti + 5) * 128]))

        barrier()

        @blk.sync
        def _(s, l=l):
            dY.wait_all(s)
            for c in range(NCH):
                dCk.issue(s, lambda c=c: s.dma_start(
                    out=y126[0:126, c * DIM:(c + 1) * DIM],
                    in_=y_dram[c * CC: c * CC + CC]))
            if l == 0:
                for rep in range(NCH):
                    dmisc.dma(s.dma_start(
                        out=y126[126:128, rep * DIM:(rep + 1) * DIM],
                        in_=zrow[0:2, 0:DIM]))
                    s.wait_ge(dmisc.h, dmisc.n)

        thin_tE = tE.n
        thin_eTH = eTH.n

        @blk.tensor
        def _(t, l=l, thin_eTH=thin_eTH):
            dCk.wait_all(t)
            t.wait_ge(dmisc.h, dmisc.n)
            t.wait_ge(eZ.h, eZ.n)
            for ch in range(DIM):
                b = ch // 7
                if ch % 7 == 0 and b >= 2:
                    t.wait_ge(eTH.h, thin_eTH + b - 1)
                ps = pmE if b % 2 == 0 else pmF
                rhs = dataclasses.replace(
                    y126[:], offset=ch, ap=[list(y126[:].ap[0]), [DIM, NCH]])
                ins = t.matmul(ps[0:2, (ch % 7) * 66:(ch % 7) * 66 + NCH],
                               swpair[:, ch * 2: ch * 2 + 2], rhs,
                               start=True, stop=True)
                if ch % 7 == 6 or ch == DIM - 1:
                    tE.inc(ins)

        NBATCH = (DIM + 6) // 7

        thin_dTh0 = dTh.idx

        @blk.vector
        def _(v, l=l, thin_tE=thin_tE, thin_dTh0=thin_dTh0):
            for b in range(NBATCH):
                v.wait_ge(tE.h, thin_tE + b + 1)
                if b >= 2:
                    dTh.wait_issue(v, thin_dTh0 + b - 2)
                ps = pmE if b % 2 == 0 else pmF
                nch_b = min(7, DIM - b * 7)
                ins = v.tensor_copy(thst[0:2, (b % 2) * 462:(b % 2) * 462 + nch_b * 66],
                                    ps[0:2, 0:nch_b * 66])
                eTH.inc(ins)

        @blk.sync
        def _(s, l=l, thin_eTH=thin_eTH):
            for b in range(NBATCH):
                s.wait_ge(eTH.h, thin_eTH + b + 1)
                nch_b = min(7, DIM - b * 7)
                src = thst[0:2, (b % 2) * 462:(b % 2) * 462 + nch_b * 66]
                dst = dataclasses.replace(
                    thin_dram.flatten(), offset=b * 7 * 66,
                    ap=[[DIM * NCH, 2], [1, nch_b * 66]])
                dTh.issue(s, lambda src=src, dst=dst: s.dma_start(out=dst, in_=src))
            dTh.wait_all(s)
            for comp, dstt in ((0, sire), (1, siim)):
                srcv = dataclasses.replace(
                    thin_dram.flatten(), offset=comp * DIM * NCH,
                    ap=[[NCH, 128], [128 * NCH, 4], [1, NCH]])
                dstv = dataclasses.replace(
                    dstt[:], ap=[[4 * NCH, 128], [NCH, 4], [1, NCH]])
                dmisc.dma(s.dma_start(out=dstv, in_=srcv))
                s.wait_ge(dmisc.h, dmisc.n)
            dmisc.dma(s.dma_start(out=misc2_dram, in_=y126[0:2, 65 * DIM:66 * DIM]))
            s.wait_ge(dmisc.h, dmisc.n)
            srct = dataclasses.replace(misc2_dram.flatten(), offset=0,
                                       ap=[[1, 128], [DIM, 2], [128, 4]])
            dstt = dataclasses.replace(tl2bf[:], ap=[[8, 128], [4, 2], [1, 4]])
            dmisc.dma(s.dma_start(out=dstt, in_=srct, allow_slow_non_contiguous=True))
            s.wait_ge(dmisc.h, dmisc.n)

        barrier()

        @blk.vector
        def _(v, l=l):
            AR, AI = 0, 4 * NCH
            lvl = 1
            while lvl < NCH:
                for g in range(4):
                    ar = apowv[:, AR + g * NCH + lvl: AR + g * NCH + lvl + 1]
                    ai = apowv[:, AI + g * NCH + lvl: AI + g * NCH + lvl + 1]
                    w = NCH - lvl
                    srs = sire[:, g * NCH: g * NCH + w]
                    sis = siim[:, g * NCH: g * NCH + w]
                    v.tensor_scalar(sc1[:, g * NCH: g * NCH + w], srs, ar, None, op0=ALU.mult)
                    v.tensor_scalar(sc2[:, g * NCH: g * NCH + w], sis, ai, None, op0=ALU.mult)
                    v.tensor_scalar(si_inr[:, g * NCH: g * NCH + w], sis, ar, None, op0=ALU.mult)
                    v.tensor_scalar(si_ini[:, g * NCH: g * NCH + w], srs, ai, None, op0=ALU.mult)
                v.drain()
                for g in range(4):
                    w = NCH - lvl
                    dr = sire[:, g * NCH + lvl: g * NCH + NCH]
                    di = siim[:, g * NCH + lvl: g * NCH + NCH]
                    v.tensor_add(dr, dr, sc1[:, g * NCH: g * NCH + w])
                    v.tensor_add(di, di, si_inr[:, g * NCH: g * NCH + w])
                v.drain()
                for g in range(4):
                    w = NCH - lvl
                    dr = sire[:, g * NCH + lvl: g * NCH + NCH]
                    di = siim[:, g * NCH + lvl: g * NCH + NCH]
                    v.tensor_sub(dr, dr, sc2[:, g * NCH: g * NCH + w])
                    v.tensor_add(di, di, si_ini[:, g * NCH: g * NCH + w])
                v.drain()
                lvl *= 2
            v.tensor_copy(tl2[:, 0:8], tl2bf[:, 0:8])
            v.drain()
            PR, PI = 0, 4 * CCP
            for g in range(4):
                a2r = powc[:, PR + g * CCP + 2: PR + g * CCP + 3]
                a2i = powc[:, PI + g * CCP + 2: PI + g * CCP + 3]
                s64r = sire[:, g * NCH + 64: g * NCH + 65]
                s64i = siim[:, g * NCH + 64: g * NCH + 65]
                y0 = tl2[:, g:g + 1]
                y1 = tl2[:, 4 + g:5 + g]
                pr_ = pch[:, g:g + 1]
                pi_ = pch[:, 4 + g:5 + g]
                er = statev[:, g:g + 1]
                ei = statev[:, 4 + g:5 + g]
                v.tensor_scalar(er, s64r, a2r, None, op0=ALU.mult)
                v.tensor_scalar(sc1[:, 0:1], s64i, a2i, None, op0=ALU.mult)
                v.tensor_scalar(sc2[:, 0:1], y0, pr_, None, op0=ALU.mult)
                v.drain()
                v.tensor_sub(er, er, sc1[:, 0:1])
                v.drain()
                v.tensor_add(er, er, sc2[:, 0:1])
                v.drain()
                v.tensor_add(er, er, y1)
                v.tensor_scalar(ei, s64i, a2r, None, op0=ALU.mult)
                v.tensor_scalar(sc1[:, 0:1], s64r, a2i, None, op0=ALU.mult)
                v.tensor_scalar(sc2[:, 0:1], y0, pi_, None, op0=ALU.mult)
                v.drain()
                v.tensor_add(ei, ei, sc1[:, 0:1])
                v.drain()
                v.tensor_add(ei, ei, sc2[:, 0:1])
                v.drain()
            v.drain()
            vE.inc(v.tensor_copy(statev[:, 8:9], statev[:, 0:1]))

        @blk.sync
        def _(s, l=l):
            s.wait_ge(vE.h, vE.n)
            def cdma(mk):
                mk().then_inc(dm.h, 16)
                nc.sync.reg_add(sreg_dmc, sreg_dmc, 16)
                s.wait_ge(dm.h, sreg_dmc)
            with nc.sync.If(sreg_slot):
                srcv = dataclasses.replace(statev[:, 0:8],
                                           ap=[[statev[:].ap[0][0], 128], [4, 2], [1, 4]])
                dstv = dataclasses.replace(sh_carry.flatten(),
                                           offset=(l * 4 + 2) * DIM,
                                           ap=[[1, 128], [DIM, 2], [128, 4]])
                cdma(lambda srcv=srcv, dstv=dstv: s.dma_start(
                    out=dstv, in_=srcv, allow_slow_non_contiguous=True))
                cdma(lambda l=l: s.dma_start(
                    out=sh_flags[0:1, 2 * l + 1: 2 * l + 2],
                    in_=myflag_sb[0:1, 2 * l + 1: 2 * l + 2],
                    allow_slow_non_contiguous=True))
                nc.sync.reg_mov(sreg_cond, 1)
                nc.sync.reg_load(sreg_tgt, flagv_sb[0:1, 2 * l: 2 * l + 1])
                with nc.sync.While(sreg_cond):
                    cdma(lambda l=l: s.dma_start(
                        out=polltmp[0:1, 0:1],
                        in_=sh_flags[0:1, 2 * l: 2 * l + 1],
                        allow_slow_non_contiguous=True))
                    nc.sync.reg_load(sreg_flagv, polltmp[0:1, 0:1])
                    nc.sync.reg_alu(sreg_cond, sreg_flagv, sreg_tgt, op=ALU.not_equal)
                srcv2 = dataclasses.replace(sh_carry.flatten(),
                                            offset=(l * 4 + 0) * DIM,
                                            ap=[[1, 128], [DIM, 2], [128, 4]])
                dstv2 = dataclasses.replace(recvv[:, 0:8],
                                            ap=[[recvv[:].ap[0][0], 128], [4, 2], [1, 4]])
                cdma(lambda srcv2=srcv2, dstv2=dstv2: s.dma_start(
                    out=dstv2, in_=srcv2, allow_slow_non_contiguous=True))
            with nc.sync.Else():
                srcv = dataclasses.replace(statev[:, 0:8],
                                           ap=[[statev[:].ap[0][0], 128], [4, 2], [1, 4]])
                dstv = dataclasses.replace(sh_carry.flatten(),
                                           offset=(l * 4 + 0) * DIM,
                                           ap=[[1, 128], [DIM, 2], [128, 4]])
                cdma(lambda srcv=srcv, dstv=dstv: s.dma_start(
                    out=dstv, in_=srcv, allow_slow_non_contiguous=True))
                cdma(lambda l=l: s.dma_start(
                    out=sh_flags[0:1, 2 * l: 2 * l + 1],
                    in_=myflag_sb[0:1, 2 * l: 2 * l + 1],
                    allow_slow_non_contiguous=True))
            nc.sync.sem_inc(sE.h, 1)
        sE.n += 1

        @blk.vector
        def _(v, l=l):
            v.wait_ge(sE.h, sE.n)
            v.tensor_scalar(carryv[:, 0:8], recvv[:, 0:8], wsel[:, 0:1], None,
                            op0=ALU.mult)
            v.tensor_scalar(sc1[:, 0:8], lcvv[:, 0:8], wsel[:, 1:2], None,
                            op0=ALU.mult)
            v.drain()
            v.tensor_add(carryv[:, 0:8], carryv[:, 0:8], sc1[:, 0:8])
            v.drain()
            AR, AI = 0, 4 * NCH
            for g in range(4):
                cr = carryv[:, g:g + 1]
                ci = carryv[:, 4 + g:5 + g]
                apr = apowv[:, AR + g * NCH: AR + (g + 1) * NCH]
                api = apowv[:, AI + g * NCH: AI + (g + 1) * NCH]
                tr = si_inr[:, g * NCH:(g + 1) * NCH]
                ti_ = si_ini[:, g * NCH:(g + 1) * NCH]
                v.tensor_scalar(tr, apr, cr, None, op0=ALU.mult)
                v.tensor_scalar(sc1[:, g * NCH:(g + 1) * NCH], api, ci, None, op0=ALU.mult)
                v.tensor_scalar(ti_, api, cr, None, op0=ALU.mult)
                v.tensor_scalar(sc2[:, g * NCH:(g + 1) * NCH], apr, ci, None, op0=ALU.mult)
            v.drain()
            for g in range(4):
                tr = si_inr[:, g * NCH:(g + 1) * NCH]
                ti_ = si_ini[:, g * NCH:(g + 1) * NCH]
                v.tensor_sub(tr, tr, sc1[:, g * NCH:(g + 1) * NCH])
                v.tensor_add(ti_, ti_, sc2[:, g * NCH:(g + 1) * NCH])
            v.drain()
            for g in range(4):
                v.tensor_add(si_inr[:, g * NCH + 1:(g + 1) * NCH],
                             si_inr[:, g * NCH + 1:(g + 1) * NCH],
                             sire[:, g * NCH:(g + 1) * NCH - 1])
                v.tensor_add(si_ini[:, g * NCH + 1:(g + 1) * NCH],
                             si_ini[:, g * NCH + 1:(g + 1) * NCH],
                             siim[:, g * NCH:(g + 1) * NCH - 1])
            v.drain()
            for g in range(4):
                v.tensor_copy(zbf[:, g * 128: g * 128 + NCH],
                              si_inr[:, g * NCH:(g + 1) * NCH])
                v.tensor_copy(zbf[:, DIM + g * 128: DIM + g * 128 + NCH],
                              si_ini[:, g * NCH:(g + 1) * NCH])
            v.drain()
            vE.inc(v.tensor_copy(sc1[:, 0:1], sc1[:, 0:1]))

        siz_eZ = eZ.n
        siz_tE = tE.n

        @blk.tensor
        def _(t, l=l, siz_eZ=siz_eZ):
            t.wait_ge(vE.h, vE.n)
            for k in range(8):
                comp, g = k // 4, k % 4
                if k >= 2:
                    t.wait_ge(eZ.h, siz_eZ + k - 1)
                pt = ptA if k % 2 == 0 else ptB
                ins = t.transpose(pt[:, 0:128],
                                  zbf[:, comp * DIM + g * 128: comp * DIM + g * 128 + 128],
                                  ident[:])
                tE.inc(ins)

        @blk.vector
        def _(v, l=l, siz_tE=siz_tE):
            for k in range(8):
                comp, g = k // 4, k % 4
                v.wait_ge(tE.h, siz_tE + k + 1)
                pt = ptA if k % 2 == 0 else ptB
                ins = v.tensor_copy(sibf[0:NCH, comp * DIM + g * 128:
                                         comp * DIM + g * 128 + 128],
                                    pt[0:NCH, 0:128])
                eZ.inc(ins)

        @blk.sync
        def _(s, l=l):
            s.wait_ge(eZ.h, eZ.n)
            for comp in range(2):
                dmisc.dma(s.dma_start(
                    out=dataclasses.replace(si_dram.flatten(),
                                            offset=comp * NCH * DIM,
                                            ap=[[DIM, NCH], [1, DIM]]),
                    in_=sibf[0:NCH, comp * DIM:(comp + 1) * DIM]))
                s.wait_ge(dmisc.h, dmisc.n)
            for comp in range(2):
                dmisc.dma(s.dma_start(
                    out=y126[126 + comp:127 + comp, :],
                    in_=dataclasses.replace(si_dram.flatten(),
                                            offset=comp * NCH * DIM,
                                            ap=[[1, 1], [1, NCH * DIM]])))
                s.wait_ge(dmisc.h, dmisc.n)

        barrier()

        conv_eC = eC.n
        conv_tE = tE.n
        SLOT = 128

        @blk.scalar
        def _(sc, l=l):
            for g in range(NTG):
                par = g % 2
                if g >= 2:
                    sc.wait_ge(tE.h, conv_tE + (g - 1) * (TGRP // 4))
                toff = par * TGRP * 128
                for b in range(4):
                    k0 = 32 * b
                    rows = 32 if b < 3 else 30
                    s_b = (125 - k0) % MODP
                    srcv = dataclasses.replace(
                        big_dram.flatten(), offset=g * TGRP * BIGN + s_b,
                        ap=[[MODP - 1, rows], [BIGN, TGRP], [1, CC]])
                    base = tbuf[k0:k0 + rows, toff:toff + TGRP * 128]
                    dstv = dataclasses.replace(
                        base, ap=[[tbuf[:].ap[0][0], rows], [128, TGRP], [1, CC]])
                    dT.issue(sc, lambda dstv=dstv, srcv=srcv: sc.dma_start(
                        out=dstv, in_=srcv))
                ch0 = g * TGRP
                p0, gr = ch0 % 128, ch0 // 128
                for comp, srct in ((0, crre_bf), (1, crimn_bf)):
                    srcv = srct[p0:p0 + TGRP, gr * CC: gr * CC + CC]
                    base = tbuf[126 + comp:127 + comp, toff:toff + TGRP * 128]
                    dstv = dataclasses.replace(
                        base, ap=[[tbuf[:].ap[0][0], 1], [128, TGRP], [1, CC]])
                    dT.issue(sc, lambda dstv=dstv, srcv=srcv: sc.dma_start(
                        out=dstv, in_=srcv))

        @blk.tensor
        def _(t, l=l, conv_eC=conv_eC):
            t.wait_ge(dmisc.h, dmisc.n)
            for g in range(NTG):
                par = g % 2
                for k in range(6):
                    dT.wait_issue(t, l * NTG * 6 + g * 6 + k)
                toff = par * TGRP * 128
                for ci in range(TGRP):
                    ch = g * TGRP + ci
                    bk = (g * TGRP + ci) // 4
                    if ci % 4 == 0 and bk >= 2:
                        t.wait_ge(eC.h, conv_eC + bk - 1)
                    ps = pmA if bk % 2 == 0 else pmB
                    rhs = dataclasses.replace(
                        y126[:], offset=ch, ap=[list(y126[:].ap[0]), [DIM, NCH]])
                    ins = t.matmul(ps[:, (ci % 4) * SLOT:(ci % 4) * SLOT + NCH],
                                   tbuf[:, toff + ci * 128: toff + (ci + 1) * 128],
                                   rhs, start=True, stop=True)
                    if ci % 4 == 3:
                        tE.inc(ins)

        @blk.vector
        def _(v, l=l, conv_tE=conv_tE, conv_eC=conv_eC):
            yw0 = l * 2 * NTG
            for bk in range(DIM // 4):
                v.wait_ge(tE.h, conv_tE + bk + 1)
                ps = pmA if bk % 2 == 0 else pmB
                g = (bk * 4) // TGRP
                cb = (bk * 4) % TGRP
                if cb == 0 and g >= 1:
                    dYW.wait_issue(v, yw0 + (g - 1) * 2 + 1)
                dst = dataclasses.replace(
                    yfst[:], offset=cb,
                    ap=[list(yfst[:].ap[0]), [1, 4], [TGRP, NCH]])
                src = dataclasses.replace(
                    ps[:], ap=[list(ps[:].ap[0]), [SLOT, 4], [1, NCH]])
                ins = v.tensor_copy(dst, src)
                eC.inc(ins)

        @blk.sync
        def _(s, l=l, conv_eC=conv_eC):
            for g in range(NTG):
                s.wait_ge(eC.h, conv_eC + (g + 1) * (TGRP // 4))
                for half in range(2):
                    c0 = half * (NCH // 2)
                    ncs = NCH // 2 if half == 0 else NCH - NCH // 2
                    srcv = dataclasses.replace(
                        yfst[0:126, :], offset=yfst[0:126, :].offset + c0 * TGRP,
                        ap=[[yfst[:].ap[0][0], 126], [TGRP, ncs], [1, TGRP]])
                    dstv = dataclasses.replace(
                        yf_dram.flatten(), offset=(c0 * CC) * DIM + g * TGRP,
                        ap=[[DIM, 126], [CC * DIM, ncs], [1, TGRP]])
                    dYW.issue(s, lambda dstv=dstv, srcv=srcv: s.dma_start(
                        out=dstv, in_=srcv))

        barrier()

        es23.close()
        esl2 = contextlib.ExitStack()
        w1T = esl2.enter_context(nc.sbuf_tensor(f"w1T{l}", [128, NDC * DFF], BF16))
        w2T = esl2.enter_context(nc.sbuf_tensor(f"w2T{l}", [128, NFC * DIM], BF16))
        y1b = esl2.enter_context(nc.sbuf_tensor(f"y1b{l}", [128, NFC * 512], BF16))
        hb2 = esl2.enter_context(nc.sbuf_tensor(f"hb2{l}", [128, 4 * DIM], F32))

        @blk.sync
        def _(s, l=l, w1T=w1T, w2T=w2T):
            s.wait_ge(dw_w1[l].h, 16)
            for dc in range(NDC):
                dPr.issue(s, lambda l=l, dc=dc, w1T=w1T: s.dma_start(
                    out=w1T[:, dc * DFF:(dc + 1) * DFF],
                    in_=w1bf_dram[l][:, dc * 128:(dc + 1) * 128], transpose=True))
            s.wait_ge(dw_w2[l].h, 16)
            for fc in range(NFC):
                dPr.issue(s, lambda l=l, fc=fc, w2T=w2T: s.dma_start(
                    out=w2T[:, fc * DIM:(fc + 1) * DIM],
                    in_=w2bf_dram[l][:, fc * 128:(fc + 1) * 128], transpose=True))

        @blk.tensor
        def _(t):
            dPr.wait_all(t)
            d_b2.wait_all(t)

        @blk.sync
        def _(s, l=l):
            dYW.wait_all(s)
            for ti in range(4):
                hb4[(l, ti)] = dHB.issue(s, lambda ti=ti: s.dma_start(
                    out=hbt[:, ti * DIM:(ti + 1) * DIM],
                    in_=h_dram[ti * 128:(ti + 1) * 128]))
                yf4[(l, ti)] = dYF.issue(s, lambda ti=ti: s.dma_start(
                    out=yfbt[:, ti * DIM:(ti + 1) * DIM],
                    in_=yf_dram[ti * 128:(ti + 1) * 128]))

        if l + 1 < n_layers:
            @blk.sync
            def _(s, l=l):
                dPr.issue(s, lambda l=l: s.dma_start(
                    out=lnlw_row[:], in_=ln1w_i[l + 1:l + 2]))
                dPr.issue(s, lambda l=l: s.dma_start(
                    out=lnlb_row[:], in_=ln1b_i[l + 1:l + 2]))

            @blk.tensor
            def _(t, l=l):
                dPr.wait_all(t)
                t.wait_ge(eC.h, eC.n)
                t.matmul(pmA[:, 0:DIM], onesf[0:1], lnlw_row[:],
                         start=True, stop=True)
                ins = t.matmul(pmB[:, 0:DIM], onesf[0:1], lnlb_row[:],
                               start=True, stop=True)
                tE.inc(ins)

            @blk.vector
            def _(v, l=l):
                v.wait_ge(tE.h, tE.n)
                v.tensor_copy(bc(BC_L1W), pmA[:, 0:DIM])
                vE.inc(v.tensor_copy(bc(BC_L1B), pmB[:, 0:DIM]))

        def emit_B4a(st_, l=None):
            sb_ = 16 + (st_ % 2) * 40

            @blk.vector
            def _(v, l=l, st_=st_, sb_=sb_):
                dYF.wait_issue(v, yf4[(l, st_ * 4 + 3)])
                dHB.wait_issue(v, hb4[(l, st_ * 4 + 3)])
                for q in range(4):
                    v.tensor_mul(yfbt[:, q * DIM:(q + 1) * DIM],
                                 yfbt[:, q * DIM:(q + 1) * DIM], bc(BC_SCL))
                v.drain()
                for q in range(4):
                    hv = hbt[:, q * DIM:(q + 1) * DIM]
                    v.tensor_add(hv, hv, yfbt[:, q * DIM:(q + 1) * DIM])
                v.drain()
                for q in range(4):
                    hv = hbt[:, q * DIM:(q + 1) * DIM]
                    v.reduce_sum(stat[:, sb_ + q:sb_ + q + 1], hv, axis=AX.X)
                    v.scalar_tensor_tensor(bscr[:, q * DIM:(q + 1) * DIM],
                                           hv, 1.0, hv, op0=ALU.mult, op1=ALU.mult,
                                           accum_out=stat[:, sb_ + 4 + q:sb_ + 5 + q])
                v.drain()
                v.tensor_scalar(stat[:, sb_ + 8:sb_ + 12], stat[:, sb_:sb_ + 4],
                                1.0 / DIM, None, op0=ALU.mult)
                v.drain()
                v.scalar_tensor_tensor(stat[:, sb_ + 12:sb_ + 16],
                                       stat[:, sb_ + 8:sb_ + 12], -1.0,
                                       stat[:, sb_ + 8:sb_ + 12],
                                       op0=ALU.mult, op1=ALU.mult)
                v.drain()
                vE.inc(v.scalar_tensor_tensor(stat[:, sb_ + 12:sb_ + 16],
                                              stat[:, sb_ + 4:sb_ + 8], 1.0 / DIM,
                                              stat[:, sb_ + 12:sb_ + 16],
                                              op0=ALU.mult, op1=ALU.add))

            @blk.scalar
            def _(sc, l=l, st_=st_, sb_=sb_):
                sc.wait_ge(vE.h, vE.n)
                sc.activation(stat[:, sb_ + 16:sb_ + 20], stat[:, sb_ + 12:sb_ + 16],
                              AF.Ln, bias=epscol[:, 0:1])
                sc.drain()
                ins = sc.activation(stat[:, sb_ + 20:sb_ + 24],
                                    stat[:, sb_ + 16:sb_ + 20], AF.Exp, scale=-0.5)
                sc.drain()
                aE.inc(ins)

            @blk.vector
            def _(v, l=l, st_=st_, sb_=sb_):
                v.wait_ge(aE.h, aE.n)
                if st_ >= 1:
                    dZB.wait_issue(v, zbA[l] + (st_ - 1) * 4 + 3)
                v.scalar_tensor_tensor(stat[:, sb_ + 24:sb_ + 28],
                                       stat[:, sb_ + 8:sb_ + 12], -1.0,
                                       stat[:, sb_ + 20:sb_ + 24],
                                       op0=ALU.mult, op1=ALU.mult)
                v.drain()
                for q in range(4):
                    v.tensor_scalar(bscr[:, q * DIM:(q + 1) * DIM],
                                    hbt[:, q * DIM:(q + 1) * DIM],
                                    stat[:, sb_ + 20 + q:sb_ + 21 + q],
                                    stat[:, sb_ + 24 + q:sb_ + 25 + q],
                                    op0=ALU.mult, op1=ALU.add)
                v.drain()
                for q in range(4):
                    v.tensor_mul(bscr[:, q * DIM:(q + 1) * DIM],
                                 bscr[:, q * DIM:(q + 1) * DIM], bc(BC_L2W))
                v.drain()
                for q in range(4):
                    ins = v.tensor_add(zbf[:, q * DIM:(q + 1) * DIM],
                                       bscr[:, q * DIM:(q + 1) * DIM], bc(BC_L2B))
                vE.inc(ins)

            @blk.sync
            def _(s, l=l, st_=st_):
                s.wait_ge(vE.h, vE.n)
                for q in range(4):
                    ti = st_ * 4 + q
                    dZB.issue(s, lambda ti=ti, q=q: s.dma_start(
                        out=zzbf_dram[ti * 128:(ti + 1) * 128],
                        in_=zbf[:, q * DIM:(q + 1) * DIM]))
                for q in range(4):
                    ti = st_ * 4 + q
                    dHW.issue(s, lambda ti=ti, q=q: s.dma_start(
                        out=h_dram[ti * 128:(ti + 1) * 128],
                        in_=hbt[:, q * DIM:(q + 1) * DIM]))
                for q in range(4):
                    ti = st_ * 4 + q
                    if ti + 4 < NBT:
                        dHW.wait_issue(s, NBT * l + ti)
                        hb4[(l, ti + 4)] = dHB.issue(s, lambda ti=ti: s.dma_start(
                            out=hbt[:, (ti % 4) * DIM:((ti % 4) + 1) * DIM],
                            in_=h_dram[(ti + 4) * 128:(ti + 5) * 128]))
                        yf4[(l, ti + 4)] = dYF.issue(s, lambda ti=ti: s.dma_start(
                            out=yfbt[:, (ti % 4) * DIM:((ti % 4) + 1) * DIM],
                            in_=yf_dram[(ti + 4) * 128:(ti + 5) * 128]))

        def emit_B4b(st_, l=None):
            zpar = st_ % 2

            @blk.sync
            def _(s, l=l, st_=st_, zpar=zpar):
                dZB.wait_issue(s, zbA[l] + st_ * 4 + 3)
                if st_ >= 2:
                    s.wait_ge(tE.h, f1_done[(l, st_ - 2)])
                zt_bases[(l, st_)] = dZT.idx
                for dc in range(NDC):
                    dZT.issue(s, lambda st_=st_, dc=dc, zpar=zpar: s.dma_start(
                        out=zT[:, zpar * 4 * DIM + dc * DIM:
                               zpar * 4 * DIM + (dc + 1) * DIM],
                        in_=zzbf_dram[st_ * 512:(st_ + 1) * 512,
                                      dc * 128:(dc + 1) * 128], transpose=True))
                if st_ >= 1:
                    dH2.wait_issue(s, h2base[(l, st_ - 1)] + 3)
                if l + 1 < n_layers and st_ >= 1:
                    s.wait_ge(vE.h, b1v2[(l + 1, st_ - 1)])
                hb2base[(l, st_)] = dHB2.idx
                dHW.wait_issue(s, NBT * l + st_ * 4 + 3)
                for q in range(4):
                    ti = st_ * 4 + q
                    dHB2.issue(s, lambda ti=ti, q=q: s.dma_start(
                        out=hb2[:, q * DIM:(q + 1) * DIM],
                        in_=h_dram[ti * 128:(ti + 1) * 128]))

            f1_tE = tE.n
            f1_fE = fE.n

            @blk.tensor
            def _(t, l=l, st_=st_, f1_fE=f1_fE, zpar=zpar):
                dZT.wait_issue(t, zt_bases[(l, st_)] + NDC - 1)
                for fc in range(NFC):
                    if fc >= 2:
                        t.wait_ge(fE.h, f1_fE + fc - 1)
                    ps = pmC if fc % 2 == 0 else pmD
                    for dc in range(NDC):
                        ins = t.matmul(ps[:, 0:512],
                                       w1T[:, dc * DFF + fc * 128: dc * DFF + (fc + 1) * 128],
                                       zT[:, zpar * 4 * DIM + dc * DIM:
                                          zpar * 4 * DIM + (dc + 1) * DIM],
                                       start=(dc == 0), stop=(dc == NDC - 1))
                    tE.inc(ins)
                f1_done[(l, st_)] = tE.n

            @blk.scalar
            def _(sc, l=l, st_=st_, f1_tE=f1_tE):
                for fc in range(NFC):
                    sc.wait_ge(tE.h, f1_tE + fc + 1)
                    ps = pmC if fc % 2 == 0 else pmD
                    ins = sc.activation(y1b[:, fc * 512:(fc + 1) * 512], ps[:, 0:512],
                                        AF.Silu, bias=b1col[:, fc:fc + 1])
                    fE.inc(ins)

            f2_tE = tE.n
            f2_hE = hE.n

            @blk.tensor
            def _(t, l=l, st_=st_, f2_hE=f2_hE):
                t.wait_ge(fE.h, fE.n)
                for tc in range(4):
                    if tc >= 2:
                        t.wait_ge(hE.h, f2_hE + tc - 1)
                    ps = pmE if tc % 2 == 0 else pmF
                    for fc in range(NFC):
                        t.matmul(ps[:, 0:DIM],
                                 y1b[:, fc * 512 + tc * 128: fc * 512 + (tc + 1) * 128],
                                 w2T[:, fc * DIM:(fc + 1) * DIM],
                                 start=(fc == 0), stop=False)
                    ins = t.matmul(ps[:, 0:DIM], onesbf[:], b2_row[:],
                                   start=False, stop=True)
                    tE.inc(ins)

            @blk.vector
            def _(v, l=l, st_=st_, f2_tE=f2_tE):
                for tc in range(4):
                    v.wait_ge(tE.h, f2_tE + tc + 1)
                    dHB2.wait_issue(v, hb2base[(l, st_)] + tc)
                    ps = pmE if tc % 2 == 0 else pmF
                    ins = v.tensor_add(hb2[:, tc * DIM:(tc + 1) * DIM],
                                       hb2[:, tc * DIM:(tc + 1) * DIM], ps[:, 0:DIM])
                    hE.inc(ins)
            he_done[(l, st_)] = hE.n

            @blk.sync
            def _(s, l=l, st_=st_):
                h2base[(l, st_)] = dH2.idx
                he0 = hE.n - 4
                for q in range(4):
                    ti = st_ * 4 + q
                    s.wait_ge(hE.h, he0 + q + 1)
                    dH2.issue(s, lambda ti=ti, q=q: s.dma_start(
                        out=h_dram[ti * 128:(ti + 1) * 128],
                        in_=hb2[:, q * DIM:(q + 1) * DIM]))

        def emit_B1p(k, l=l):
            lp = l + 1
            sb_ = 96 + (k % 2) * 28
            ycol = (k % 2) * 2 * 4 * DIM

            @blk.vector
            def _(v, k=k, lp=lp, sb_=sb_, ycol=ycol):
                v.wait_ge(hE.h, he_done[(l, k)])
                if k >= 2:
                    dY.wait_issue(v, dyb1[(lp, (k - 2) * 4 + 3)])
                for q in range(4):
                    hv = hb2[:, q * DIM:(q + 1) * DIM]
                    v.reduce_sum(stat[:, sb_ + q:sb_ + q + 1], hv, axis=AX.X)
                    v.scalar_tensor_tensor(
                        y126[:, ycol + q * DIM: ycol + (q + 1) * DIM],
                        hv, 1.0, hv, op0=ALU.mult, op1=ALU.mult,
                        accum_out=stat[:, sb_ + 4 + q:sb_ + 5 + q])
                v.drain()
                v.tensor_scalar(stat[:, sb_ + 8:sb_ + 12], stat[:, sb_:sb_ + 4],
                                1.0 / DIM, None, op0=ALU.mult)
                v.drain()
                v.scalar_tensor_tensor(stat[:, sb_ + 12:sb_ + 16],
                                       stat[:, sb_ + 8:sb_ + 12], -1.0,
                                       stat[:, sb_ + 8:sb_ + 12],
                                       op0=ALU.mult, op1=ALU.mult)
                v.drain()
                vE.inc(v.scalar_tensor_tensor(stat[:, sb_ + 12:sb_ + 16],
                                              stat[:, sb_ + 4:sb_ + 8], 1.0 / DIM,
                                              stat[:, sb_ + 12:sb_ + 16],
                                              op0=ALU.mult, op1=ALU.add))

            @blk.scalar
            def _(sc, k=k, sb_=sb_):
                sc.wait_ge(vE.h, vE.n)
                sc.activation(stat[:, sb_ + 16:sb_ + 20], stat[:, sb_ + 12:sb_ + 16],
                              AF.Ln, bias=epscol[:, 0:1])
                sc.drain()
                ins = sc.activation(stat[:, sb_ + 20:sb_ + 24],
                                    stat[:, sb_ + 16:sb_ + 20], AF.Exp, scale=-0.5)
                sc.drain()
                aE.inc(ins)

            @blk.vector
            def _(v, k=k, lp=lp, sb_=sb_, ycol=ycol):
                v.wait_ge(aE.h, aE.n)
                v.scalar_tensor_tensor(stat[:, sb_ + 24:sb_ + 28],
                                       stat[:, sb_ + 8:sb_ + 12], -1.0,
                                       stat[:, sb_ + 20:sb_ + 24],
                                       op0=ALU.mult, op1=ALU.mult)
                v.drain()
                for q in range(4):
                    v.tensor_scalar(y126[:, ycol + q * DIM: ycol + (q + 1) * DIM],
                                    hb2[:, q * DIM:(q + 1) * DIM],
                                    stat[:, sb_ + 20 + q:sb_ + 21 + q],
                                    stat[:, sb_ + 24 + q:sb_ + 25 + q],
                                    op0=ALU.mult, op1=ALU.add)
                v.drain()
                for q in range(4):
                    v.tensor_mul(y126[:, ycol + q * DIM: ycol + (q + 1) * DIM],
                                 y126[:, ycol + q * DIM: ycol + (q + 1) * DIM],
                                 bc(BC_L1W))
                v.drain()
                for q in range(4):
                    ins = v.tensor_add(
                        y126[:, ycol + (4 + q) * DIM: ycol + (5 + q) * DIM],
                        y126[:, ycol + q * DIM: ycol + (q + 1) * DIM],
                        bc(BC_L1B))
                vE.inc(ins)
                b1v2[(lp, k)] = vE.n

            @blk.scalar
            def _(sc, k=k, lp=lp, ycol=ycol):
                sc.wait_ge(vE.h, b1v2[(lp, k)])
                for q in range(4):
                    ins = sc.activation(
                        y126[:, ycol + q * DIM: ycol + (q + 1) * DIM],
                        y126[:, ycol + (4 + q) * DIM: ycol + (5 + q) * DIM],
                        AF.Silu)
                    aE.inc(ins)

            @blk.sync
            def _(s, k=k, lp=lp, ycol=ycol):
                a0 = aE.n - 4
                for q in range(4):
                    ti = k * 4 + q
                    s.wait_ge(aE.h, a0 + q + 1)
                    dyb1[(lp, ti)] = dY.issue(s, lambda ti=ti, q=q, ycol=ycol: s.dma_start(
                        out=y_dram[ti * 128:(ti + 1) * 128],
                        in_=y126[:, ycol + q * DIM: ycol + (q + 1) * DIM]))

        zbA[l] = dZB.idx
        do_b1p = (l + 1 < n_layers)
        for k in range(NST):
            emit_B4a(k, l=l)
            if k >= 2:
                emit_B4b(k - 2, l=l)
                if do_b1p:
                    emit_B1p(k - 2, l=l)
        for k in (NST - 2, NST - 1):
            emit_B4b(k, l=l)
            if do_b1p:
                emit_B1p(k, l=l)

        @blk.sync
        def _(s, l=l):
            dH2.wait_all(s)

        barrier()
        esl2.close()


    @blk.sync
    def _(s):
        dHW.wait_all(s)
        dY.wait_all(s)
    barrier()

    if "y" in dumps:
        @blk.gpsimd
        def _(g_):
            dw_base.dma(g_.dma_start(out=dump_t["y"], in_=y_dram))
            g_.wait_ge(dw_base.h, dw_base.n)
        barrier()
    if "yf" in dumps:
        @blk.sync
        def _(s):
            dmisc.dma(s.dma_start(out=dump_t["yf"], in_=yf_dram))
            s.wait_ge(dmisc.h, dmisc.n)
        barrier()
    if "h" in dumps:
        @blk.sync
        def _(s):
            dmisc.dma(s.dma_start(out=dump_t["h"], in_=h_dram))
            s.wait_ge(dmisc.h, dmisc.n)
        barrier()

    if n_layers < DEPTH or stop_after == "B":
        _emit_dummy_out(blk, nc, out_t, bcast_pool, dmisc, vE)
        esB.close()
        es.close()
        return nc

    esB.close()
    x1_dram = nc.dram_tensor("x1_dram", [NOUT, V], F32, kind="Internal").ap()
    houter_dram = nc.dram_tensor("houter_dram", [NOUT, DIM], F32, kind="Internal").ap()
    z2bf_dram = nc.dram_tensor("z2bf_dram", [NOUT, V], BF16, kind="Internal").ap()

    esC = contextlib.ExitStack()

    def sbufC(name, shape, dtype):
        return esC.enter_context(nc.sbuf_tensor(name, shape, dtype))

    houtb = sbufC("houtb", [128, 2 * DIM], F32)
    zcf = sbufC("zcf", [128, DIM], F32)
    zcbf = sbufC("zcbf", [128, DIM], BF16)
    zcT = sbufC("zcT", [128, DIM], BF16)
    woutT_sb = sbufC("woutT_sb", [128, NDC * V], BF16)
    boutbf = sbufC("boutbf", [1, V], BF16)
    abuf = sbufC("abuf", [128, V], F32)
    xob = sbufC("xob", [128, 2 * V], F32)
    z2bf = sbufC("z2bf", [128, V], BF16)

    dHC = SemRing(nc, "dHC", 2)
    dXO = SemRing(nc, "dXO", 2)
    dX1 = SemRing(nc, "dX1", 2)
    dZ2 = SemRing(nc, "dZ2", 2)
    NCT = NOUT // 128

    @blk.sync
    def _(s):
        def cdma(mk):
            mk().then_inc(dm.h, 16)
            nc.sync.reg_add(sreg_dmc, sreg_dmc, 16)
            s.wait_ge(dm.h, sreg_dmc)
        rvp = snap_pub()
        for blkrow in range(4):
            cdma(lambda blkrow=blkrow: s.dma_start(
                out=sh_h[bass.ds(rvp + blkrow * (NIN // 4), NIN // 4), :],
                in_=h_dram[blkrow * (NIN // 4):(blkrow + 1) * (NIN // 4)]))
        rvm = snap_mycol()
        cdma(lambda: s.dma_start(out=sh_flags[0:1, bass.ds(rvm, 1)],
                                 in_=myflag_sb[0:1, 14:15],
                                 allow_slow_non_contiguous=True))
        nc.sync.reg_mov(sreg_cond, 1)
        nc.sync.reg_load(sreg_tgt, flagv_sb[0:1, 15:16])
        with nc.sync.While(sreg_cond):
            rvpc = snap_pollc()
            cdma(lambda: s.dma_start(out=polltmp[0:1, 0:1],
                                     in_=sh_flags[0:1, bass.ds(rvpc, 1)],
                                     allow_slow_non_contiguous=True))
            nc.sync.reg_load(sreg_flagv, polltmp[0:1, 0:1])
            nc.sync.reg_alu(sreg_cond, sreg_flagv, sreg_tgt, op=ALU.not_equal)
        rvh = snap_hoff()
        cdma(lambda: s.dma_start(out=houter_dram,
                                 in_=sh_h[bass.ds(rvh, NOUT), :]))
        dPr.issue(s, lambda: s.dma_start(out=lnlw_row[:], in_=lnlw_i))
        dPr.issue(s, lambda: s.dma_start(out=lnlb_row[:], in_=lnlb_i))
        s.wait_ge(dw_wout.h, 16)
        for dc in range(NDC):
            dPr.issue(s, lambda dc=dc: s.dma_start(
                out=woutT_sb[:, dc * V:(dc + 1) * V],
                in_=woutbf_dram[:, dc * 128:(dc + 1) * 128], transpose=True))
        for ti in range(2):
            dHC.issue(s, lambda ti=ti: s.dma_start(
                out=houtb[:, (ti % 2) * DIM:((ti % 2) + 1) * DIM],
                in_=houter_dram[ti * 128:(ti + 1) * 128]))
            dXO.issue(s, lambda ti=ti: s.dma_start(
                out=xob[:, (ti % 2) * V:((ti % 2) + 1) * V],
                in_=x_outer[ti * 128:(ti + 1) * 128]))

    @blk.gpsimd
    def _(g_):
        d_b2.issue(g_, lambda: g_.dma_start(out=boutbf[:], in_=bout_i))

    @blk.tensor
    def _(t):
        dPr.wait_all(t)
        d_b2.wait_all(t)
        t.matmul(pmA[:, 0:DIM], onesf[0:1], lnlw_row[:], start=True, stop=True)
        ins = t.matmul(pmB[:, 0:DIM], onesf[0:1], lnlb_row[:], start=True, stop=True)
        tE.inc(ins)

    @blk.vector
    def _(v):
        v.wait_ge(tE.h, tE.n)
        v.tensor_copy(bc(BC_L1W), pmA[:, 0:DIM])
        vE.inc(v.tensor_copy(bc(BC_L1B), pmB[:, 0:DIM]))

    barrier()

    for ti in range(NCT):
        par = ti % 2
        sb_ = 16 + par * 40

        @blk.vector
        def _(v, ti=ti, par=par, sb_=sb_):
            dHC.wait_issue(v, ti)
            hv = houtb[:, par * DIM:(par + 1) * DIM]
            v.reduce_sum(stat[:, sb_:sb_ + 1], hv, axis=AX.X)
            v.scalar_tensor_tensor(zcf[:], hv, 1.0, hv, op0=ALU.mult,
                                   op1=ALU.mult, accum_out=stat[:, sb_ + 1:sb_ + 2])
            v.drain()
            v.tensor_scalar(stat[:, sb_ + 2:sb_ + 3], stat[:, sb_:sb_ + 1],
                            1.0 / DIM, None, op0=ALU.mult)
            v.drain()
            v.scalar_tensor_tensor(stat[:, sb_ + 3:sb_ + 4],
                                   stat[:, sb_ + 2:sb_ + 3], -1.0,
                                   stat[:, sb_ + 2:sb_ + 3],
                                   op0=ALU.mult, op1=ALU.mult)
            v.drain()
            vE.inc(v.scalar_tensor_tensor(stat[:, sb_ + 3:sb_ + 4],
                                          stat[:, sb_ + 1:sb_ + 2], 1.0 / DIM,
                                          stat[:, sb_ + 3:sb_ + 4],
                                          op0=ALU.mult, op1=ALU.add))

        @blk.scalar
        def _(sc, ti=ti, sb_=sb_):
            sc.wait_ge(vE.h, vE.n)
            sc.activation(stat[:, sb_ + 4:sb_ + 5], stat[:, sb_ + 3:sb_ + 4],
                          AF.Ln, bias=epscol[:, 0:1])
            sc.drain()
            ins = sc.activation(stat[:, sb_ + 5:sb_ + 6], stat[:, sb_ + 4:sb_ + 5],
                                AF.Exp, scale=-0.5)
            sc.drain()
            aE.inc(ins)

        @blk.vector
        def _(v, ti=ti, par=par, sb_=sb_):
            v.wait_ge(aE.h, aE.n)
            hv = houtb[:, par * DIM:(par + 1) * DIM]
            nm = stat[:, sb_ + 6:sb_ + 7]
            v.scalar_tensor_tensor(nm, stat[:, sb_ + 2:sb_ + 3], -1.0,
                                   stat[:, sb_ + 5:sb_ + 6],
                                   op0=ALU.mult, op1=ALU.mult)
            v.drain()
            v.tensor_scalar(zcf[:], hv, stat[:, sb_ + 5:sb_ + 6], nm,
                            op0=ALU.mult, op1=ALU.add)
            v.drain()
            v.tensor_mul(zcf[:], zcf[:], bc(BC_L1W))
            v.drain()
            v.tensor_add(zcf[:], zcf[:], bc(BC_L1B))
            v.drain()
            vE.inc(v.tensor_copy(zcbf[:], zcf[:]))

        ct_eZ = eZ.n
        ct_tE = tE.n

        @blk.tensor
        def _(t, ti=ti, ct_eZ=ct_eZ):
            t.wait_ge(vE.h, vE.n)
            for k in range(4):
                if k >= 2:
                    t.wait_ge(eZ.h, ct_eZ + k - 1)
                pt = ptA if k % 2 == 0 else ptB
                ins = t.transpose(pt[:, 0:128], zcbf[:, k * 128:(k + 1) * 128],
                                  ident[:])
                tE.inc(ins)

        @blk.scalar
        def _(sc, ti=ti, ct_tE=ct_tE):
            for k in range(4):
                sc.wait_ge(tE.h, ct_tE + k + 1)
                pt = ptA if k % 2 == 0 else ptB
                ins = sc.copy(zcT[:, k * 128:(k + 1) * 128], pt[:, 0:128])
                eZ.inc(ins)

        co_tE = tE.n
        co_aE = aE.n

        @blk.tensor
        def _(t, ti=ti, co_aE=co_aE):
            t.wait_ge(eZ.h, eZ.n)
            for og in range(8):
                if og >= 2:
                    t.wait_ge(aE.h, co_aE + og - 1)
                ps = pmC if og % 2 == 0 else pmD
                for dc in range(NDC):
                    t.matmul(ps[:, 0:512],
                             zcT[:, dc * 128:(dc + 1) * 128],
                             woutT_sb[:, dc * V + og * 512: dc * V + og * 512 + 512],
                             start=(dc == 0), stop=False)
                ins = t.matmul(ps[:, 0:512], onesbf[:],
                               boutbf[0:1, og * 512:(og + 1) * 512],
                               start=False, stop=True)
                tE.inc(ins)

        @blk.scalar
        def _(sc, ti=ti, co_tE=co_tE):
            if ti >= 1:
                dX1.wait_issue(sc, ti - 1)
            for og in range(8):
                sc.wait_ge(tE.h, co_tE + og + 1)
                ps = pmC if og % 2 == 0 else pmD
                ins = sc.copy(abuf[:, og * 512:(og + 1) * 512], ps[:, 0:512])
                aE.inc(ins)

        @blk.vector
        def _(v, ti=ti, par=par, sb_=sb_):
            v.wait_ge(aE.h, aE.n)
            dXO.wait_issue(v, ti)
            if ti >= 1:
                dZ2.wait_issue(v, ti - 1)
            v.reduce_sum(stat[:, sb_ + 7:sb_ + 8], abuf[:], axis=AX.X)
            v.scalar_tensor_tensor(z2bf[:], abuf[:], 1.0, abuf[:], op0=ALU.mult,
                                   op1=ALU.mult, accum_out=stat[:, sb_ + 8:sb_ + 9])
            v.drain()
            v.tensor_scalar(stat[:, sb_ + 9:sb_ + 10], stat[:, sb_ + 7:sb_ + 8],
                            1.0 / V, None, op0=ALU.mult)
            v.drain()
            v.scalar_tensor_tensor(stat[:, sb_ + 10:sb_ + 11],
                                   stat[:, sb_ + 9:sb_ + 10], -1.0,
                                   stat[:, sb_ + 9:sb_ + 10],
                                   op0=ALU.mult, op1=ALU.mult)
            v.drain()
            vE.inc(v.scalar_tensor_tensor(stat[:, sb_ + 10:sb_ + 11],
                                          stat[:, sb_ + 8:sb_ + 9], 1.0 / V,
                                          stat[:, sb_ + 10:sb_ + 11],
                                          op0=ALU.mult, op1=ALU.add))

        @blk.scalar
        def _(sc, ti=ti, sb_=sb_):
            sc.wait_ge(vE.h, vE.n)
            sc.activation(stat[:, sb_ + 11:sb_ + 12], stat[:, sb_ + 10:sb_ + 11],
                          AF.Ln, bias=epscol[:, 0:1])
            sc.drain()
            ins = sc.activation(stat[:, sb_ + 12:sb_ + 13], stat[:, sb_ + 11:sb_ + 12],
                                AF.Exp, scale=-0.5)
            sc.drain()
            aE.inc(ins)

        @blk.vector
        def _(v, ti=ti, par=par, sb_=sb_):
            v.wait_ge(aE.h, aE.n)
            nm = stat[:, sb_ + 13:sb_ + 14]
            v.scalar_tensor_tensor(nm, stat[:, sb_ + 9:sb_ + 10], -1.0,
                                   stat[:, sb_ + 12:sb_ + 13],
                                   op0=ALU.mult, op1=ALU.mult)
            v.drain()
            v.tensor_scalar(abuf[:], abuf[:], stat[:, sb_ + 12:sb_ + 13], nm,
                            op0=ALU.mult, op1=ALU.add)
            v.drain()
            v.tensor_add(abuf[:], abuf[:], xob[:, par * V:(par + 1) * V])
            v.drain()
            v.reduce_sum(stat[:, sb_ + 14:sb_ + 15], abuf[:], axis=AX.X)
            v.scalar_tensor_tensor(z2bf[:], abuf[:], 1.0, abuf[:], op0=ALU.mult,
                                   op1=ALU.mult, accum_out=stat[:, sb_ + 15:sb_ + 16])
            v.drain()
            v.tensor_scalar(stat[:, sb_ + 16:sb_ + 17], stat[:, sb_ + 14:sb_ + 15],
                            1.0 / V, None, op0=ALU.mult)
            v.drain()
            v.scalar_tensor_tensor(stat[:, sb_ + 17:sb_ + 18],
                                   stat[:, sb_ + 16:sb_ + 17], -1.0,
                                   stat[:, sb_ + 16:sb_ + 17],
                                   op0=ALU.mult, op1=ALU.mult)
            v.drain()
            vE.inc(v.scalar_tensor_tensor(stat[:, sb_ + 17:sb_ + 18],
                                          stat[:, sb_ + 15:sb_ + 16], 1.0 / V,
                                          stat[:, sb_ + 17:sb_ + 18],
                                          op0=ALU.mult, op1=ALU.add))

        @blk.scalar
        def _(sc, ti=ti, sb_=sb_):
            sc.wait_ge(vE.h, vE.n)
            sc.activation(stat[:, sb_ + 18:sb_ + 19], stat[:, sb_ + 17:sb_ + 18],
                          AF.Ln, bias=epscol[:, 0:1])
            sc.drain()
            ins = sc.activation(stat[:, sb_ + 19:sb_ + 20], stat[:, sb_ + 18:sb_ + 19],
                                AF.Exp, scale=-0.5)
            sc.drain()
            aE.inc(ins)

        @blk.vector
        def _(v, ti=ti, sb_=sb_):
            v.wait_ge(aE.h, aE.n)
            nm = stat[:, sb_ + 20:sb_ + 21]
            v.scalar_tensor_tensor(nm, stat[:, sb_ + 16:sb_ + 17], -1.0,
                                   stat[:, sb_ + 19:sb_ + 20],
                                   op0=ALU.mult, op1=ALU.mult)
            v.drain()
            v.tensor_scalar(z2bf[:], abuf[:], stat[:, sb_ + 19:sb_ + 20], nm,
                            op0=ALU.mult, op1=ALU.add)
            v.drain()
            vE.inc(v.tensor_copy(stat[:, sb_:sb_ + 1], stat[:, sb_:sb_ + 1]))

        @blk.sync
        def _(s, ti=ti):
            s.wait_ge(vE.h, vE.n)
            dX1.issue(s, lambda ti=ti: s.dma_start(
                out=x1_dram[ti * 128:(ti + 1) * 128], in_=abuf[:]))
            dZ2.issue(s, lambda ti=ti: s.dma_start(
                out=z2bf_dram[ti * 128:(ti + 1) * 128], in_=z2bf[:]))
            if ti + 2 < NCT:
                dHC.issue(s, lambda ti=ti: s.dma_start(
                    out=houtb[:, (ti % 2) * DIM:((ti % 2) + 1) * DIM],
                    in_=houter_dram[(ti + 2) * 128:(ti + 3) * 128]))
                dXO.issue(s, lambda ti=ti: s.dma_start(
                    out=xob[:, (ti % 2) * V:((ti % 2) + 1) * V],
                    in_=x_outer[(ti + 2) * 128:(ti + 3) * 128]))

    @blk.sync
    def _(s):
        dX1.wait_all(s)
        dZ2.wait_all(s)
    barrier()

    if "x1" in dumps:
        @blk.sync
        def _(s):
            dmisc.dma(s.dma_start(out=dump_t["x1"], in_=x1_dram))
            s.wait_ge(dmisc.h, dmisc.n)
        barrier()
    if "z2" in dumps:
        @blk.gpsimd
        def _(g_):
            dw_base.dma(g_.dma_start(out=dump_t["z2"], in_=z2bf_dram))
            g_.wait_ge(dw_base.h, dw_base.n)
        barrier()

    if stop_after == "C":
        _emit_dummy_out(blk, nc, out_t, bcast_pool, dmisc, vE)
        esC.close()
        es.close()
        return nc

    esC.close()
    ffn_dram = nc.dram_tensor("ffn_dram", [NOUT, V], BF16, kind="Internal").ap()

    esD1 = contextlib.ExitStack()

    def sbufD(name, shape, dtype):
        return esD1.enter_context(nc.sbuf_tensor(name, shape, dtype))

    HT = NOUT // 2
    NTT = HT // 128
    zdT = sbufD("zdT", [128, 32 * HT], BF16)
    gla = sbufD("gla", [128, 32 * 256], BF16)
    wsl = sbufD("wsl", [128, 2 * 32 * 512], BF16)
    lbTsb = sbufD("lbTsb", [128, 2 * V], BF16)
    bsb = sbufD("bsb", [128, 2 * 128], F32)
    bsbbf = sbufD("bsbbf", [128, 2 * 128], BF16)
    hAgT = sbufD("hAgT", [128, NTT * 256], BF16)
    hAgbf = sbufD("hAgbf", [128, 256], BF16)
    gha = sbufD("gha", [128, 256], F32)
    msk = sbufD("msk", [128, 2 * NE], F32)
    gsc = sbufD("gsc", [128, 8], F32)
    gt = sbufD("gt", [128, NE], F32)
    fevac = sbufD("fevac", [128, 2 * 512], BF16)

    @blk.sync
    def _(s):
        s.wait_ge(dw_gate.h, 16)
        s.wait_ge(dw_lora.h, 16)
        for dc in range(32):
            dPr.issue(s, lambda dc=dc: s.dma_start(
                out=gla[:, dc * 256: dc * 256 + 32],
                in_=gatebf_dram[:, dc * 128:(dc + 1) * 128], transpose=True))
            dPr.issue(s, lambda dc=dc: s.dma_start(
                out=gla[:, dc * 256 + 32:(dc + 1) * 256],
                in_=lorabf_dram[:, dc * 128:(dc + 1) * 128], transpose=True))

    lb_tE = tE.n
    lb_vE = vE.n
    lb_eZ = eZ.n
    lb_dXO = dXO.idx

    @blk.sync
    def _(s):
        for k in range(64):
            oc, ch = k // 2, k % 2
            e0 = ch * 16
            ne_ = 16 if ch == 0 else NE - 16
            srcb = dataclasses.replace(
                loraB_i.flatten(), offset=e0 * V * RK + oc * 128 * RK,
                ap=[[RK, 128], [V * RK, ne_], [1, RK]])
            dXO.issue(s, lambda k=k, srcb=srcb, ne_=ne_: s.dma_start(
                out=bsb[:, (k % 2) * 128: (k % 2) * 128 + ne_ * RK], in_=srcb))

    @blk.vector
    def _(v):
        v.memset(bsb[:], 0.0)
        v.drain()
        for k in range(64):
            dXO.wait_issue(v, lb_dXO + k)
            if k >= 2:
                v.wait_ge(tE.h, lb_tE + k - 1)
            ins = v.tensor_copy(bsbbf[:, (k % 2) * 128:(k % 2 + 1) * 128],
                                bsb[:, (k % 2) * 128:(k % 2 + 1) * 128])
            vE.inc(ins)

    @blk.tensor
    def _(t):
        for k in range(64):
            t.wait_ge(vE.h, lb_vE + k + 1)
            if k >= 2:
                t.wait_ge(eZ.h, lb_eZ + k - 1)
            pt = ptA if k % 2 == 0 else ptB
            ins = t.transpose(pt[:, 0:128],
                              bsbbf[:, (k % 2) * 128:(k % 2 + 1) * 128],
                              ident[:])
            tE.inc(ins)

    @blk.scalar
    def _(sc):
        for k in range(64):
            oc, ch = k // 2, k % 2
            sc.wait_ge(tE.h, lb_tE + k + 1)
            pt = ptA if k % 2 == 0 else ptB
            if ch == 0:
                ins = sc.copy(lbTsb[:, oc * 128:(oc + 1) * 128], pt[:, 0:128])
            else:
                ins = sc.copy(lbTsb[0:96, V + oc * 128: V + (oc + 1) * 128],
                              pt[0:96, 0:128])
            sc.drain()
            eZ.inc(ins)

    @blk.vector
    def _(v):
        vE.inc(v.memset(hAgbf[:], 0.0))

    barrier()

    dYW_D0 = dYW.idx
    wsl_base = {}
    lbT_base = {}

    for half in range(2):
        zd_base = dCk.idx

        @blk.sync
        def _(s, half=half):
            for dc in range(32):
                dCk.issue(s, lambda dc=dc, half=half: s.dma_start(
                    out=zdT[:, dc * HT:(dc + 1) * HT],
                    in_=z2bf_dram[half * HT: (half + 1) * HT,
                                  dc * 128:(dc + 1) * 128], transpose=True))
            if half == 0:
                s.wait_ge(dw_base.h, 16)
                wsl_base[(0, 0)] = dT.idx
                for dc in range(32):
                    dT.issue(s, lambda dc=dc: s.dma_start(
                        out=wsl[:, dc * 512:(dc + 1) * 512],
                        in_=basebf_dram[0:512, dc * 128:(dc + 1) * 128],
                        transpose=True))
            else:
                s.wait_ge(tE.h, tE.n)

        mm_done = {}
        tr_done = {}
        v1_done = {}
        v2_done = {}
        ez_done = {-1: eZ.n, -2: eZ.n}

        def emit_transpose(t, tt):
            t.wait_ge(vE.h, v2_done[tt])
            if tt >= 2:
                t.wait_ge(eZ.h, ez_done[tt - 2])
            for k in range(2):
                pt = ptA if k == 0 else ptB
                ins = t.transpose(pt[:, 0:128],
                                  hAgbf[:, k * 128:(k + 1) * 128], ident[:])
                tE.inc(ins)
            tr_done[tt] = tE.n

        def emit_evac(sc, tt):
            sc.wait_ge(tE.h, tr_done[tt] - 1)
            sc.copy(hAgT[:, tt * 256: tt * 256 + 128], ptA[:, 0:128])
            sc.wait_ge(tE.h, tr_done[tt])
            ins = sc.copy(hAgT[0:96, tt * 256 + 128: tt * 256 + 256],
                          ptB[0:96, 0:128])
            sc.drain()
            eZ.inc(ins)
            ez_done[tt] = eZ.n

        for tt in range(NTT):
            @blk.tensor
            def _(t, half=half, tt=tt, zd_base=zd_base):
                if tt == 0:
                    if half == 0:
                        dPr.wait_all(t)
                    dCk.wait_issue(t, zd_base + 31)
                if tt >= 2:
                    t.wait_ge(vE.h, v1_done[tt - 2])
                ps = pmA if tt % 2 == 0 else pmB
                for dc in range(32):
                    ins = t.matmul(ps[:, 0:256],
                                   zdT[:, dc * HT + tt * 128: dc * HT + (tt + 1) * 128],
                                   gla[:, dc * 256:(dc + 1) * 256],
                                   start=(dc == 0), stop=(dc == 31))
                tE.inc(ins)
                mm_done[tt] = tE.n
                if tt >= 1:
                    emit_transpose(t, tt - 1)

            @blk.vector
            def _(v, half=half, tt=tt):
                v.wait_ge(tE.h, mm_done[tt])
                ps = pmA if tt % 2 == 0 else pmB
                v.tensor_copy(gha[:], ps[:, 0:256])
                v.drain()
                v.memset(gha[:, NE:32], -1e30)
                v.drain()
                v.reduce_max(gsc[:, 0:1], gha[:, 0:32], axis=AX.X)
                v.drain()
                v.tensor_scalar(msk[:, 0:NE], gha[:, 0:NE], gsc[:, 0:1], None,
                                op0=ALU.is_ge)
                v.drain()
                v.scalar_tensor_tensor(msk[:, NE:2 * NE], msk[:, 0:NE], -1e30,
                                       gha[:, 0:NE], op0=ALU.mult, op1=ALU.add)
                v.drain()
                v.reduce_max(gsc[:, 1:2], msk[:, NE:2 * NE], axis=AX.X)
                v.drain()
                v.tensor_scalar(msk[:, NE:2 * NE], msk[:, NE:2 * NE],
                                gsc[:, 1:2], None, op0=ALU.is_ge)
                v.tensor_sub(gsc[:, 2:3], gsc[:, 0:1], gsc[:, 1:2])
                v.tensor_sub(gsc[:, 3:4], gsc[:, 1:2], gsc[:, 0:1])
                v.drain()
                vE.inc(v.tensor_copy(gsc[:, 6:7], gsc[:, 0:1]))
            v1_done[tt] = vE.n

            @blk.scalar
            def _(sc, half=half, tt=tt):
                sc.wait_ge(vE.h, v1_done[tt])
                sc.activation(gsc[:, 4:5], gsc[:, 2:3], AF.Sigmoid)
                ins = sc.activation(gsc[:, 5:6], gsc[:, 3:4], AF.Sigmoid)
                sc.drain()
                aE.inc(ins)
                if tt >= 1:
                    emit_evac(sc, tt - 1)

            @blk.vector
            def _(v, half=half, tt=tt):
                v.wait_ge(aE.h, aE.n)
                if tt >= 1:
                    v.wait_ge(tE.h, tr_done[tt - 1])
                v.tensor_scalar(gt[:], msk[:, 0:NE], gsc[:, 4:5], None,
                                op0=ALU.mult)
                v.tensor_scalar(msk[:, NE:2 * NE], msk[:, NE:2 * NE],
                                gsc[:, 5:6], None, op0=ALU.mult)
                v.drain()
                v.tensor_add(gt[:], gt[:], msk[:, NE:2 * NE])
                v.drain()
                v.tensor_scalar(gt[:], gt[:], SCALING, None, op0=ALU.mult)
                v.drain()
                gv = dataclasses.replace(
                    gt[:], ap=[list(gt[:].ap[0]), [1, NE], [0, RK]])
                v.tensor_mul(hAgbf[:, 0:ER], gha[:, 32:256], gv)
                v.drain()
                vE.inc(v.tensor_copy(gsc[:, 7:8], gsc[:, 0:1]))
            v2_done[tt] = vE.n

        @blk.tensor
        def _(t, half=half):
            emit_transpose(t, NTT - 1)

        @blk.scalar
        def _(sc, half=half):
            emit_evac(sc, NTT - 1)

        for og in range(8):
            par = og % 2
            o_tE = tE.n
            o_eC = eC.n

            @blk.tensor
            def _(t, half=half, og=og, o_eC=o_eC,
                  wb=None, lb=None):
                wb = wsl_base[(half, og)]
                if og == 0:
                    t.wait_ge(eZ.h, eZ.n)
                dT.wait_issue(t, wb + 31)
                for tt in range(NTT):
                    if tt >= 2:
                        t.wait_ge(eC.h, o_eC + tt - 1)
                    ps = pmC if tt % 2 == 0 else pmD
                    for dc in range(32):
                        t.matmul(ps[:, 0:512],
                                 zdT[:, dc * HT + tt * 128: dc * HT + (tt + 1) * 128],
                                 wsl[:, par * 32 * 512 + dc * 512:
                                     par * 32 * 512 + (dc + 1) * 512],
                                 start=(dc == 0), stop=False)
                    t.matmul(ps[:, 0:512], hAgT[:, tt * 256: tt * 256 + 128],
                             lbTsb[:, og * 512:(og + 1) * 512],
                             start=False, stop=False)
                    ins = t.matmul(ps[:, 0:512],
                                   hAgT[0:96, tt * 256 + 128: tt * 256 + 256],
                                   lbTsb[0:96, V + og * 512: V + (og + 1) * 512],
                                   start=False, stop=True)
                    tE.inc(ins)

            @blk.scalar
            def _(sc, half=half, og=og, o_tE=o_tE):
                for tt in range(NTT):
                    sc.wait_ge(tE.h, o_tE + tt + 1)
                    gidx = (half * 8 + og) * NTT + tt
                    if gidx >= 2:
                        dYW.wait_issue(sc, dYW_D0 + gidx - 2)
                    ps = pmC if tt % 2 == 0 else pmD
                    ins = sc.copy(fevac[:, (tt % 2) * 512:((tt % 2) + 1) * 512],
                                  ps[:, 0:512])
                    sc.drain()
                    eC.inc(ins)

            @blk.sync
            def _(s, half=half, og=og, o_eC=o_eC, o_tE=o_tE):
                for tt in range(NTT):
                    s.wait_ge(eC.h, o_eC + tt + 1)
                    dYW.issue(s, lambda tt=tt, half=half, og=og: s.dma_start(
                        out=ffn_dram[half * HT + tt * 128: half * HT + (tt + 1) * 128,
                                     og * 512:(og + 1) * 512],
                        in_=fevac[:, (tt % 2) * 512:((tt % 2) + 1) * 512]))
                nx = half * 8 + og + 1
                if nx < 16:
                    og2 = nx % 8
                    h2 = nx // 8
                    s.wait_ge(tE.h, o_tE)
                    wsl_base[(h2, og2)] = dT.idx
                    for dc in range(32):
                        dT.issue(s, lambda dc=dc, og2=og2: s.dma_start(
                            out=wsl[:, ((og + 1) % 2) * 32 * 512 + dc * 512:
                                    ((og + 1) % 2) * 32 * 512 + (dc + 1) * 512],
                            in_=basebf_dram[og2 * 512:(og2 + 1) * 512,
                                            dc * 128:(dc + 1) * 128],
                            transpose=True))


    @blk.sync
    def _(s):
        dYW.wait_all(s)
    barrier()
    esD1.close()

    if "ffn" in dumps:
        @blk.gpsimd
        def _(g_):
            dw_base.dma(g_.dma_start(out=dump_t["ffn"], in_=ffn_dram))
            g_.wait_ge(dw_base.h, dw_base.n)
        barrier()

    esD2 = contextlib.ExitStack()
    fbufbf = esD2.enter_context(nc.sbuf_tensor("fbufbf", [128, 2 * V], BF16))
    x1t = esD2.enter_context(nc.sbuf_tensor("x1t", [128, 2 * V], F32))
    obuf = esD2.enter_context(nc.sbuf_tensor("obuf", [128, V], F32))

    f_dHB = dHB.idx
    f_dXO = dXO.idx
    f_dY = dY.idx

    @blk.sync
    def _(s):
        for fi in range(2):
            dHB.issue(s, lambda fi=fi: s.dma_start(
                out=fbufbf[:, (fi % 2) * V:((fi % 2) + 1) * V],
                in_=ffn_dram[fi * 128:(fi + 1) * 128]))
            dXO.issue(s, lambda fi=fi: s.dma_start(
                out=x1t[:, (fi % 2) * V:((fi % 2) + 1) * V],
                in_=x1_dram[fi * 128:(fi + 1) * 128]))

    for fi in range(NCT):
        par = fi % 2
        sb_ = 16 + par * 40

        @blk.vector
        def _(v, fi=fi, par=par, sb_=sb_):
            dHB.wait_issue(v, f_dHB + fi)
            fv = fbufbf[:, par * V:(par + 1) * V]
            if fi >= 1:
                dY.wait_issue(v, f_dY + fi - 1)
            v.reduce_sum(stat[:, sb_:sb_ + 1], fv, axis=AX.X)
            v.scalar_tensor_tensor(obuf[:], fv, 1.0, fv, op0=ALU.mult,
                                   op1=ALU.mult, accum_out=stat[:, sb_ + 1:sb_ + 2])
            v.drain()
            v.tensor_scalar(stat[:, sb_ + 2:sb_ + 3], stat[:, sb_:sb_ + 1],
                            1.0 / V, None, op0=ALU.mult)
            v.drain()
            v.scalar_tensor_tensor(stat[:, sb_ + 3:sb_ + 4],
                                   stat[:, sb_ + 2:sb_ + 3], -1.0,
                                   stat[:, sb_ + 2:sb_ + 3],
                                   op0=ALU.mult, op1=ALU.mult)
            v.drain()
            vE.inc(v.scalar_tensor_tensor(stat[:, sb_ + 3:sb_ + 4],
                                          stat[:, sb_ + 1:sb_ + 2], 1.0 / V,
                                          stat[:, sb_ + 3:sb_ + 4],
                                          op0=ALU.mult, op1=ALU.add))

        @blk.scalar
        def _(sc, fi=fi, sb_=sb_):
            sc.wait_ge(vE.h, vE.n)
            sc.activation(stat[:, sb_ + 4:sb_ + 5], stat[:, sb_ + 3:sb_ + 4],
                          AF.Ln, bias=epscol[:, 0:1])
            sc.drain()
            ins = sc.activation(stat[:, sb_ + 5:sb_ + 6], stat[:, sb_ + 4:sb_ + 5],
                                AF.Exp, scale=-0.5)
            sc.drain()
            aE.inc(ins)

        @blk.vector
        def _(v, fi=fi, par=par, sb_=sb_):
            v.wait_ge(aE.h, aE.n)
            dXO.wait_issue(v, f_dXO + fi)
            fv = fbufbf[:, par * V:(par + 1) * V]
            nm = stat[:, sb_ + 6:sb_ + 7]
            v.scalar_tensor_tensor(nm, stat[:, sb_ + 2:sb_ + 3], -1.0,
                                   stat[:, sb_ + 5:sb_ + 6],
                                   op0=ALU.mult, op1=ALU.mult)
            v.drain()
            v.tensor_scalar(obuf[:], fv, stat[:, sb_ + 5:sb_ + 6], nm,
                            op0=ALU.mult, op1=ALU.add)
            v.drain()
            v.tensor_add(obuf[:], obuf[:], x1t[:, par * V:(par + 1) * V])
            v.drain()
            vE.inc(v.tensor_copy(stat[:, sb_:sb_ + 1], stat[:, sb_:sb_ + 1]))

        @blk.sync
        def _(s, fi=fi):
            s.wait_ge(vE.h, vE.n)
            dY.issue(s, lambda fi=fi: s.dma_start(
                out=out_t[fi * 128:(fi + 1) * 128], in_=obuf[:]))
            if fi + 2 < NCT:
                dHB.issue(s, lambda fi=fi: s.dma_start(
                    out=fbufbf[:, (fi % 2) * V:((fi % 2) + 1) * V],
                    in_=ffn_dram[(fi + 2) * 128:(fi + 3) * 128]))
                dXO.issue(s, lambda fi=fi: s.dma_start(
                    out=x1t[:, (fi % 2) * V:((fi % 2) + 1) * V],
                    in_=x1_dram[(fi + 2) * 128:(fi + 3) * 128]))

    @blk.sync
    def _(s):
        dY.wait_all(s)
    barrier()
    esD2.close()
    es.close()
    return nc


def _emit_dummy_out(blk, nc, out_t, scratch, dm, vE):
    @blk.vector
    def _(v):
        v.memset(scratch[:, 0:DIM], 0.0)
        v.drain()
        vE.inc(v.tensor_copy(scratch[:, 0:1], scratch[:, 0:1]))

    @blk.sync
    def _(s):
        s.wait_ge(vE.h, vE.n)
        for i in range(NOUT // 128):
            for j in range(V // DIM):
                dm.dma(s.dma_start(out=out_t[i * 128:(i + 1) * 128,
                                             j * DIM:(j + 1) * DIM],
                                   in_=scratch[:, 0:DIM]))
                s.wait_ge(dm.h, dm.n)


def _prep_inputs(inputs, nonce=None):
    x = np.ascontiguousarray(np.asarray(inputs["L0_attn_x"])[0])
    gate_pad = np.zeros((32, V), np.float32)
    gate_pad[:NE] = np.asarray(inputs["L0_ffn_gate_W"])
    loraA_flat = np.ascontiguousarray(
        np.asarray(inputs["L0_ffn_lora_A"]).reshape(ER, V))
    if nonce is None:
        nonce = int(np.random.randint(1 << 20, 1 << 30))
    common = dict(
        Win_i=inputs["L0_attn_Win"],
        bin_i=np.asarray(inputs["L0_attn_bin"])[None, :],
        ln1w_i=inputs["L0_attn_ln1_w"], ln1b_i=inputs["L0_attn_ln1_b"],
        phzr_i=inputs["L0_attn_phz_re"], phzi_i=inputs["L0_attn_phz_im"],
        lcr_i=inputs["L0_attn_lc_re"], lci_i=inputs["L0_attn_lc_im"],
        scl_i=inputs["L0_attn_sc_lin"],
        ln2w_i=inputs["L0_attn_ln2_w"], ln2b_i=inputs["L0_attn_ln2_b"],
        W1_i=inputs["L0_attn_W1"], b1_i=inputs["L0_attn_b1"],
        W2_i=inputs["L0_attn_W2"], b2_i=inputs["L0_attn_b2"],
        lnlw_i=np.asarray(inputs["L0_attn_lnl_w"])[None, :],
        lnlb_i=np.asarray(inputs["L0_attn_lnl_b"])[None, :],
        Wout_i=inputs["L0_attn_Wout"],
        bout_i=np.asarray(inputs["L0_attn_bout"])[None, :],
        baseW_i=inputs["L0_ffn_base_W"],
        gateW_i=gate_pad,
        loraA_i=loraA_flat,
        loraB_i=inputs["L0_ffn_lora_B"],
    )
    common = {k: np.ascontiguousarray(np.asarray(v, np.float32))
              for k, v in common.items()}
    in_maps = []
    for r in range(8):
        erole = (r % 2 == 0)
        inner_lo = 0 if erole else NIN
        outer_lo = NOUT * r
        own = inner_lo <= outer_lo and outer_lo + NOUT <= inner_lo + NIN
        myslot = 0 if erole else 1
        rolef = np.zeros((1, 8), np.float32)
        rolef[0, 0] = 0.0 if erole else 1.0
        rolef[0, 1] = 0.0 if own else 1.0
        rolef[0, 2] = float(myslot)
        flags = np.zeros((1, 16), np.int32)
        myflag = np.zeros((1, 16), np.int32)
        for l in range(DEPTH):
            flags[0, 2 * l + (1 - myslot)] = nonce + 2 * l + (1 - myslot)
            myflag[0, 2 * l + myslot] = nonce + 2 * l + myslot
        flags[0, 8 + (1 - myslot)] = nonce + 8 + (1 - myslot)
        flags[0, 8 + myslot] = nonce + 8 + myslot
        myflag[0, 8 + myslot] = nonce + 8 + myslot
        pollcol = (8 + myslot) if own else (8 + (1 - myslot))
        flags[0, 10] = 8 + myslot
        flags[0, 11] = pollcol
        flags[0, 12] = myslot
        flags[0, 13] = 0 if own else 1
        flags[0, 14] = NOUT * r
        flags[0, 15] = nonce + pollcol
        myflag[0, 14] = nonce + 8 + myslot
        myflag[0, 15] = NIN * myslot
        m = dict(common)
        m["x_inner"] = np.ascontiguousarray(x[inner_lo:inner_lo + NIN])
        m["x_outer"] = np.ascontiguousarray(x[outer_lo:outer_lo + NOUT])
        m["rolef_i"] = rolef
        m["flags_i"] = flags.copy()
        m["myflag_i"] = myflag.copy()
        in_maps.append(m)
    return in_maps


def kernel(**inputs):
    in_maps = _prep_inputs(inputs)
    nc = build_program()
    res = run_bass_kernel_spmd(nc, in_maps, core_ids=list(range(8)))
    out = np.concatenate([np.asarray(res.results[r]["out"]) for r in range(8)],
                         axis=0)
    return out[None, :, :].astype(np.float32)


# version 07 (final)
# speedup vs baseline: 1.0663x; 1.0214x over previous
import contextlib
import dataclasses
import numpy as np
import ml_dtypes

import concourse.bass as bass
import concourse.mybir as mybir
from concourse.bass_utils import run_bass_kernel_spmd

F32 = mybir.dt.float32
BF16 = mybir.dt.bfloat16
I32 = mybir.dt.int32
AF = mybir.ActivationFunctionType
PM2 = mybir.MatmulPerfMode.DoubleRow
ALU = mybir.AluOpType
AX = mybir.AxisListType
DS = bass.ds

L_FULL = 16384
V = 4096
DIM = 512
DFF = 2048
DEPTH = 4
EPS = 1e-5
NIN = 8192
NOUT = 2048
CC = 126
NCH = 66
GRID = NCH * CC
ATILE = 256
NAT = NIN // ATILE
NVC = V // 128
NDC = DIM // 128
NFC = DFF // 128
MODP = 2 * CC - 1
NREP = 33
BIGN = NREP * MODP
TGRP = 32
NTG = DIM // TGRP
NE = 28
RK = 8
ER = NE * RK
SCALING = 2.0


class Sem:
    def __init__(self, nc, name):
        self.h = nc.alloc_semaphore(name=name)
        self.n = 0

    def inc(self, instr, k=1):
        instr.then_inc(self.h, k)
        self.n += k
        return self.n

    def dma(self, instr):
        return self.inc(instr, 16)


class SemRing:

    def __init__(self, nc, name, k):
        self.sem = Sem(nc, name)
        self.k = k
        self.idx = 0

    def issue(self, eng, mk):
        self.sem.dma(mk())
        self.idx += 1
        return self.idx - 1

    def wait_issue(self, eng, j):
        eng.wait_ge(self.sem.h, 16 * (j + 1))

    def wait_all(self, eng):
        if self.idx:
            eng.wait_ge(self.sem.h, 16 * self.idx)


RACE_CHECK = False


def build_program(n_layers=DEPTH, stop_after=None, dumps=(), nat=NAT):
    nc = bass.Bass("TRN2", target_bir_lowering=False, debug=False,
                   detect_race_conditions=RACE_CHECK,
                   dynamic_dma_scratch_size=32768)
    es = contextlib.ExitStack()

    def din(name, shape, dtype=F32):
        return nc.dram_tensor(name, shape, dtype, kind="ExternalInput").ap()

    x_inner = din("x_inner", [NIN, V])
    x_outer = din("x_outer", [NOUT, V])
    Win_i = din("Win_i", [DIM, V])
    bin_i = din("bin_i", [1, DIM])
    ln1w_i = din("ln1w_i", [DEPTH, DIM]); ln1b_i = din("ln1b_i", [DEPTH, DIM])
    phzr_i = din("phzr_i", [DEPTH, DIM]); phzi_i = din("phzi_i", [DEPTH, DIM])
    lcr_i = din("lcr_i", [DEPTH, DIM]); lci_i = din("lci_i", [DEPTH, DIM])
    scl_i = din("scl_i", [DEPTH, DIM])
    ln2w_i = din("ln2w_i", [DEPTH, DIM]); ln2b_i = din("ln2b_i", [DEPTH, DIM])
    W1_i = din("W1_i", [DEPTH, DFF, DIM]); b1_i = din("b1_i", [DEPTH, DFF])
    W2_i = din("W2_i", [DEPTH, DIM, DFF]); b2_i = din("b2_i", [DEPTH, DIM])
    lnlw_i = din("lnlw_i", [1, DIM]); lnlb_i = din("lnlb_i", [1, DIM])
    Wout_i = din("Wout_i", [V, DIM]); bout_i = din("bout_i", [1, V])
    baseW_i = din("baseW_i", [V, V])
    gateW_i = din("gateW_i", [32, V])
    loraA_i = din("loraA_i", [ER, V])
    loraB_i = din("loraB_i", [NE, V, RK])
    rolef_i = din("rolef_i", [1, 8])
    flags_i = din("flags_i", [1, 16], I32)
    myflag_i = din("myflag_i", [1, 16], I32)

    out_t = nc.dram_tensor("out", [NOUT, V], F32, kind="ExternalOutput").ap()
    dump_t = {}
    for nm, shape in (("h", [NIN, DIM]), ("y", [GRID, DIM]), ("yf", [GRID, DIM]),
                      ("si", [128, 8 * NCH]), ("x1", [NOUT, V]), ("z2", [NOUT, V]),
                      ("ffn", [NOUT, V])):
        if nm in dumps:
            dump_t[nm] = nc.dram_tensor("dbg_" + nm, shape, F32,
                                        kind="ExternalOutput").ap()

    h_dram = nc.dram_tensor("h_dram", [NIN, DIM], F32, kind="Internal").ap()
    y_dram = nc.dram_tensor("y_dram", [GRID, DIM], BF16, kind="Internal").ap()
    zzbf_dram = nc.dram_tensor("zzbf_dram", [NIN, DIM], BF16, kind="Internal").ap()
    yf_dram = nc.dram_tensor("yf_dram", [GRID, DIM], F32, kind="Internal").ap()
    winbf_dram = nc.dram_tensor("winbf_dram", [DIM, V], BF16, kind="Internal").ap()
    xbf_dram = nc.dram_tensor("xbf_dram", [NIN, V], BF16, kind="Internal").ap()
    w1bf_dram = nc.dram_tensor("w1bf_dram", [DEPTH, DFF, DIM], BF16, kind="Internal").ap()
    w2bf_dram = nc.dram_tensor("w2bf_dram", [DEPTH, DIM, DFF], BF16, kind="Internal").ap()
    woutbf_dram = nc.dram_tensor("woutbf_dram", [V, DIM], BF16, kind="Internal").ap()
    basebf_dram = nc.dram_tensor("basebf_dram", [V, V], BF16, kind="Internal").ap()
    gatebf_dram = nc.dram_tensor("gatebf_dram", [32, V], BF16, kind="Internal").ap()
    lorabf_dram = nc.dram_tensor("lorabf_dram", [ER, V], BF16, kind="Internal").ap()
    powrow_dram = nc.dram_tensor("powrow_dram", [DIM, MODP], BF16, kind="Internal").ap()
    big_dram = nc.dram_tensor("big_dram", [DIM, BIGN], BF16, kind="Internal").ap()
    misc_dram = nc.dram_tensor("misc_dram", [128, 64], F32, kind="Internal").ap()

    sh_h = nc.dram_tensor("sh_h", [L_FULL, DIM], F32, kind="Internal",
                          addr_space="Shared").ap()
    sh_carry = nc.dram_tensor("sh_carry", [DEPTH, 2, 2, DIM], F32, kind="Internal",
                              addr_space="Shared").ap()
    sh_flags = nc.dram_tensor("sh_flags", [1, 16], I32, kind="Internal",
                              addr_space="Shared").ap()

    def sbuf(name, shape, dtype):
        return es.enter_context(nc.sbuf_tensor(name, shape, dtype))

    ident = sbuf("ident", [128, 128], BF16)
    onesf = sbuf("onesf", [65, 128], F32)
    onesbf = sbuf("onesbf", [1, 128], BF16)
    lnlw_row = sbuf("lnlw_row", [1, DIM], F32)
    lnlb_row = sbuf("lnlb_row", [1, DIM], F32)
    b1col = sbuf("b1col", [128, NFC], F32)
    b2_row = sbuf("b2_row", [1, DIM], BF16)
    rolef_sb = sbuf("rolef_sb", [1, 8], F32)
    flagv_sb = sbuf("flagv_sb", [1, 16], I32)
    myflag_sb = sbuf("myflag_sb", [1, 16], I32)
    polltmp = sbuf("polltmp", [1, 2], I32)
    wsel = sbuf("wsel", [128, 4], F32)
    epscol = sbuf("epscol", [128, 1], F32)

    NBC = 5
    NBT = NIN // 128
    NST = NBT // 4
    bcast_pool = sbuf("bcast_pool", [128, NBC * DIM], F32)
    BC_SCL, BC_L1W, BC_L1B, BC_L2W, BC_L2B = range(5)

    def bc(i):
        return bcast_pool[:, i * DIM:(i + 1) * DIM]

    stat = sbuf("stat", [128, 160], F32)

    esA = contextlib.ExitStack()

    def sbufA(name, shape, dtype):
        return esA.enter_context(nc.sbuf_tensor(name, shape, dtype))

    bin_row = sbufA("bin_row", [1, DIM], F32)
    wcs_row = sbufA("wcs_row", [1, DIM], F32)
    abc = sbufA("abc", [128, 2 * DIM], F32)
    xa = sbufA("xa", [128, 2 * 2 * V], F32)
    xsq = sbufA("xsq", [128, V], F32)
    xaT = sbufA("xaT", [128, 2 * NVC * ATILE], BF16)
    winT = sbufA("winT", [128, NVC * DIM], BF16)
    habuf = sbufA("habuf", [128, 2 * 2 * DIM], F32)

    ptA = es.enter_context(nc.psum_tensor("ptA", [128, 1024], BF16))
    ptB = es.enter_context(nc.psum_tensor("ptB", [128, 1024], BF16))
    pmA = es.enter_context(nc.psum_tensor("pmA", [128, 512], F32))
    pmB = es.enter_context(nc.psum_tensor("pmB", [128, 512], F32))
    pmC = es.enter_context(nc.psum_tensor("pmC", [128, 512], F32))
    pmD = es.enter_context(nc.psum_tensor("pmD", [128, 512], F32))
    pmE = es.enter_context(nc.psum_tensor("pmE", [128, 512], F32))
    pmF = es.enter_context(nc.psum_tensor("pmF", [128, 512], F32))

    dS = Sem(nc, "dS")
    tE = Sem(nc, "tE"); vE = Sem(nc, "vE"); aE = Sem(nc, "aE"); pE = Sem(nc, "pE")
    bar = Sem(nc, "bar")
    nbar = [0]

    blk = es.enter_context(nc.Block())

    def barrier():
        nbar[0] += 1
        tgt = 5 * nbar[0]
        for en in ("sync", "tensor", "vector", "scalar", "gpsimd"):
            def body(eng, tgt=tgt):
                eng.sem_inc(bar.h, 1)
                eng.wait_ge(bar.h, tgt)
            getattr(blk, en)(body)
        bar.n = tgt

    dw_win = Sem(nc, "dw_win")
    dw_w1 = [Sem(nc, f"dw_w1_{l}") for l in range(DEPTH)]
    dw_w2 = [Sem(nc, f"dw_w2_{l}") for l in range(DEPTH)]
    dw_wout = Sem(nc, "dw_wout")
    dw_gate = Sem(nc, "dw_gate")
    dw_lora = Sem(nc, "dw_lora")
    dw_bout = Sem(nc, "dw_bout")
    dw_base = Sem(nc, "dw_base")
    d0 = SemRing(nc, "d0", 4)
    dmisc = Sem(nc, "dmisc")
    dwT = SemRing(nc, "dwT", 4)

    dw_xbf = Sem(nc, "dw_xbf")

    @blk.gpsimd
    def _(g):
        for xc in range(8):
            dw_xbf.dma(g.dma_start(out=xbf_dram[xc * (NIN // 8):(xc + 1) * (NIN // 8)],
                                   in_=x_inner[xc * (NIN // 8):(xc + 1) * (NIN // 8)]))
        dw_win.dma(g.dma_start(out=winbf_dram, in_=Win_i))
        for l in range(n_layers):
            dw_w1[l].dma(g.dma_start(out=w1bf_dram[l], in_=W1_i[l]))
            dw_w2[l].dma(g.dma_start(out=w2bf_dram[l], in_=W2_i[l]))
        dw_wout.dma(g.dma_start(out=woutbf_dram, in_=Wout_i))
        dw_gate.dma(g.dma_start(out=gatebf_dram, in_=gateW_i))
        dw_lora.dma(g.dma_start(out=lorabf_dram, in_=loraA_i))
        dw_base.dma(g.dma_start(out=basebf_dram, in_=baseW_i))

    @blk.sync
    def _(s):
        for dst, srcp in ((bin_row[:], bin_i), (lnlw_row[:], lnlw_i),
                          (lnlb_row[:], lnlb_i), (rolef_sb[:], rolef_i),
                          (flagv_sb[:], flags_i), (myflag_sb[:], myflag_i)):
            d0.issue(s, lambda dst=dst, srcp=srcp: s.dma_start(out=dst, in_=srcp))
        for c in range(4):
            d0.issue(s, lambda c=c: s.dma_start(out=xa[:, c * V:(c + 1) * V],
                                                in_=Win_i[c * 128:(c + 1) * 128]))

    @blk.gpsimd
    def _(g):
        g.memset(ident[:], 0.0)
        g.drain()
        g.affine_select(out=ident[:], in_=ident[:], compare_op=ALU.not_equal,
                        fill=1.0, base=0, pattern=[[-1, 128]], channel_multiplier=1)
        ins = g.memset(onesbf[:], 1.0)
        pE.inc(ins)

    @blk.vector
    def _(v):
        d0.wait_all(v)
        v.memset(onesf[:], 1.0)
        v.memset(epscol[:], EPS)
        for c in range(4):
            v.reduce_sum(stat[:, c:c + 1], xa[:, c * V:(c + 1) * V], axis=AX.X)
        v.drain()
        vE.inc(v.tensor_copy(stat[:, 8:9], stat[:, 0:1]))

    @blk.sync
    def _(s):
        s.wait_ge(vE.h, vE.n)
        dmisc.dma(s.dma_start(out=misc_dram[:, 0:4], in_=stat[:, 0:4]))
        s.wait_ge(dmisc.h, dmisc.n)
        srcw = dataclasses.replace(misc_dram.flatten(), offset=0,
                                   ap=[[1, 1], [1, 4], [64, 128]])
        dstw = dataclasses.replace(wcs_row[:], offset=0, ap=[[DIM, 1], [1, DIM]])
        dmisc.dma(s.dma_start(out=dstw, in_=srcw, allow_slow_non_contiguous=True))
        s.wait_ge(dmisc.h, dmisc.n)

    @blk.tensor
    def _(t):
        t.wait_ge(dmisc.h, dmisc.n)
        t.wait_ge(pE.h, pE.n)
        t.wait_ge(vE.h, vE.n)
        t.matmul(pmA[:, 0:DIM], onesf[0:1], wcs_row[:], start=True, stop=True)
        t.matmul(pmB[:, 0:DIM], onesf[0:1], bin_row[:], start=True, stop=True)
        ins = t.matmul(pmC[:, 0:8], onesf[0:1], rolef_sb[:], start=True, stop=True)
        tE.inc(ins)

    @blk.vector
    def _(v):
        v.wait_ge(tE.h, tE.n)
        v.tensor_copy(abc[:, 0:DIM], pmA[:, 0:DIM])
        v.tensor_copy(abc[:, DIM:2 * DIM], pmB[:, 0:DIM])
        v.tensor_copy(wsel[:, 0:1], pmC[:, 0:1])
        v.tensor_scalar(wsel[:, 1:2], pmC[:, 0:1], -1.0, 1.0,
                        op0=ALU.mult, op1=ALU.add)
        v.drain()
        vE.inc(v.tensor_copy(stat[:, 9:10], stat[:, 0:1]))

    @blk.sync
    def _(s):
        s.wait_ge(dw_win.h, 16)
        for vck in range(NVC):
            dwT.issue(s, lambda vck=vck: s.dma_start(
                out=winT[:, vck * DIM:(vck + 1) * DIM],
                in_=winbf_dram[:, vck * 128:(vck + 1) * 128], transpose=True))

    barrier()

    AC = ATILE // 128
    NGR = AC * NVC // 4

    dX2 = SemRing(nc, "dX2", 2)
    dH2 = SemRing(nc, "dH2", 2)
    eT = Sem(nc, "eT")

    xaT_base = {}
    mmA_done = {}
    evacA_done = {}

    @blk.sync
    def _(s):
        dwT.wait_all(s)
        s.wait_ge(dw_xbf.h, 16)
        for c in range(AC):
            dX2.issue(s, lambda c=c: s.dma_start(
                out=xa[:, c * V:(c + 1) * V], in_=x_inner[c * 128:(c + 1) * 128]))
        xaT_base[0] = dwT.idx
        for vck in range(NVC):
            dwT.issue(s, lambda vck=vck: s.dma_start(
                out=xaT[:, vck * ATILE:(vck + 1) * ATILE],
                in_=xbf_dram[0:ATILE, vck * 128:(vck + 1) * 128], transpose=True))

    for ti in range(nat):
        par = ti % 2
        xoff = par * AC * V
        sbase = 16 + par * 40

        @blk.sync
        def _(s, ti=ti):
            if ti + 1 < nat:
                nb = (ti + 1) % 2
                if ti >= 1:
                    s.wait_ge(vE.h, vE.n)
                    s.wait_ge(tE.h, mmA_done[ti - 1])
                for c in range(AC):
                    dX2.issue(s, lambda ti=ti, nb=nb, c=c: s.dma_start(
                        out=xa[:, (nb * AC + c) * V:(nb * AC + c + 1) * V],
                        in_=x_inner[(ti + 1) * ATILE + c * 128:
                                    (ti + 1) * ATILE + (c + 1) * 128]))
                need = ((ti + 2) * ATILE - 1) // (NIN // 8) + 1
                s.wait_ge(dw_xbf.h, 16 * min(need, 8))
                xaT_base[ti + 1] = dwT.idx
                toff = nb * NVC * ATILE
                for vck in range(NVC):
                    dwT.issue(s, lambda ti=ti, vck=vck, toff=toff: s.dma_start(
                        out=xaT[:, toff + vck * ATILE: toff + (vck + 1) * ATILE],
                        in_=xbf_dram[(ti + 1) * ATILE:(ti + 2) * ATILE,
                                     vck * 128:(vck + 1) * 128], transpose=True))

        @blk.vector
        def _(v, ti=ti, par=par, xoff=xoff, sbase=sbase):
            dX2.wait_issue(v, 2 * ti)
            dX2.wait_issue(v, 2 * ti + 1)
            for c in range(AC):
                xv = xa[:, xoff + c * V: xoff + (c + 1) * V]
                v.reduce_sum(stat[:, sbase + c: sbase + c + 1], xv, axis=AX.X)
                v.scalar_tensor_tensor(
                    xsq[:], xv, 1.0, xv, op0=ALU.mult, op1=ALU.mult,
                    accum_out=stat[:, sbase + 2 + c: sbase + 3 + c])
            v.drain()
            v.tensor_scalar(stat[:, sbase + 4: sbase + 6], stat[:, sbase: sbase + 2],
                            1.0 / V, None, op0=ALU.mult)
            v.drain()
            v.scalar_tensor_tensor(stat[:, sbase + 6: sbase + 8],
                                   stat[:, sbase + 4: sbase + 6], -1.0,
                                   stat[:, sbase + 4: sbase + 6],
                                   op0=ALU.mult, op1=ALU.mult)
            v.drain()
            vE.inc(v.scalar_tensor_tensor(stat[:, sbase + 6: sbase + 8],
                                          stat[:, sbase + 2: sbase + 4], 1.0 / V,
                                          stat[:, sbase + 6: sbase + 8],
                                          op0=ALU.mult, op1=ALU.add))

        @blk.scalar
        def _(sc, ti=ti, sbase=sbase):
            sc.wait_ge(vE.h, vE.n)
            sc.activation(stat[:, sbase + 8: sbase + 10],
                          stat[:, sbase + 6: sbase + 8], AF.Ln, bias=epscol[:, 0:1])
            sc.drain()
            ins = sc.activation(stat[:, sbase + 10: sbase + 12],
                                stat[:, sbase + 8: sbase + 10], AF.Exp, scale=-0.5)
            sc.drain()
            aE.inc(ins)

        @blk.tensor
        def _(t, ti=ti, par=par):
            dwT.wait_issue(t, xaT_base[ti] + NVC - 1)
            if ti >= 1:
                t.wait_ge(aE.h, evacA_done[ti - 1])
            toff = par * NVC * ATILE
            for c in range(AC):
                ps = pmA if c == 0 else pmB
                for kc in range(NVC):
                    xt = xaT[:, toff + kc * ATILE + c * 128:
                             toff + kc * ATILE + c * 128 + 128]
                    ins = t.matmul(ps[:, 0:DIM], xt, winT[:, kc * DIM:(kc + 1) * DIM],
                                   start=(kc == 0), stop=(kc == NVC - 1))
            tE.inc(ins)
            mmA_done[ti] = tE.n

        @blk.scalar
        def _(sc, ti=ti, par=par, sbase=sbase):
            sc.wait_ge(tE.h, mmA_done[ti])
            sc.wait_ge(aE.h, aE.n)
            if ti >= 2:
                dH2.wait_issue(sc, 2 * (ti - 2))
                dH2.wait_issue(sc, 2 * (ti - 2) + 1)
            for c in range(AC):
                ps = pmA if c == 0 else pmB
                rs = stat[:, sbase + 10 + c: sbase + 11 + c]
                sc.activation(habuf[:, (par * AC + c) * DIM:(par * AC + c + 1) * DIM],
                              ps[:, 0:DIM], AF.Copy, scale=rs)
            sc.drain()
            aE.inc(sc.copy(stat[:, sbase + 14:sbase + 15], stat[:, sbase:sbase + 1]))
            evacA_done[ti] = aE.n

        @blk.vector
        def _(v, ti=ti, par=par, sbase=sbase):
            v.wait_ge(aE.h, aE.n)
            v.scalar_tensor_tensor(stat[:, sbase + 12: sbase + 14],
                                   stat[:, sbase + 4: sbase + 6], -1.0,
                                   stat[:, sbase + 10: sbase + 12],
                                   op0=ALU.mult, op1=ALU.mult)
            v.drain()
            for c in range(AC):
                hb = habuf[:, (par * AC + c) * DIM:(par * AC + c + 1) * DIM]
                v.scalar_tensor_tensor(hb, abc[:, 0:DIM],
                                       stat[:, sbase + 12 + c: sbase + 13 + c],
                                       hb, op0=ALU.mult, op1=ALU.add)
            v.drain()
            for c in range(AC):
                hb = habuf[:, (par * AC + c) * DIM:(par * AC + c + 1) * DIM]
                ins = v.tensor_add(hb, hb, abc[:, DIM:2 * DIM])
            vE.inc(ins)

        @blk.sync
        def _(s, ti=ti, par=par):
            s.wait_ge(vE.h, vE.n)
            for c in range(AC):
                dH2.issue(s, lambda ti=ti, par=par, c=c: s.dma_start(
                    out=h_dram[ti * ATILE + c * 128: ti * ATILE + (c + 1) * 128],
                    in_=habuf[:, (par * AC + c) * DIM:(par * AC + c + 1) * DIM]))

    barrier()

    if "h" in dumps:
        @blk.sync
        def _(s):
            nrow = nat * ATILE
            dH2.wait_all(s)
            dmisc.dma(s.dma_start(out=dump_t["h"][0:nrow], in_=h_dram[0:nrow]))
            s.wait_ge(dmisc.h, dmisc.n)
        barrier()

    if stop_after == "A":
        _emit_dummy_out(blk, nc, out_t, bcast_pool, dmisc, vE)
        esA.close()
        es.close()
        return nc

    @blk.sync
    def _(s):
        dH2.wait_all(s)
        dX2.wait_all(s)
    barrier()
    esA.close()

    esB = contextlib.ExitStack()

    def sbufB(name, shape, dtype):
        return esB.enter_context(nc.sbuf_tensor(name, shape, dtype))

    CCP = CC + 1
    hbt = sbufB("hbt", [128, 4 * DIM], F32)
    bscr = sbufB("bscr", [128, 4 * DIM], F32)
    y126 = sbufB("y126", [128, NCH * DIM], BF16)
    yfbt = sbufB("yfbt", [128, 4 * DIM], F32)
    zbf = sbufB("zbf", [128, 4 * DIM], BF16)
    zT = sbufB("zT", [128, 2 * 4 * DIM], BF16)
    pch = sbufB("pch", [128, 16], F32)
    powc = sbufB("powc", [128, 2 * 4 * CCP], F32)
    apowv = sbufB("apowv", [128, 2 * 4 * NCH], F32)
    lcvv = sbufB("lcvv", [128, 8], F32)
    carryv = sbufB("carryv", [128, 8], F32)
    recvv = sbufB("recvv", [128, 8], F32)
    statev = sbufB("statev", [128, 16], F32)
    sire = sbufB("sire", [128, 4 * NCH], F32)
    siim = sbufB("siim", [128, 4 * NCH], F32)
    si_inr = sbufB("si_inr", [128, 4 * NCH], F32)
    si_ini = sbufB("si_ini", [128, 4 * NCH], F32)
    sc1 = sbufB("sc1", [128, 4 * NCH], F32)
    sc2 = sbufB("sc2", [128, 4 * NCH], F32)


    tl2 = sbufB("tl2", [128, 8], F32)
    tl2bf = sbufB("tl2bf", [128, 8], BF16)
    zrow = sbufB("zrow", [128, DIM], BF16)

    dHB = SemRing(nc, "dHB", 4)
    dY = SemRing(nc, "dY", 4)
    dCk = SemRing(nc, "dCk", 4)
    dTh = SemRing(nc, "dTh", 4)
    dT = SemRing(nc, "dT", 2)
    dYW = SemRing(nc, "dYW", 2)
    dYF = SemRing(nc, "dYF", 4)
    dHW = SemRing(nc, "dHW", 4)
    dPr = SemRing(nc, "dPr", 4)
    dSt = SemRing(nc, "dSt", 2)
    eC = Sem(nc, "eC")
    sE = Sem(nc, "sE")
    eTH = Sem(nc, "eTH")
    eZ = Sem(nc, "eZ")
    thin_dram = nc.dram_tensor("thin_dram", [2, DIM * NCH], F32, kind="Internal").ap()
    si_dram = nc.dram_tensor("si_dram", [2, NCH * DIM], BF16, kind="Internal").ap()
    misc2_dram = nc.dram_tensor("misc2_dram", [2, DIM], BF16, kind="Internal").ap()

    sreg_slot = nc.alloc_register(mybir.EngineType.SP, "sreg_slot")
    sreg_flagv = nc.alloc_register(mybir.EngineType.SP, "sreg_flagv")
    sreg_tgt = nc.alloc_register(mybir.EngineType.SP, "sreg_tgt")
    sreg_cond = nc.alloc_register(mybir.EngineType.SP, "sreg_cond")
    sreg_dmc = nc.alloc_register(mybir.EngineType.SP, "sreg_dmc")
    sreg_mycol = nc.alloc_register(mybir.EngineType.SP, "sreg_mycol")
    sreg_pollc = nc.alloc_register(mybir.EngineType.SP, "sreg_pollc")
    sreg_hoff = nc.alloc_register(mybir.EngineType.SP, "sreg_hoff")
    sreg_pub = nc.alloc_register(mybir.EngineType.SP, "sreg_pub")
    dm = Sem(nc, "dm")

    d_b2 = SemRing(nc, "d_b2", 2)
    dZB = SemRing(nc, "dZB", 4)
    dZT = SemRing(nc, "dZT", 4)
    dHB2 = SemRing(nc, "dHB2", 4)
    dH2g = SemRing(nc, "dH2g", 4)
    dYg = SemRing(nc, "dYg", 4)
    fE = Sem(nc, "fE")
    hE = Sem(nc, "hE")
    zt_bases = {}
    f1_done = {}
    zbA = {}
    h2base = {}
    hb2base = {}
    hb4 = {}
    yf4 = {}
    he_done = {}
    b1v2 = {}
    dyb1 = {}

    def snap_pub():
        return nc.sync.snap(sreg_pub, min_val=0, max_val=NIN)

    def snap_hoff():
        return nc.sync.snap(sreg_hoff, min_val=0, max_val=L_FULL - NOUT)

    def snap_mycol():
        return nc.sync.snap(sreg_mycol, min_val=8, max_val=9)

    def snap_pollc():
        return nc.sync.snap(sreg_pollc, min_val=8, max_val=9)

    @blk.sync
    def _(s):
        nc.sync.reg_load(sreg_slot, flagv_sb[0:1, 12:13])
        nc.sync.reg_load(sreg_mycol, flagv_sb[0:1, 10:11])
        nc.sync.reg_load(sreg_pollc, flagv_sb[0:1, 11:12])
        nc.sync.reg_load(sreg_hoff, flagv_sb[0:1, 14:15])
        nc.sync.reg_load(sreg_pub, myflag_sb[0:1, 15:16])
        nc.sync.reg_mov(sreg_dmc, 0)

    @blk.vector
    def _(v):
        vE.inc(v.memset(recvv[:], 0.0))

    for l in range(n_layers):
        es23 = contextlib.ExitStack()
        tbuf = es23.enter_context(nc.sbuf_tensor(f"tbuf{l}", [128, 2 * TGRP * 128], BF16))
        yfst = es23.enter_context(nc.sbuf_tensor(f"yfst{l}", [128, TGRP * NCH], F32))
        thst = es23.enter_context(nc.sbuf_tensor(f"thst{l}", [2, 2 * 462], F32))
        swpair = es23.enter_context(nc.sbuf_tensor(f"swpair{l}", [128, 2 * DIM], BF16))
        swbf = es23.enter_context(nc.sbuf_tensor(f"swbf{l}", [128, 2 * DIM], BF16))
        crre_bf = es23.enter_context(nc.sbuf_tensor(f"crre{l}", [128, 4 * CC], BF16))
        powrow_sb = es23.enter_context(nc.sbuf_tensor(f"powrow{l}", [128, 4 * MODP], BF16))
        sirow = es23.enter_context(nc.sbuf_tensor(f"sirow{l}", [128, 2 * DIM], F32))
        crimn_bf = es23.enter_context(nc.sbuf_tensor(f"crimn{l}", [128, 4 * CC], BF16))
        sibf = es23.enter_context(nc.sbuf_tensor(f"sibf{l}", [128, 2 * DIM], BF16))

        @blk.sync
        def _(s, l=l):
            for (k, srcp, dstt, dcol) in ((0, phzr_i, pch, 0), (1, phzi_i, pch, 4),
                                          (2, lcr_i, lcvv, 0), (3, lci_i, lcvv, 4)):
                srcv = dataclasses.replace(srcp.flatten(), offset=l * DIM,
                                           ap=[[1, 128], [128, 4]])
                dstv = dataclasses.replace(
                    dstt[:, dcol:dcol + 4], ap=[[dstt[:].ap[0][0], 128], [1, 4]])
                dPr.issue(s, lambda dstv=dstv, srcv=srcv: s.dma_start(
                    out=dstv, in_=srcv, allow_slow_non_contiguous=True))
            dPr.issue(s, lambda l=l: s.dma_start(out=sirow[0:1, 0:DIM], in_=ln1w_i[l:l + 1]))
            dPr.issue(s, lambda l=l: s.dma_start(out=sirow[0:1, DIM:2 * DIM], in_=ln1b_i[l:l + 1]))
            dPr.issue(s, lambda l=l: s.dma_start(out=sirow[32:33, 0:DIM], in_=ln2w_i[l:l + 1]))
            dPr.issue(s, lambda l=l: s.dma_start(out=sirow[32:33, DIM:2 * DIM], in_=ln2b_i[l:l + 1]))
            dPr.issue(s, lambda l=l: s.dma_start(out=sirow[64:65, 0:DIM], in_=scl_i[l:l + 1]))
            srcb1 = dataclasses.replace(b1_i.flatten(), offset=l * DFF,
                                        ap=[[1, 128], [128, NFC]])
            dstb1 = dataclasses.replace(b1col[:], ap=[[NFC, 128], [1, NFC]])
            dPr.issue(s, lambda srcb1=srcb1, dstb1=dstb1: s.dma_start(
                out=dstb1, in_=srcb1, allow_slow_non_contiguous=True))

        @blk.gpsimd
        def _(g_, l=l):
            d_b2.issue(g_, lambda l=l: g_.dma_start(out=b2_row[:], in_=b2_i[l:l + 1]))

        @blk.tensor
        def _(t, l=l):
            dPr.wait_all(t)
            t.matmul(pmA[:, 0:DIM], onesf[0:1], sirow[0:1, 0:DIM], start=True, stop=True)
            t.matmul(pmB[:, 0:DIM], onesf[0:1], sirow[0:1, DIM:2 * DIM], start=True, stop=True)
            t.matmul(pmC[:, 0:DIM], onesf[32:33], sirow[32:33, 0:DIM], start=True, stop=True)
            t.matmul(pmD[:, 0:DIM], onesf[32:33], sirow[32:33, DIM:2 * DIM], start=True, stop=True)
            ins = t.matmul(pmE[:, 0:DIM], onesf[64:65], sirow[64:65, 0:DIM], start=True, stop=True)
            tE.inc(ins)

        @blk.vector
        def _(v, l=l):
            v.wait_ge(tE.h, tE.n)
            v.tensor_copy(bc(BC_L1W), pmA[:, 0:DIM])
            v.tensor_copy(bc(BC_L1B), pmB[:, 0:DIM])
            v.tensor_copy(bc(BC_L2W), pmC[:, 0:DIM])
            v.tensor_copy(bc(BC_L2B), pmD[:, 0:DIM])
            v.tensor_copy(bc(BC_SCL), pmE[:, 0:DIM])
            v.memset(zrow[:], 0.0)
            re_ = pch[:, 0:4]; im_ = pch[:, 4:8]
            r2 = pch[:, 8:12]
            v.drain()
            v.scalar_tensor_tensor(r2, re_, 1.0, re_, op0=ALU.mult, op1=ALU.mult)
            v.scalar_tensor_tensor(sc1[:, 0:4], im_, 1.0, im_, op0=ALU.mult, op1=ALU.mult)
            v.drain()
            vE.inc(v.tensor_add(r2, r2, sc1[:, 0:4]))

        @blk.scalar
        def _(sc, l=l):
            sc.wait_ge(vE.h, vE.n)
            sc.activation(pch[:, 12:16], pch[:, 8:12], AF.Ln)
            sc.drain()
            sc.activation(pch[:, 8:12], pch[:, 12:16], AF.Exp, scale=0.5)
            sc.drain()
            aE.inc(sc.activation(pch[:, 12:16], pch[:, 8:12], AF.Tanh))

        @blk.vector
        def _(v, l=l):
            v.wait_ge(aE.h, aE.n)
            rinv = sc1[:, 0:4]
            v.reciprocal(rinv, pch[:, 8:12])
            v.drain()
            tor = sc1[:, 4:8]
            v.tensor_mul(tor, pch[:, 12:16], rinv)
            v.drain()
            v.tensor_mul(pch[:, 0:4], pch[:, 0:4], tor)
            v.tensor_mul(pch[:, 4:8], pch[:, 4:8], tor)
            v.drain()
            PR, PI = 0, 4 * CCP
            for g in range(4):
                v.memset(powc[:, PR + g * CCP: PR + g * CCP + 1], 1.0)
                v.memset(powc[:, PI + g * CCP: PI + g * CCP + 1], 0.0)
            v.drain()
            for g in range(4):
                mr = pch[:, g:g + 1]; mi = pch[:, 4 + g:5 + g]
                br = powc[:, PR + g * CCP: PR + g * CCP + 1]
                bi = powc[:, PI + g * CCP: PI + g * CCP + 1]
                v.tensor_scalar(sc1[:, g * NCH: g * NCH + 1], br, mr, None, op0=ALU.mult)
                v.tensor_scalar(sc2[:, g * NCH: g * NCH + 1], bi, mi, None, op0=ALU.mult)
                v.tensor_scalar(si_inr[:, g * NCH: g * NCH + 1], bi, mr, None, op0=ALU.mult)
                v.tensor_scalar(si_ini[:, g * NCH: g * NCH + 1], br, mi, None, op0=ALU.mult)
            v.drain()
            for g in range(4):
                v.tensor_sub(powc[:, PR + g * CCP + 1: PR + g * CCP + 2],
                             sc1[:, g * NCH: g * NCH + 1],
                             sc2[:, g * NCH: g * NCH + 1])
                v.tensor_add(powc[:, PI + g * CCP + 1: PI + g * CCP + 2],
                             si_inr[:, g * NCH: g * NCH + 1],
                             si_ini[:, g * NCH: g * NCH + 1])
            v.drain()
            K = 2
            while K < CCP:
                w = min(K - 1, CCP - K)
                for g in range(4):
                    mr = powc[:, PR + g * CCP + K - 1: PR + g * CCP + K]
                    mi = powc[:, PI + g * CCP + K - 1: PI + g * CCP + K]
                    br = powc[:, PR + g * CCP + 1: PR + g * CCP + 1 + w]
                    bi = powc[:, PI + g * CCP + 1: PI + g * CCP + 1 + w]
                    v.tensor_scalar(sc1[:, g * NCH: g * NCH + w], br, mr, None, op0=ALU.mult)
                    v.tensor_scalar(sc2[:, g * NCH: g * NCH + w], bi, mi, None, op0=ALU.mult)
                    v.tensor_scalar(si_inr[:, g * NCH: g * NCH + w], bi, mr, None, op0=ALU.mult)
                    v.tensor_scalar(si_ini[:, g * NCH: g * NCH + w], br, mi, None, op0=ALU.mult)
                v.drain()
                for g in range(4):
                    v.tensor_sub(powc[:, PR + g * CCP + K: PR + g * CCP + K + w],
                                 sc1[:, g * NCH: g * NCH + w],
                                 sc2[:, g * NCH: g * NCH + w])
                    v.tensor_add(powc[:, PI + g * CCP + K: PI + g * CCP + K + w],
                                 si_inr[:, g * NCH: g * NCH + w],
                                 si_ini[:, g * NCH: g * NCH + w])
                v.drain()
                K += w
            for g in range(4):
                v.tensor_copy(crre_bf[:, g * CC:(g + 1) * CC],
                              powc[:, PR + g * CCP + 1: PR + g * CCP + 1 + CC])
                v.tensor_scalar(crimn_bf[:, g * CC:(g + 1) * CC],
                                powc[:, PI + g * CCP + 1: PI + g * CCP + 1 + CC],
                                -1.0, None, op0=ALU.mult)
            v.drain()
            AR, AI = 0, 4 * NCH
            for g in range(4):
                v.memset(apowv[:, AR + g * NCH: AR + g * NCH + 1], 1.0)
                v.memset(apowv[:, AI + g * NCH: AI + g * NCH + 1], 0.0)
            v.drain()
            for g in range(4):
                mr = powc[:, PR + g * CCP + CC: PR + g * CCP + CCP]
                mi = powc[:, PI + g * CCP + CC: PI + g * CCP + CCP]
                br = apowv[:, AR + g * NCH: AR + g * NCH + 1]
                bi = apowv[:, AI + g * NCH: AI + g * NCH + 1]
                v.tensor_scalar(sc1[:, g * NCH: g * NCH + 1], br, mr, None, op0=ALU.mult)
                v.tensor_scalar(sc2[:, g * NCH: g * NCH + 1], bi, mi, None, op0=ALU.mult)
                v.tensor_scalar(si_inr[:, g * NCH: g * NCH + 1], bi, mr, None, op0=ALU.mult)
                v.tensor_scalar(si_ini[:, g * NCH: g * NCH + 1], br, mi, None, op0=ALU.mult)
            v.drain()
            for g in range(4):
                v.tensor_sub(apowv[:, AR + g * NCH + 1: AR + g * NCH + 2],
                             sc1[:, g * NCH: g * NCH + 1],
                             sc2[:, g * NCH: g * NCH + 1])
                v.tensor_add(apowv[:, AI + g * NCH + 1: AI + g * NCH + 2],
                             si_inr[:, g * NCH: g * NCH + 1],
                             si_ini[:, g * NCH: g * NCH + 1])
            v.drain()
            K = 2
            while K < NCH:
                w = min(K - 1, NCH - K)
                for g in range(4):
                    mr = apowv[:, AR + g * NCH + K - 1: AR + g * NCH + K]
                    mi = apowv[:, AI + g * NCH + K - 1: AI + g * NCH + K]
                    br = apowv[:, AR + g * NCH + 1: AR + g * NCH + 1 + w]
                    bi = apowv[:, AI + g * NCH + 1: AI + g * NCH + 1 + w]
                    v.tensor_scalar(sc1[:, g * NCH: g * NCH + w], br, mr, None, op0=ALU.mult)
                    v.tensor_scalar(sc2[:, g * NCH: g * NCH + w], bi, mi, None, op0=ALU.mult)
                    v.tensor_scalar(si_inr[:, g * NCH: g * NCH + w], bi, mr, None, op0=ALU.mult)
                    v.tensor_scalar(si_ini[:, g * NCH: g * NCH + w], br, mi, None, op0=ALU.mult)
                v.drain()
                for g in range(4):
                    v.tensor_sub(apowv[:, AR + g * NCH + K: AR + g * NCH + K + w],
                                 sc1[:, g * NCH: g * NCH + w],
                                 sc2[:, g * NCH: g * NCH + w])
                    v.tensor_add(apowv[:, AI + g * NCH + K: AI + g * NCH + K + w],
                                 si_inr[:, g * NCH: g * NCH + w],
                                 si_ini[:, g * NCH: g * NCH + w])
                v.drain()
                K += w
            v.memset(powrow_sb[:], 0.0)
            v.drain()
            for g in range(4):
                v.tensor_copy(powrow_sb[:, g * MODP + CC - 1: g * MODP + CC - 1 + CC],
                              powc[:, PR + g * CCP: PR + g * CCP + CC])
            v.memset(swbf[:], 0.0)
            v.drain()
            for g in range(4):
                rsrc_r = dataclasses.replace(
                    powc[:], offset=PR + g * CCP + CC - 1,
                    ap=[list(powc[:].ap[0]), [-1, CC]])
                rsrc_i = dataclasses.replace(
                    powc[:], offset=PI + g * CCP + CC - 1,
                    ap=[list(powc[:].ap[0]), [-1, CC]])
                v.tensor_copy(swbf[:, g * 128: g * 128 + CC], rsrc_r)
                v.tensor_copy(swbf[:, DIM + g * 128: DIM + g * 128 + CC], rsrc_i)
            v.drain()
            vE.inc(v.tensor_copy(sc1[:, 0:1], sc1[:, 0:1]))

        @blk.gpsimd
        def _(g_, l=l, tbuf=tbuf, swpair=swpair):
            g_.memset(tbuf[:], 0.0)
            g_.drain()
            pE.inc(g_.memset(swpair[:], 0.0))

        eZ_base = eZ.n
        tE_base = tE.n

        @blk.tensor
        def _(t, l=l, eZ_base=eZ_base):
            t.wait_ge(vE.h, vE.n)
            t.wait_ge(pE.h, pE.n)
            for k in range(8):
                comp, g = k % 2, k // 2
                if k >= 2:
                    t.wait_ge(eZ.h, eZ_base + k - 1)
                pt = ptA if k % 2 == 0 else ptB
                ins = t.transpose(pt[:, 0:128],
                                  swbf[:, comp * DIM + g * 128: comp * DIM + g * 128 + 128],
                                  ident[:])
                tE.inc(ins)

        @blk.vector
        def _(v, l=l, tE_base=tE_base):
            for k in range(8):
                comp, g = k % 2, k // 2
                v.wait_ge(tE.h, tE_base + k + 1)
                pt = ptA if k % 2 == 0 else ptB
                dst = dataclasses.replace(
                    swpair[:], offset=g * 256 + comp,
                    ap=[list(swpair[:].ap[0]), [2, 128]])
                ins = v.tensor_copy(dst, pt[:, 0:128])
                eZ.inc(ins)

        @blk.sync
        def _(s, l=l):
            s.wait_ge(vE.h, vE.n)
            srcp = dataclasses.replace(
                powrow_sb[:], ap=[[4 * MODP, 128], [MODP, 4], [1, MODP]])
            dstp = dataclasses.replace(
                powrow_dram.flatten(), offset=0,
                ap=[[MODP, 128], [128 * MODP, 4], [1, MODP]])
            dmisc.dma(s.dma_start(out=dstp, in_=srcp))
            s.wait_ge(dmisc.h, dmisc.n)
            dmisc.dma(s.dma_start(out=big_dram[:, 0:MODP], in_=powrow_dram))
            s.wait_ge(dmisc.h, dmisc.n)
            n = MODP
            while n < BIGN:
                m_ = min(n, BIGN - n)
                dmisc.dma(s.dma_start(out=big_dram[:, n:n + m_],
                                      in_=big_dram[:, 0:m_]))
                s.wait_ge(dmisc.h, dmisc.n)
                n += m_

        barrier()

        if l == 0:
            @blk.sync
            def _(s, l=l):
                dmisc.dma(s.dma_start(out=y_dram[NIN:GRID], in_=zrow[0:GRID - NIN, :]))
                s.wait_ge(dmisc.h, dmisc.n)
                for ti in range(4):
                    dHB.issue(s, lambda ti=ti: s.dma_start(
                        out=hbt[:, ti * DIM:(ti + 1) * DIM],
                        in_=h_dram[ti * 128:(ti + 1) * 128]))

        for st_ in (range(NST) if l == 0 else ()):
            sb_ = 16 + (st_ % 2) * 40

            @blk.vector
            def _(v, l=l, st_=st_, sb_=sb_):
                dHB.wait_issue(v, 128 * l + st_ * 4 + 3)
                for q in range(4):
                    hv = hbt[:, q * DIM:(q + 1) * DIM]
                    v.reduce_sum(stat[:, sb_ + q:sb_ + q + 1], hv, axis=AX.X)
                    v.scalar_tensor_tensor(bscr[:, q * DIM:(q + 1) * DIM], hv, 1.0,
                                           hv, op0=ALU.mult, op1=ALU.mult,
                                           accum_out=stat[:, sb_ + 4 + q:sb_ + 5 + q])
                v.drain()
                v.tensor_scalar(stat[:, sb_ + 8:sb_ + 12], stat[:, sb_:sb_ + 4],
                                1.0 / DIM, None, op0=ALU.mult)
                v.drain()
                v.scalar_tensor_tensor(stat[:, sb_ + 12:sb_ + 16],
                                       stat[:, sb_ + 8:sb_ + 12], -1.0,
                                       stat[:, sb_ + 8:sb_ + 12],
                                       op0=ALU.mult, op1=ALU.mult)
                v.drain()
                vE.inc(v.scalar_tensor_tensor(stat[:, sb_ + 12:sb_ + 16],
                                              stat[:, sb_ + 4:sb_ + 8], 1.0 / DIM,
                                              stat[:, sb_ + 12:sb_ + 16],
                                              op0=ALU.mult, op1=ALU.add))

            @blk.scalar
            def _(sc, l=l, st_=st_, sb_=sb_):
                sc.wait_ge(vE.h, vE.n)
                sc.activation(stat[:, sb_ + 16:sb_ + 20], stat[:, sb_ + 12:sb_ + 16],
                              AF.Ln, bias=epscol[:, 0:1])
                sc.drain()
                ins = sc.activation(stat[:, sb_ + 20:sb_ + 24],
                                    stat[:, sb_ + 16:sb_ + 20], AF.Exp, scale=-0.5)
                sc.drain()
                aE.inc(ins)

            @blk.vector
            def _(v, l=l, st_=st_, sb_=sb_):
                v.wait_ge(aE.h, aE.n)
                v.scalar_tensor_tensor(stat[:, sb_ + 24:sb_ + 28],
                                       stat[:, sb_ + 8:sb_ + 12], -1.0,
                                       stat[:, sb_ + 20:sb_ + 24],
                                       op0=ALU.mult, op1=ALU.mult)
                v.drain()
                for q in range(4):
                    v.tensor_scalar(bscr[:, q * DIM:(q + 1) * DIM],
                                    hbt[:, q * DIM:(q + 1) * DIM],
                                    stat[:, sb_ + 20 + q:sb_ + 21 + q],
                                    stat[:, sb_ + 24 + q:sb_ + 25 + q],
                                    op0=ALU.mult, op1=ALU.add)
                v.drain()
                for q in range(4):
                    v.tensor_mul(bscr[:, q * DIM:(q + 1) * DIM],
                                 bscr[:, q * DIM:(q + 1) * DIM], bc(BC_L1W))
                v.drain()
                for q in range(4):
                    ins = v.tensor_add(bscr[:, q * DIM:(q + 1) * DIM],
                                       bscr[:, q * DIM:(q + 1) * DIM], bc(BC_L1B))
                vE.inc(ins)

            @blk.scalar
            def _(sc, l=l, st_=st_):
                sc.wait_ge(vE.h, vE.n)
                for q in range(4):
                    ti = st_ * 4 + q
                    if l * NBT + ti >= 4:
                        dY.wait_issue(sc, l * NBT + ti - 4)
                    ins = sc.activation(zbf[:, q * DIM:(q + 1) * DIM],
                                        bscr[:, q * DIM:(q + 1) * DIM], AF.Silu)
                    aE.inc(ins)

            @blk.sync
            def _(s, l=l, st_=st_):
                base_aE = aE.n - 4
                for q in range(4):
                    ti = st_ * 4 + q
                    s.wait_ge(aE.h, base_aE + q + 1)
                    dY.issue(s, lambda ti=ti, q=q: s.dma_start(
                        out=y_dram[ti * 128:(ti + 1) * 128],
                        in_=zbf[:, q * DIM:(q + 1) * DIM]))
                    if ti + 4 < NBT:
                        dHB.issue(s, lambda ti=ti, q=q: s.dma_start(
                            out=hbt[:, q * DIM:(q + 1) * DIM],
                            in_=h_dram[(ti + 4) * 128:(ti + 5) * 128]))

        barrier()

        @blk.sync
        def _(s, l=l):
            dY.wait_all(s)
            dYg.wait_all(s)
            for c in range(NCH):
                dCk.issue(s, lambda c=c: s.dma_start(
                    out=y126[0:126, c * DIM:(c + 1) * DIM],
                    in_=y_dram[c * CC: c * CC + CC]))
            if l == 0:
                for rep in range(NCH):
                    dmisc.dma(s.dma_start(
                        out=y126[126:128, rep * DIM:(rep + 1) * DIM],
                        in_=zrow[0:2, 0:DIM]))
                    s.wait_ge(dmisc.h, dmisc.n)

        thin_tE = tE.n
        thin_eTH = eTH.n

        @blk.tensor
        def _(t, l=l, thin_eTH=thin_eTH):
            dCk.wait_all(t)
            t.wait_ge(dmisc.h, dmisc.n)
            t.wait_ge(eZ.h, eZ.n)
            for ch in range(DIM):
                b = ch // 7
                if ch % 7 == 0 and b >= 2:
                    t.wait_ge(eTH.h, thin_eTH + b - 1)
                ps = pmE if b % 2 == 0 else pmF
                rhs = dataclasses.replace(
                    y126[:], offset=ch, ap=[list(y126[:].ap[0]), [DIM, NCH]])
                ins = t.matmul(ps[0:2, (ch % 7) * 66:(ch % 7) * 66 + NCH],
                               swpair[:, ch * 2: ch * 2 + 2], rhs,
                               start=True, stop=True)
                if ch % 7 == 6 or ch == DIM - 1:
                    tE.inc(ins)

        NBATCH = (DIM + 6) // 7

        thin_dTh0 = dTh.idx

        @blk.vector
        def _(v, l=l, thin_tE=thin_tE, thin_dTh0=thin_dTh0):
            for b in range(NBATCH):
                v.wait_ge(tE.h, thin_tE + b + 1)
                if b >= 2:
                    dTh.wait_issue(v, thin_dTh0 + b - 2)
                ps = pmE if b % 2 == 0 else pmF
                nch_b = min(7, DIM - b * 7)
                ins = v.tensor_copy(thst[0:2, (b % 2) * 462:(b % 2) * 462 + nch_b * 66],
                                    ps[0:2, 0:nch_b * 66])
                eTH.inc(ins)

        @blk.sync
        def _(s, l=l, thin_eTH=thin_eTH):
            for b in range(NBATCH):
                s.wait_ge(eTH.h, thin_eTH + b + 1)
                nch_b = min(7, DIM - b * 7)
                src = thst[0:2, (b % 2) * 462:(b % 2) * 462 + nch_b * 66]
                dst = dataclasses.replace(
                    thin_dram.flatten(), offset=b * 7 * 66,
                    ap=[[DIM * NCH, 2], [1, nch_b * 66]])
                dTh.issue(s, lambda src=src, dst=dst: s.dma_start(out=dst, in_=src))
            dTh.wait_all(s)
            for comp, dstt in ((0, sire), (1, siim)):
                srcv = dataclasses.replace(
                    thin_dram.flatten(), offset=comp * DIM * NCH,
                    ap=[[NCH, 128], [128 * NCH, 4], [1, NCH]])
                dstv = dataclasses.replace(
                    dstt[:], ap=[[4 * NCH, 128], [NCH, 4], [1, NCH]])
                dmisc.dma(s.dma_start(out=dstv, in_=srcv))
                s.wait_ge(dmisc.h, dmisc.n)
            dmisc.dma(s.dma_start(out=misc2_dram, in_=y126[0:2, 65 * DIM:66 * DIM]))
            s.wait_ge(dmisc.h, dmisc.n)
            srct = dataclasses.replace(misc2_dram.flatten(), offset=0,
                                       ap=[[1, 128], [DIM, 2], [128, 4]])
            dstt = dataclasses.replace(tl2bf[:], ap=[[8, 128], [4, 2], [1, 4]])
            dmisc.dma(s.dma_start(out=dstt, in_=srct, allow_slow_non_contiguous=True))
            s.wait_ge(dmisc.h, dmisc.n)

        barrier()

        @blk.vector
        def _(v, l=l):
            AR, AI = 0, 4 * NCH
            lvl = 1
            while lvl < NCH:
                for g in range(4):
                    ar = apowv[:, AR + g * NCH + lvl: AR + g * NCH + lvl + 1]
                    ai = apowv[:, AI + g * NCH + lvl: AI + g * NCH + lvl + 1]
                    w = NCH - lvl
                    srs = sire[:, g * NCH: g * NCH + w]
                    sis = siim[:, g * NCH: g * NCH + w]
                    v.tensor_scalar(sc1[:, g * NCH: g * NCH + w], srs, ar, None, op0=ALU.mult)
                    v.tensor_scalar(sc2[:, g * NCH: g * NCH + w], sis, ai, None, op0=ALU.mult)
                    v.tensor_scalar(si_inr[:, g * NCH: g * NCH + w], sis, ar, None, op0=ALU.mult)
                    v.tensor_scalar(si_ini[:, g * NCH: g * NCH + w], srs, ai, None, op0=ALU.mult)
                v.drain()
                for g in range(4):
                    w = NCH - lvl
                    dr = sire[:, g * NCH + lvl: g * NCH + NCH]
                    di = siim[:, g * NCH + lvl: g * NCH + NCH]
                    v.tensor_add(dr, dr, sc1[:, g * NCH: g * NCH + w])
                    v.tensor_add(di, di, si_inr[:, g * NCH: g * NCH + w])
                v.drain()
                for g in range(4):
                    w = NCH - lvl
                    dr = sire[:, g * NCH + lvl: g * NCH + NCH]
                    di = siim[:, g * NCH + lvl: g * NCH + NCH]
                    v.tensor_sub(dr, dr, sc2[:, g * NCH: g * NCH + w])
                    v.tensor_add(di, di, si_ini[:, g * NCH: g * NCH + w])
                v.drain()
                lvl *= 2
            v.tensor_copy(tl2[:, 0:8], tl2bf[:, 0:8])
            v.drain()
            PR, PI = 0, 4 * CCP
            for g in range(4):
                a2r = powc[:, PR + g * CCP + 2: PR + g * CCP + 3]
                a2i = powc[:, PI + g * CCP + 2: PI + g * CCP + 3]
                s64r = sire[:, g * NCH + 64: g * NCH + 65]
                s64i = siim[:, g * NCH + 64: g * NCH + 65]
                y0 = tl2[:, g:g + 1]
                y1 = tl2[:, 4 + g:5 + g]
                pr_ = pch[:, g:g + 1]
                pi_ = pch[:, 4 + g:5 + g]
                er = statev[:, g:g + 1]
                ei = statev[:, 4 + g:5 + g]
                v.tensor_scalar(er, s64r, a2r, None, op0=ALU.mult)
                v.tensor_scalar(sc1[:, 0:1], s64i, a2i, None, op0=ALU.mult)
                v.tensor_scalar(sc2[:, 0:1], y0, pr_, None, op0=ALU.mult)
                v.drain()
                v.tensor_sub(er, er, sc1[:, 0:1])
                v.drain()
                v.tensor_add(er, er, sc2[:, 0:1])
                v.drain()
                v.tensor_add(er, er, y1)
                v.tensor_scalar(ei, s64i, a2r, None, op0=ALU.mult)
                v.tensor_scalar(sc1[:, 0:1], s64r, a2i, None, op0=ALU.mult)
                v.tensor_scalar(sc2[:, 0:1], y0, pi_, None, op0=ALU.mult)
                v.drain()
                v.tensor_add(ei, ei, sc1[:, 0:1])
                v.drain()
                v.tensor_add(ei, ei, sc2[:, 0:1])
                v.drain()
            v.drain()
            vE.inc(v.tensor_copy(statev[:, 8:9], statev[:, 0:1]))

        @blk.sync
        def _(s, l=l):
            s.wait_ge(vE.h, vE.n)
            def cdma(mk):
                mk().then_inc(dm.h, 16)
                nc.sync.reg_add(sreg_dmc, sreg_dmc, 16)
                s.wait_ge(dm.h, sreg_dmc)
            with nc.sync.If(sreg_slot):
                srcv = dataclasses.replace(statev[:, 0:8],
                                           ap=[[statev[:].ap[0][0], 128], [4, 2], [1, 4]])
                dstv = dataclasses.replace(sh_carry.flatten(),
                                           offset=(l * 4 + 2) * DIM,
                                           ap=[[1, 128], [DIM, 2], [128, 4]])
                cdma(lambda srcv=srcv, dstv=dstv: s.dma_start(
                    out=dstv, in_=srcv, allow_slow_non_contiguous=True))
                cdma(lambda l=l: s.dma_start(
                    out=sh_flags[0:1, 2 * l + 1: 2 * l + 2],
                    in_=myflag_sb[0:1, 2 * l + 1: 2 * l + 2],
                    allow_slow_non_contiguous=True))
                nc.sync.reg_mov(sreg_cond, 1)
                nc.sync.reg_load(sreg_tgt, flagv_sb[0:1, 2 * l: 2 * l + 1])
                with nc.sync.While(sreg_cond):
                    cdma(lambda l=l: s.dma_start(
                        out=polltmp[0:1, 0:1],
                        in_=sh_flags[0:1, 2 * l: 2 * l + 1],
                        allow_slow_non_contiguous=True))
                    nc.sync.reg_load(sreg_flagv, polltmp[0:1, 0:1])
                    nc.sync.reg_alu(sreg_cond, sreg_flagv, sreg_tgt, op=ALU.not_equal)
                srcv2 = dataclasses.replace(sh_carry.flatten(),
                                            offset=(l * 4 + 0) * DIM,
                                            ap=[[1, 128], [DIM, 2], [128, 4]])
                dstv2 = dataclasses.replace(recvv[:, 0:8],
                                            ap=[[recvv[:].ap[0][0], 128], [4, 2], [1, 4]])
                cdma(lambda srcv2=srcv2, dstv2=dstv2: s.dma_start(
                    out=dstv2, in_=srcv2, allow_slow_non_contiguous=True))
            with nc.sync.Else():
                srcv = dataclasses.replace(statev[:, 0:8],
                                           ap=[[statev[:].ap[0][0], 128], [4, 2], [1, 4]])
                dstv = dataclasses.replace(sh_carry.flatten(),
                                           offset=(l * 4 + 0) * DIM,
                                           ap=[[1, 128], [DIM, 2], [128, 4]])
                cdma(lambda srcv=srcv, dstv=dstv: s.dma_start(
                    out=dstv, in_=srcv, allow_slow_non_contiguous=True))
                cdma(lambda l=l: s.dma_start(
                    out=sh_flags[0:1, 2 * l: 2 * l + 1],
                    in_=myflag_sb[0:1, 2 * l: 2 * l + 1],
                    allow_slow_non_contiguous=True))
            nc.sync.sem_inc(sE.h, 1)
        sE.n += 1

        @blk.vector
        def _(v, l=l):
            v.wait_ge(sE.h, sE.n)
            v.tensor_scalar(carryv[:, 0:8], recvv[:, 0:8], wsel[:, 0:1], None,
                            op0=ALU.mult)
            v.tensor_scalar(sc1[:, 0:8], lcvv[:, 0:8], wsel[:, 1:2], None,
                            op0=ALU.mult)
            v.drain()
            v.tensor_add(carryv[:, 0:8], carryv[:, 0:8], sc1[:, 0:8])
            v.drain()
            AR, AI = 0, 4 * NCH
            for g in range(4):
                cr = carryv[:, g:g + 1]
                ci = carryv[:, 4 + g:5 + g]
                apr = apowv[:, AR + g * NCH: AR + (g + 1) * NCH]
                api = apowv[:, AI + g * NCH: AI + (g + 1) * NCH]
                tr = si_inr[:, g * NCH:(g + 1) * NCH]
                ti_ = si_ini[:, g * NCH:(g + 1) * NCH]
                v.tensor_scalar(tr, apr, cr, None, op0=ALU.mult)
                v.tensor_scalar(sc1[:, g * NCH:(g + 1) * NCH], api, ci, None, op0=ALU.mult)
                v.tensor_scalar(ti_, api, cr, None, op0=ALU.mult)
                v.tensor_scalar(sc2[:, g * NCH:(g + 1) * NCH], apr, ci, None, op0=ALU.mult)
            v.drain()
            for g in range(4):
                tr = si_inr[:, g * NCH:(g + 1) * NCH]
                ti_ = si_ini[:, g * NCH:(g + 1) * NCH]
                v.tensor_sub(tr, tr, sc1[:, g * NCH:(g + 1) * NCH])
                v.tensor_add(ti_, ti_, sc2[:, g * NCH:(g + 1) * NCH])
            v.drain()
            for g in range(4):
                v.tensor_add(si_inr[:, g * NCH + 1:(g + 1) * NCH],
                             si_inr[:, g * NCH + 1:(g + 1) * NCH],
                             sire[:, g * NCH:(g + 1) * NCH - 1])
                v.tensor_add(si_ini[:, g * NCH + 1:(g + 1) * NCH],
                             si_ini[:, g * NCH + 1:(g + 1) * NCH],
                             siim[:, g * NCH:(g + 1) * NCH - 1])
            v.drain()
            for g in range(4):
                v.tensor_copy(zbf[:, g * 128: g * 128 + NCH],
                              si_inr[:, g * NCH:(g + 1) * NCH])
                v.tensor_copy(zbf[:, DIM + g * 128: DIM + g * 128 + NCH],
                              si_ini[:, g * NCH:(g + 1) * NCH])
            v.drain()
            vE.inc(v.tensor_copy(sc1[:, 0:1], sc1[:, 0:1]))

        siz_eZ = eZ.n
        siz_tE = tE.n

        @blk.tensor
        def _(t, l=l, siz_eZ=siz_eZ):
            t.wait_ge(vE.h, vE.n)
            for k in range(8):
                comp, g = k // 4, k % 4
                if k >= 2:
                    t.wait_ge(eZ.h, siz_eZ + k - 1)
                pt = ptA if k % 2 == 0 else ptB
                ins = t.transpose(pt[:, 0:128],
                                  zbf[:, comp * DIM + g * 128: comp * DIM + g * 128 + 128],
                                  ident[:])
                tE.inc(ins)

        @blk.vector
        def _(v, l=l, siz_tE=siz_tE):
            for k in range(8):
                comp, g = k // 4, k % 4
                v.wait_ge(tE.h, siz_tE + k + 1)
                pt = ptA if k % 2 == 0 else ptB
                ins = v.tensor_copy(sibf[0:NCH, comp * DIM + g * 128:
                                         comp * DIM + g * 128 + 128],
                                    pt[0:NCH, 0:128])
                eZ.inc(ins)

        @blk.sync
        def _(s, l=l):
            s.wait_ge(eZ.h, eZ.n)
            for comp in range(2):
                dmisc.dma(s.dma_start(
                    out=dataclasses.replace(si_dram.flatten(),
                                            offset=comp * NCH * DIM,
                                            ap=[[DIM, NCH], [1, DIM]]),
                    in_=sibf[0:NCH, comp * DIM:(comp + 1) * DIM]))
                s.wait_ge(dmisc.h, dmisc.n)
            for comp in range(2):
                dmisc.dma(s.dma_start(
                    out=y126[126 + comp:127 + comp, :],
                    in_=dataclasses.replace(si_dram.flatten(),
                                            offset=comp * NCH * DIM,
                                            ap=[[1, 1], [1, NCH * DIM]])))
                s.wait_ge(dmisc.h, dmisc.n)

        barrier()

        conv_eC = eC.n
        conv_tE = tE.n
        SLOT = 128

        @blk.scalar
        def _(sc, l=l):
            for g in range(NTG):
                par = g % 2
                if g >= 2:
                    sc.wait_ge(tE.h, conv_tE + (g - 1) * (TGRP // 4))
                toff = par * TGRP * 128
                for b in range(4):
                    k0 = 32 * b
                    rows = 32 if b < 3 else 30
                    s_b = (125 - k0) % MODP
                    srcv = dataclasses.replace(
                        big_dram.flatten(), offset=g * TGRP * BIGN + s_b,
                        ap=[[MODP - 1, rows], [BIGN, TGRP], [1, CC]])
                    base = tbuf[k0:k0 + rows, toff:toff + TGRP * 128]
                    dstv = dataclasses.replace(
                        base, ap=[[tbuf[:].ap[0][0], rows], [128, TGRP], [1, CC]])
                    dT.issue(sc, lambda dstv=dstv, srcv=srcv: sc.dma_start(
                        out=dstv, in_=srcv))
                ch0 = g * TGRP
                p0, gr = ch0 % 128, ch0 // 128
                for comp, srct in ((0, crre_bf), (1, crimn_bf)):
                    srcv = srct[p0:p0 + TGRP, gr * CC: gr * CC + CC]
                    base = tbuf[126 + comp:127 + comp, toff:toff + TGRP * 128]
                    dstv = dataclasses.replace(
                        base, ap=[[tbuf[:].ap[0][0], 1], [128, TGRP], [1, CC]])
                    dT.issue(sc, lambda dstv=dstv, srcv=srcv: sc.dma_start(
                        out=dstv, in_=srcv))

        @blk.tensor
        def _(t, l=l, conv_eC=conv_eC):
            t.wait_ge(dmisc.h, dmisc.n)
            t.wait_ge(eTH.h, eTH.n)
            for g in range(NTG):
                par = g % 2
                dT.wait_issue(t, l * NTG * 6 + g * 6 + 5)
                toff = par * TGRP * 128
                for ci in range(TGRP):
                    ch = g * TGRP + ci
                    bk = (g * TGRP + ci) // 4
                    if ci % 4 == 0 and bk >= 4:
                        t.wait_ge(eC.h, conv_eC + bk - 3)
                    ps = (pmA, pmB, pmE, pmF)[bk % 4]
                    rhs = dataclasses.replace(
                        y126[:], offset=ch, ap=[list(y126[:].ap[0]), [DIM, NCH]])
                    ins = t.matmul(ps[:, (ci % 4) * SLOT:(ci % 4) * SLOT + NCH],
                                   tbuf[:, toff + ci * 128: toff + (ci + 1) * 128],
                                   rhs, start=True, stop=True)
                    if ci % 4 == 3:
                        tE.inc(ins)

        @blk.vector
        def _(v, l=l, conv_tE=conv_tE, conv_eC=conv_eC):
            yw0 = l * 2 * NTG
            for bk in range(DIM // 4):
                v.wait_ge(tE.h, conv_tE + bk + 1)
                ps = (pmA, pmB, pmE, pmF)[bk % 4]
                g = (bk * 4) // TGRP
                cb = (bk * 4) % TGRP
                if cb == 0 and g >= 1:
                    dYW.wait_issue(v, yw0 + (g - 1) * 2 + 1)
                dst = dataclasses.replace(
                    yfst[:], offset=cb,
                    ap=[list(yfst[:].ap[0]), [1, 4], [TGRP, NCH]])
                src = dataclasses.replace(
                    ps[:], ap=[list(ps[:].ap[0]), [SLOT, 4], [1, NCH]])
                ins = v.tensor_copy(dst, src)
                eC.inc(ins)

        @blk.sync
        def _(s, l=l, conv_eC=conv_eC):
            for g in range(NTG):
                s.wait_ge(eC.h, conv_eC + (g + 1) * (TGRP // 4))
                for half in range(2):
                    c0 = half * (NCH // 2)
                    ncs = NCH // 2 if half == 0 else NCH - NCH // 2
                    srcv = dataclasses.replace(
                        yfst[0:126, :], offset=yfst[0:126, :].offset + c0 * TGRP,
                        ap=[[yfst[:].ap[0][0], 126], [TGRP, ncs], [1, TGRP]])
                    dstv = dataclasses.replace(
                        yf_dram.flatten(), offset=(c0 * CC) * DIM + g * TGRP,
                        ap=[[DIM, 126], [CC * DIM, ncs], [1, TGRP]])
                    dYW.issue(s, lambda dstv=dstv, srcv=srcv: s.dma_start(
                        out=dstv, in_=srcv))

        barrier()

        es23.close()
        esl2 = contextlib.ExitStack()
        w1T = esl2.enter_context(nc.sbuf_tensor(f"w1T{l}", [128, NDC * DFF], BF16))
        w2T = esl2.enter_context(nc.sbuf_tensor(f"w2T{l}", [128, NFC * DIM], BF16))
        y1b = esl2.enter_context(nc.sbuf_tensor(f"y1b{l}", [128, NFC * 512], BF16))
        hb2 = esl2.enter_context(nc.sbuf_tensor(f"hb2{l}", [128, 4 * DIM], F32))

        @blk.sync
        def _(s, l=l, w1T=w1T, w2T=w2T):
            s.wait_ge(dw_w1[l].h, 16)
            for dc in range(NDC):
                dPr.issue(s, lambda l=l, dc=dc, w1T=w1T: s.dma_start(
                    out=w1T[:, dc * DFF:(dc + 1) * DFF],
                    in_=w1bf_dram[l][:, dc * 128:(dc + 1) * 128], transpose=True))
            s.wait_ge(dw_w2[l].h, 16)
            for fc in range(NFC):
                dPr.issue(s, lambda l=l, fc=fc, w2T=w2T: s.dma_start(
                    out=w2T[:, fc * DIM:(fc + 1) * DIM],
                    in_=w2bf_dram[l][:, fc * 128:(fc + 1) * 128], transpose=True))

        @blk.tensor
        def _(t):
            dPr.wait_all(t)
            d_b2.wait_all(t)

        @blk.sync
        def _(s, l=l):
            dYW.wait_all(s)
            for ti in range(4):
                hb4[(l, ti)] = dHB.issue(s, lambda ti=ti: s.dma_start(
                    out=hbt[:, ti * DIM:(ti + 1) * DIM],
                    in_=h_dram[ti * 128:(ti + 1) * 128]))
                yf4[(l, ti)] = dYF.issue(s, lambda ti=ti: s.dma_start(
                    out=yfbt[:, ti * DIM:(ti + 1) * DIM],
                    in_=yf_dram[ti * 128:(ti + 1) * 128]))

        if l + 1 < n_layers:
            @blk.sync
            def _(s, l=l):
                dPr.issue(s, lambda l=l: s.dma_start(
                    out=lnlw_row[:], in_=ln1w_i[l + 1:l + 2]))
                dPr.issue(s, lambda l=l: s.dma_start(
                    out=lnlb_row[:], in_=ln1b_i[l + 1:l + 2]))

            @blk.tensor
            def _(t, l=l):
                dPr.wait_all(t)
                t.wait_ge(eC.h, eC.n)
                t.matmul(pmA[:, 0:DIM], onesf[0:1], lnlw_row[:],
                         start=True, stop=True)
                ins = t.matmul(pmB[:, 0:DIM], onesf[0:1], lnlb_row[:],
                               start=True, stop=True)
                tE.inc(ins)

            @blk.vector
            def _(v, l=l):
                v.wait_ge(tE.h, tE.n)
                v.tensor_copy(bc(BC_L1W), pmA[:, 0:DIM])
                vE.inc(v.tensor_copy(bc(BC_L1B), pmB[:, 0:DIM]))

        def emit_B4a(st_, l=None):
            sb_ = 16 + (st_ % 2) * 40

            @blk.vector
            def _(v, l=l, st_=st_, sb_=sb_):
                dYF.wait_issue(v, yf4[(l, st_ * 4 + 3)])
                dHB.wait_issue(v, hb4[(l, st_ * 4 + 3)])
                for q in range(4):
                    v.tensor_mul(yfbt[:, q * DIM:(q + 1) * DIM],
                                 yfbt[:, q * DIM:(q + 1) * DIM], bc(BC_SCL))
                v.drain()
                for q in range(4):
                    hv = hbt[:, q * DIM:(q + 1) * DIM]
                    v.tensor_add(hv, hv, yfbt[:, q * DIM:(q + 1) * DIM])
                v.drain()
                for q in range(4):
                    hv = hbt[:, q * DIM:(q + 1) * DIM]
                    v.reduce_sum(stat[:, sb_ + q:sb_ + q + 1], hv, axis=AX.X)
                    v.scalar_tensor_tensor(bscr[:, q * DIM:(q + 1) * DIM],
                                           hv, 1.0, hv, op0=ALU.mult, op1=ALU.mult,
                                           accum_out=stat[:, sb_ + 4 + q:sb_ + 5 + q])
                v.drain()
                v.tensor_scalar(stat[:, sb_ + 8:sb_ + 12], stat[:, sb_:sb_ + 4],
                                1.0 / DIM, None, op0=ALU.mult)
                v.drain()
                v.scalar_tensor_tensor(stat[:, sb_ + 12:sb_ + 16],
                                       stat[:, sb_ + 8:sb_ + 12], -1.0,
                                       stat[:, sb_ + 8:sb_ + 12],
                                       op0=ALU.mult, op1=ALU.mult)
                v.drain()
                vE.inc(v.scalar_tensor_tensor(stat[:, sb_ + 12:sb_ + 16],
                                              stat[:, sb_ + 4:sb_ + 8], 1.0 / DIM,
                                              stat[:, sb_ + 12:sb_ + 16],
                                              op0=ALU.mult, op1=ALU.add))

            @blk.scalar
            def _(sc, l=l, st_=st_, sb_=sb_):
                sc.wait_ge(vE.h, vE.n)
                sc.activation(stat[:, sb_ + 16:sb_ + 20], stat[:, sb_ + 12:sb_ + 16],
                              AF.Ln, bias=epscol[:, 0:1])
                sc.drain()
                ins = sc.activation(stat[:, sb_ + 20:sb_ + 24],
                                    stat[:, sb_ + 16:sb_ + 20], AF.Exp, scale=-0.5)
                sc.drain()
                aE.inc(ins)

            @blk.vector
            def _(v, l=l, st_=st_, sb_=sb_):
                v.wait_ge(aE.h, aE.n)
                if st_ >= 1:
                    dZB.wait_issue(v, zbA[l] + (st_ - 1) * 4 + 3)
                v.scalar_tensor_tensor(stat[:, sb_ + 24:sb_ + 28],
                                       stat[:, sb_ + 8:sb_ + 12], -1.0,
                                       stat[:, sb_ + 20:sb_ + 24],
                                       op0=ALU.mult, op1=ALU.mult)
                v.drain()
                for q in range(4):
                    v.tensor_scalar(bscr[:, q * DIM:(q + 1) * DIM],
                                    hbt[:, q * DIM:(q + 1) * DIM],
                                    stat[:, sb_ + 20 + q:sb_ + 21 + q],
                                    stat[:, sb_ + 24 + q:sb_ + 25 + q],
                                    op0=ALU.mult, op1=ALU.add)
                v.drain()
                for q in range(4):
                    v.tensor_mul(bscr[:, q * DIM:(q + 1) * DIM],
                                 bscr[:, q * DIM:(q + 1) * DIM], bc(BC_L2W))
                v.drain()
                for q in range(4):
                    ins = v.tensor_add(zbf[:, q * DIM:(q + 1) * DIM],
                                       bscr[:, q * DIM:(q + 1) * DIM], bc(BC_L2B))
                vE.inc(ins)

            @blk.sync
            def _(s, l=l, st_=st_):
                s.wait_ge(vE.h, vE.n)
                for q in range(4):
                    ti = st_ * 4 + q
                    dZB.issue(s, lambda ti=ti, q=q: s.dma_start(
                        out=zzbf_dram[ti * 128:(ti + 1) * 128],
                        in_=zbf[:, q * DIM:(q + 1) * DIM]))
                for q in range(4):
                    ti = st_ * 4 + q
                    dHW.issue(s, lambda ti=ti, q=q: s.dma_start(
                        out=h_dram[ti * 128:(ti + 1) * 128],
                        in_=hbt[:, q * DIM:(q + 1) * DIM]))
                for q in range(4):
                    ti = st_ * 4 + q
                    if ti + 4 < NBT:
                        dHW.wait_issue(s, NBT * l + ti)
                        hb4[(l, ti + 4)] = dHB.issue(s, lambda ti=ti: s.dma_start(
                            out=hbt[:, (ti % 4) * DIM:((ti % 4) + 1) * DIM],
                            in_=h_dram[(ti + 4) * 128:(ti + 5) * 128]))
                        yf4[(l, ti + 4)] = dYF.issue(s, lambda ti=ti: s.dma_start(
                            out=yfbt[:, (ti % 4) * DIM:((ti % 4) + 1) * DIM],
                            in_=yf_dram[(ti + 4) * 128:(ti + 5) * 128]))

        def emit_B4b(st_, l=None):
            zpar = st_ % 2

            @blk.sync
            def _(s, l=l, st_=st_, zpar=zpar):
                dZB.wait_issue(s, zbA[l] + st_ * 4 + 3)
                if st_ >= 2:
                    s.wait_ge(tE.h, f1_done[(l, st_ - 2)])
                zt_bases[(l, st_)] = dZT.idx
                for dc in range(NDC):
                    dZT.issue(s, lambda st_=st_, dc=dc, zpar=zpar: s.dma_start(
                        out=zT[:, zpar * 4 * DIM + dc * DIM:
                               zpar * 4 * DIM + (dc + 1) * DIM],
                        in_=zzbf_dram[st_ * 512:(st_ + 1) * 512,
                                      dc * 128:(dc + 1) * 128], transpose=True))
                if st_ >= 1:
                    dH2g.wait_issue(s, h2base[(l, st_ - 1)] + 3)
                if l + 1 < n_layers and st_ >= 1:
                    s.wait_ge(vE.h, b1v2[(l + 1, st_ - 1)])
                hb2base[(l, st_)] = dHB2.idx
                dHW.wait_issue(s, NBT * l + st_ * 4 + 3)
                for q in range(4):
                    ti = st_ * 4 + q
                    dHB2.issue(s, lambda ti=ti, q=q: s.dma_start(
                        out=hb2[:, q * DIM:(q + 1) * DIM],
                        in_=h_dram[ti * 128:(ti + 1) * 128]))

            f1_tE = tE.n
            f1_fE = fE.n

            @blk.tensor
            def _(t, l=l, st_=st_, f1_fE=f1_fE, zpar=zpar):
                dZT.wait_issue(t, zt_bases[(l, st_)] + NDC - 1)
                for fc in range(NFC):
                    if fc >= 2:
                        t.wait_ge(fE.h, f1_fE + fc - 1)
                    ps = pmC if fc % 2 == 0 else pmD
                    for dc in range(NDC):
                        ins = t.matmul(ps[:, 0:512],
                                       w1T[:, dc * DFF + fc * 128: dc * DFF + (fc + 1) * 128],
                                       zT[:, zpar * 4 * DIM + dc * DIM:
                                          zpar * 4 * DIM + (dc + 1) * DIM],
                                       start=(dc == 0), stop=(dc == NDC - 1))
                    tE.inc(ins)
                f1_done[(l, st_)] = tE.n

            @blk.scalar
            def _(sc, l=l, st_=st_, f1_tE=f1_tE):
                for fc in range(NFC):
                    sc.wait_ge(tE.h, f1_tE + fc + 1)
                    ps = pmC if fc % 2 == 0 else pmD
                    ins = sc.activation(y1b[:, fc * 512:(fc + 1) * 512], ps[:, 0:512],
                                        AF.Silu, bias=b1col[:, fc:fc + 1])
                    fE.inc(ins)

            f2_tE = tE.n
            f2_hE = hE.n

            @blk.tensor
            def _(t, l=l, st_=st_, f2_hE=f2_hE):
                t.wait_ge(fE.h, fE.n)
                for tc in range(4):
                    if tc >= 2:
                        t.wait_ge(hE.h, f2_hE + tc - 1)
                    ps = pmE if tc % 2 == 0 else pmF
                    for fc in range(NFC):
                        t.matmul(ps[:, 0:DIM],
                                 y1b[:, fc * 512 + tc * 128: fc * 512 + (tc + 1) * 128],
                                 w2T[:, fc * DIM:(fc + 1) * DIM],
                                 start=(fc == 0), stop=False)
                    ins = t.matmul(ps[:, 0:DIM], onesbf[:], b2_row[:],
                                   start=False, stop=True)
                    tE.inc(ins)

            @blk.vector
            def _(v, l=l, st_=st_, f2_tE=f2_tE):
                for tc in range(4):
                    v.wait_ge(tE.h, f2_tE + tc + 1)
                    dHB2.wait_issue(v, hb2base[(l, st_)] + tc)
                    ps = pmE if tc % 2 == 0 else pmF
                    ins = v.tensor_add(hb2[:, tc * DIM:(tc + 1) * DIM],
                                       hb2[:, tc * DIM:(tc + 1) * DIM], ps[:, 0:DIM])
                    hE.inc(ins)
            he_done[(l, st_)] = hE.n

            @blk.gpsimd
            def _(g_, l=l, st_=st_):
                h2base[(l, st_)] = dH2g.idx
                he0 = hE.n - 4
                for q in range(4):
                    ti = st_ * 4 + q
                    g_.wait_ge(hE.h, he0 + q + 1)
                    dH2g.issue(g_, lambda ti=ti, q=q: g_.dma_start(
                        out=h_dram[ti * 128:(ti + 1) * 128],
                        in_=hb2[:, q * DIM:(q + 1) * DIM]))

        def emit_B1p(k, l=l):
            lp = l + 1
            sb_ = 96 + (k % 2) * 28
            ycol = (k % 2) * 2 * 4 * DIM

            @blk.vector
            def _(v, k=k, lp=lp, sb_=sb_, ycol=ycol):
                v.wait_ge(hE.h, he_done[(l, k)])
                if k >= 2:
                    dYg.wait_issue(v, dyb1[(lp, (k - 2) * 4 + 3)])
                for q in range(4):
                    hv = hb2[:, q * DIM:(q + 1) * DIM]
                    v.reduce_sum(stat[:, sb_ + q:sb_ + q + 1], hv, axis=AX.X)
                    v.scalar_tensor_tensor(
                        y126[:, ycol + q * DIM: ycol + (q + 1) * DIM],
                        hv, 1.0, hv, op0=ALU.mult, op1=ALU.mult,
                        accum_out=stat[:, sb_ + 4 + q:sb_ + 5 + q])
                v.drain()
                v.tensor_scalar(stat[:, sb_ + 8:sb_ + 12], stat[:, sb_:sb_ + 4],
                                1.0 / DIM, None, op0=ALU.mult)
                v.drain()
                v.scalar_tensor_tensor(stat[:, sb_ + 12:sb_ + 16],
                                       stat[:, sb_ + 8:sb_ + 12], -1.0,
                                       stat[:, sb_ + 8:sb_ + 12],
                                       op0=ALU.mult, op1=ALU.mult)
                v.drain()
                vE.inc(v.scalar_tensor_tensor(stat[:, sb_ + 12:sb_ + 16],
                                              stat[:, sb_ + 4:sb_ + 8], 1.0 / DIM,
                                              stat[:, sb_ + 12:sb_ + 16],
                                              op0=ALU.mult, op1=ALU.add))

            @blk.scalar
            def _(sc, k=k, sb_=sb_):
                sc.wait_ge(vE.h, vE.n)
                sc.activation(stat[:, sb_ + 16:sb_ + 20], stat[:, sb_ + 12:sb_ + 16],
                              AF.Ln, bias=epscol[:, 0:1])
                sc.drain()
                ins = sc.activation(stat[:, sb_ + 20:sb_ + 24],
                                    stat[:, sb_ + 16:sb_ + 20], AF.Exp, scale=-0.5)
                sc.drain()
                aE.inc(ins)

            @blk.vector
            def _(v, k=k, lp=lp, sb_=sb_, ycol=ycol):
                v.wait_ge(aE.h, aE.n)
                v.scalar_tensor_tensor(stat[:, sb_ + 24:sb_ + 28],
                                       stat[:, sb_ + 8:sb_ + 12], -1.0,
                                       stat[:, sb_ + 20:sb_ + 24],
                                       op0=ALU.mult, op1=ALU.mult)
                v.drain()
                for q in range(4):
                    v.tensor_scalar(y126[:, ycol + q * DIM: ycol + (q + 1) * DIM],
                                    hb2[:, q * DIM:(q + 1) * DIM],
                                    stat[:, sb_ + 20 + q:sb_ + 21 + q],
                                    stat[:, sb_ + 24 + q:sb_ + 25 + q],
                                    op0=ALU.mult, op1=ALU.add)
                v.drain()
                for q in range(4):
                    v.tensor_mul(y126[:, ycol + q * DIM: ycol + (q + 1) * DIM],
                                 y126[:, ycol + q * DIM: ycol + (q + 1) * DIM],
                                 bc(BC_L1W))
                v.drain()
                for q in range(4):
                    ins = v.tensor_add(
                        y126[:, ycol + (4 + q) * DIM: ycol + (5 + q) * DIM],
                        y126[:, ycol + q * DIM: ycol + (q + 1) * DIM],
                        bc(BC_L1B))
                vE.inc(ins)
                b1v2[(lp, k)] = vE.n

            @blk.scalar
            def _(sc, k=k, lp=lp, ycol=ycol):
                sc.wait_ge(vE.h, b1v2[(lp, k)])
                for q in range(4):
                    ins = sc.activation(
                        y126[:, ycol + q * DIM: ycol + (q + 1) * DIM],
                        y126[:, ycol + (4 + q) * DIM: ycol + (5 + q) * DIM],
                        AF.Silu)
                    aE.inc(ins)

            @blk.gpsimd
            def _(g_, k=k, lp=lp, ycol=ycol):
                a0 = aE.n - 4
                for q in range(4):
                    ti = k * 4 + q
                    g_.wait_ge(aE.h, a0 + q + 1)
                    dyb1[(lp, ti)] = dYg.issue(g_, lambda ti=ti, q=q, ycol=ycol: g_.dma_start(
                        out=y_dram[ti * 128:(ti + 1) * 128],
                        in_=y126[:, ycol + q * DIM: ycol + (q + 1) * DIM]))

        zbA[l] = dZB.idx
        do_b1p = (l + 1 < n_layers)
        for k in range(NST):
            emit_B4a(k, l=l)
            if k >= 2:
                emit_B4b(k - 2, l=l)
                if do_b1p:
                    emit_B1p(k - 2, l=l)
        for k in (NST - 2, NST - 1):
            emit_B4b(k, l=l)
            if do_b1p:
                emit_B1p(k, l=l)

        @blk.sync
        def _(s, l=l):
            dH2.wait_all(s)
            dH2g.wait_all(s)
            dYg.wait_all(s)

        barrier()
        esl2.close()


    @blk.sync
    def _(s):
        dHW.wait_all(s)
        dY.wait_all(s)
    barrier()

    if "y" in dumps:
        @blk.gpsimd
        def _(g_):
            dw_base.dma(g_.dma_start(out=dump_t["y"], in_=y_dram))
            g_.wait_ge(dw_base.h, dw_base.n)
        barrier()
    if "yf" in dumps:
        @blk.sync
        def _(s):
            dmisc.dma(s.dma_start(out=dump_t["yf"], in_=yf_dram))
            s.wait_ge(dmisc.h, dmisc.n)
        barrier()
    if "h" in dumps:
        @blk.sync
        def _(s):
            dmisc.dma(s.dma_start(out=dump_t["h"], in_=h_dram))
            s.wait_ge(dmisc.h, dmisc.n)
        barrier()

    if n_layers < DEPTH or stop_after == "B":
        _emit_dummy_out(blk, nc, out_t, bcast_pool, dmisc, vE)
        esB.close()
        es.close()
        return nc

    esB.close()
    x1_dram = nc.dram_tensor("x1_dram", [NOUT, V], F32, kind="Internal").ap()
    houter_dram = nc.dram_tensor("houter_dram", [NOUT, DIM], F32, kind="Internal").ap()
    z2bf_dram = nc.dram_tensor("z2bf_dram", [NOUT, V], BF16, kind="Internal").ap()

    esC = contextlib.ExitStack()

    def sbufC(name, shape, dtype):
        return esC.enter_context(nc.sbuf_tensor(name, shape, dtype))

    houtb = sbufC("houtb", [128, 2 * DIM], F32)
    zcf = sbufC("zcf", [128, DIM], F32)
    zcbf = sbufC("zcbf", [128, DIM], BF16)
    zcT = sbufC("zcT", [128, DIM], BF16)
    woutT_sb = sbufC("woutT_sb", [128, NDC * V], BF16)
    boutbf = sbufC("boutbf", [1, V], BF16)
    abuf = sbufC("abuf", [128, V], F32)
    xob = sbufC("xob", [128, 2 * V], F32)
    z2bf = sbufC("z2bf", [128, V], BF16)

    dHC = SemRing(nc, "dHC", 2)
    dXO = SemRing(nc, "dXO", 2)
    dX1 = SemRing(nc, "dX1", 2)
    dZ2 = SemRing(nc, "dZ2", 2)
    NCT = NOUT // 128

    @blk.sync
    def _(s):
        def cdma(mk):
            mk().then_inc(dm.h, 16)
            nc.sync.reg_add(sreg_dmc, sreg_dmc, 16)
            s.wait_ge(dm.h, sreg_dmc)
        rvp = snap_pub()
        for blkrow in range(4):
            cdma(lambda blkrow=blkrow: s.dma_start(
                out=sh_h[bass.ds(rvp + blkrow * (NIN // 4), NIN // 4), :],
                in_=h_dram[blkrow * (NIN // 4):(blkrow + 1) * (NIN // 4)]))
        rvm = snap_mycol()
        cdma(lambda: s.dma_start(out=sh_flags[0:1, bass.ds(rvm, 1)],
                                 in_=myflag_sb[0:1, 14:15],
                                 allow_slow_non_contiguous=True))
        nc.sync.reg_mov(sreg_cond, 1)
        nc.sync.reg_load(sreg_tgt, flagv_sb[0:1, 15:16])
        with nc.sync.While(sreg_cond):
            rvpc = snap_pollc()
            cdma(lambda: s.dma_start(out=polltmp[0:1, 0:1],
                                     in_=sh_flags[0:1, bass.ds(rvpc, 1)],
                                     allow_slow_non_contiguous=True))
            nc.sync.reg_load(sreg_flagv, polltmp[0:1, 0:1])
            nc.sync.reg_alu(sreg_cond, sreg_flagv, sreg_tgt, op=ALU.not_equal)
        rvh = snap_hoff()
        cdma(lambda: s.dma_start(out=houter_dram,
                                 in_=sh_h[bass.ds(rvh, NOUT), :]))
        dPr.issue(s, lambda: s.dma_start(out=lnlw_row[:], in_=lnlw_i))
        dPr.issue(s, lambda: s.dma_start(out=lnlb_row[:], in_=lnlb_i))
        s.wait_ge(dw_wout.h, 16)
        for dc in range(NDC):
            dPr.issue(s, lambda dc=dc: s.dma_start(
                out=woutT_sb[:, dc * V:(dc + 1) * V],
                in_=woutbf_dram[:, dc * 128:(dc + 1) * 128], transpose=True))
        for ti in range(2):
            dHC.issue(s, lambda ti=ti: s.dma_start(
                out=houtb[:, (ti % 2) * DIM:((ti % 2) + 1) * DIM],
                in_=houter_dram[ti * 128:(ti + 1) * 128]))
            dXO.issue(s, lambda ti=ti: s.dma_start(
                out=xob[:, (ti % 2) * V:((ti % 2) + 1) * V],
                in_=x_outer[ti * 128:(ti + 1) * 128]))

    @blk.gpsimd
    def _(g_):
        d_b2.issue(g_, lambda: g_.dma_start(out=boutbf[:], in_=bout_i))

    @blk.tensor
    def _(t):
        dPr.wait_all(t)
        d_b2.wait_all(t)
        t.matmul(pmA[:, 0:DIM], onesf[0:1], lnlw_row[:], start=True, stop=True)
        ins = t.matmul(pmB[:, 0:DIM], onesf[0:1], lnlb_row[:], start=True, stop=True)
        tE.inc(ins)

    @blk.vector
    def _(v):
        v.wait_ge(tE.h, tE.n)
        v.tensor_copy(bc(BC_L1W), pmA[:, 0:DIM])
        vE.inc(v.tensor_copy(bc(BC_L1B), pmB[:, 0:DIM]))

    barrier()

    for ti in range(NCT):
        par = ti % 2
        sb_ = 16 + par * 40

        @blk.vector
        def _(v, ti=ti, par=par, sb_=sb_):
            dHC.wait_issue(v, ti)
            hv = houtb[:, par * DIM:(par + 1) * DIM]
            v.reduce_sum(stat[:, sb_:sb_ + 1], hv, axis=AX.X)
            v.scalar_tensor_tensor(zcf[:], hv, 1.0, hv, op0=ALU.mult,
                                   op1=ALU.mult, accum_out=stat[:, sb_ + 1:sb_ + 2])
            v.drain()
            v.tensor_scalar(stat[:, sb_ + 2:sb_ + 3], stat[:, sb_:sb_ + 1],
                            1.0 / DIM, None, op0=ALU.mult)
            v.drain()
            v.scalar_tensor_tensor(stat[:, sb_ + 3:sb_ + 4],
                                   stat[:, sb_ + 2:sb_ + 3], -1.0,
                                   stat[:, sb_ + 2:sb_ + 3],
                                   op0=ALU.mult, op1=ALU.mult)
            v.drain()
            vE.inc(v.scalar_tensor_tensor(stat[:, sb_ + 3:sb_ + 4],
                                          stat[:, sb_ + 1:sb_ + 2], 1.0 / DIM,
                                          stat[:, sb_ + 3:sb_ + 4],
                                          op0=ALU.mult, op1=ALU.add))

        @blk.scalar
        def _(sc, ti=ti, sb_=sb_):
            sc.wait_ge(vE.h, vE.n)
            sc.activation(stat[:, sb_ + 4:sb_ + 5], stat[:, sb_ + 3:sb_ + 4],
                          AF.Ln, bias=epscol[:, 0:1])
            sc.drain()
            ins = sc.activation(stat[:, sb_ + 5:sb_ + 6], stat[:, sb_ + 4:sb_ + 5],
                                AF.Exp, scale=-0.5)
            sc.drain()
            aE.inc(ins)

        @blk.vector
        def _(v, ti=ti, par=par, sb_=sb_):
            v.wait_ge(aE.h, aE.n)
            hv = houtb[:, par * DIM:(par + 1) * DIM]
            nm = stat[:, sb_ + 6:sb_ + 7]
            v.scalar_tensor_tensor(nm, stat[:, sb_ + 2:sb_ + 3], -1.0,
                                   stat[:, sb_ + 5:sb_ + 6],
                                   op0=ALU.mult, op1=ALU.mult)
            v.drain()
            v.tensor_scalar(zcf[:], hv, stat[:, sb_ + 5:sb_ + 6], nm,
                            op0=ALU.mult, op1=ALU.add)
            v.drain()
            v.tensor_mul(zcf[:], zcf[:], bc(BC_L1W))
            v.drain()
            v.tensor_add(zcf[:], zcf[:], bc(BC_L1B))
            v.drain()
            vE.inc(v.tensor_copy(zcbf[:], zcf[:]))

        ct_eZ = eZ.n
        ct_tE = tE.n

        @blk.tensor
        def _(t, ti=ti, ct_eZ=ct_eZ):
            t.wait_ge(vE.h, vE.n)
            for k in range(4):
                if k >= 2:
                    t.wait_ge(eZ.h, ct_eZ + k - 1)
                pt = ptA if k % 2 == 0 else ptB
                ins = t.transpose(pt[:, 0:128], zcbf[:, k * 128:(k + 1) * 128],
                                  ident[:])
                tE.inc(ins)

        @blk.scalar
        def _(sc, ti=ti, ct_tE=ct_tE):
            for k in range(4):
                sc.wait_ge(tE.h, ct_tE + k + 1)
                pt = ptA if k % 2 == 0 else ptB
                ins = sc.copy(zcT[:, k * 128:(k + 1) * 128], pt[:, 0:128])
                eZ.inc(ins)

        co_tE = tE.n
        co_aE = aE.n

        @blk.tensor
        def _(t, ti=ti, co_aE=co_aE):
            t.wait_ge(eZ.h, eZ.n)
            for og in range(8):
                if og >= 2:
                    t.wait_ge(aE.h, co_aE + og - 1)
                ps = pmC if og % 2 == 0 else pmD
                for dc in range(NDC):
                    t.matmul(ps[:, 0:512],
                             zcT[:, dc * 128:(dc + 1) * 128],
                             woutT_sb[:, dc * V + og * 512: dc * V + og * 512 + 512],
                             start=(dc == 0), stop=False)
                ins = t.matmul(ps[:, 0:512], onesbf[:],
                               boutbf[0:1, og * 512:(og + 1) * 512],
                               start=False, stop=True)
                tE.inc(ins)

        @blk.scalar
        def _(sc, ti=ti, co_tE=co_tE):
            if ti >= 1:
                dX1.wait_issue(sc, ti - 1)
            for og in range(8):
                sc.wait_ge(tE.h, co_tE + og + 1)
                ps = pmC if og % 2 == 0 else pmD
                ins = sc.copy(abuf[:, og * 512:(og + 1) * 512], ps[:, 0:512])
                aE.inc(ins)

        @blk.vector
        def _(v, ti=ti, par=par, sb_=sb_):
            v.wait_ge(aE.h, aE.n)
            dXO.wait_issue(v, ti)
            if ti >= 1:
                dZ2.wait_issue(v, ti - 1)
            v.reduce_sum(stat[:, sb_ + 7:sb_ + 8], abuf[:], axis=AX.X)
            v.scalar_tensor_tensor(z2bf[:], abuf[:], 1.0, abuf[:], op0=ALU.mult,
                                   op1=ALU.mult, accum_out=stat[:, sb_ + 8:sb_ + 9])
            v.drain()
            v.tensor_scalar(stat[:, sb_ + 9:sb_ + 10], stat[:, sb_ + 7:sb_ + 8],
                            1.0 / V, None, op0=ALU.mult)
            v.drain()
            v.scalar_tensor_tensor(stat[:, sb_ + 10:sb_ + 11],
                                   stat[:, sb_ + 9:sb_ + 10], -1.0,
                                   stat[:, sb_ + 9:sb_ + 10],
                                   op0=ALU.mult, op1=ALU.mult)
            v.drain()
            vE.inc(v.scalar_tensor_tensor(stat[:, sb_ + 10:sb_ + 11],
                                          stat[:, sb_ + 8:sb_ + 9], 1.0 / V,
                                          stat[:, sb_ + 10:sb_ + 11],
                                          op0=ALU.mult, op1=ALU.add))

        @blk.scalar
        def _(sc, ti=ti, sb_=sb_):
            sc.wait_ge(vE.h, vE.n)
            sc.activation(stat[:, sb_ + 11:sb_ + 12], stat[:, sb_ + 10:sb_ + 11],
                          AF.Ln, bias=epscol[:, 0:1])
            sc.drain()
            ins = sc.activation(stat[:, sb_ + 12:sb_ + 13], stat[:, sb_ + 11:sb_ + 12],
                                AF.Exp, scale=-0.5)
            sc.drain()
            aE.inc(ins)

        @blk.vector
        def _(v, ti=ti, par=par, sb_=sb_):
            v.wait_ge(aE.h, aE.n)
            nm = stat[:, sb_ + 13:sb_ + 14]
            v.scalar_tensor_tensor(nm, stat[:, sb_ + 9:sb_ + 10], -1.0,
                                   stat[:, sb_ + 12:sb_ + 13],
                                   op0=ALU.mult, op1=ALU.mult)
            v.drain()
            v.tensor_scalar(abuf[:], abuf[:], stat[:, sb_ + 12:sb_ + 13], nm,
                            op0=ALU.mult, op1=ALU.add)
            v.drain()
            v.tensor_add(abuf[:], abuf[:], xob[:, par * V:(par + 1) * V])
            v.drain()
            v.reduce_sum(stat[:, sb_ + 14:sb_ + 15], abuf[:], axis=AX.X)
            v.scalar_tensor_tensor(z2bf[:], abuf[:], 1.0, abuf[:], op0=ALU.mult,
                                   op1=ALU.mult, accum_out=stat[:, sb_ + 15:sb_ + 16])
            v.drain()
            v.tensor_scalar(stat[:, sb_ + 16:sb_ + 17], stat[:, sb_ + 14:sb_ + 15],
                            1.0 / V, None, op0=ALU.mult)
            v.drain()
            v.scalar_tensor_tensor(stat[:, sb_ + 17:sb_ + 18],
                                   stat[:, sb_ + 16:sb_ + 17], -1.0,
                                   stat[:, sb_ + 16:sb_ + 17],
                                   op0=ALU.mult, op1=ALU.mult)
            v.drain()
            vE.inc(v.scalar_tensor_tensor(stat[:, sb_ + 17:sb_ + 18],
                                          stat[:, sb_ + 15:sb_ + 16], 1.0 / V,
                                          stat[:, sb_ + 17:sb_ + 18],
                                          op0=ALU.mult, op1=ALU.add))

        @blk.scalar
        def _(sc, ti=ti, sb_=sb_):
            sc.wait_ge(vE.h, vE.n)
            sc.activation(stat[:, sb_ + 18:sb_ + 19], stat[:, sb_ + 17:sb_ + 18],
                          AF.Ln, bias=epscol[:, 0:1])
            sc.drain()
            ins = sc.activation(stat[:, sb_ + 19:sb_ + 20], stat[:, sb_ + 18:sb_ + 19],
                                AF.Exp, scale=-0.5)
            sc.drain()
            aE.inc(ins)

        @blk.vector
        def _(v, ti=ti, sb_=sb_):
            v.wait_ge(aE.h, aE.n)
            nm = stat[:, sb_ + 20:sb_ + 21]
            v.scalar_tensor_tensor(nm, stat[:, sb_ + 16:sb_ + 17], -1.0,
                                   stat[:, sb_ + 19:sb_ + 20],
                                   op0=ALU.mult, op1=ALU.mult)
            v.drain()
            v.tensor_scalar(z2bf[:], abuf[:], stat[:, sb_ + 19:sb_ + 20], nm,
                            op0=ALU.mult, op1=ALU.add)
            v.drain()
            vE.inc(v.tensor_copy(stat[:, sb_:sb_ + 1], stat[:, sb_:sb_ + 1]))

        @blk.sync
        def _(s, ti=ti):
            s.wait_ge(vE.h, vE.n)
            dX1.issue(s, lambda ti=ti: s.dma_start(
                out=x1_dram[ti * 128:(ti + 1) * 128], in_=abuf[:]))
            dZ2.issue(s, lambda ti=ti: s.dma_start(
                out=z2bf_dram[ti * 128:(ti + 1) * 128], in_=z2bf[:]))
            if ti + 2 < NCT:
                dHC.issue(s, lambda ti=ti: s.dma_start(
                    out=houtb[:, (ti % 2) * DIM:((ti % 2) + 1) * DIM],
                    in_=houter_dram[(ti + 2) * 128:(ti + 3) * 128]))
                dXO.issue(s, lambda ti=ti: s.dma_start(
                    out=xob[:, (ti % 2) * V:((ti % 2) + 1) * V],
                    in_=x_outer[(ti + 2) * 128:(ti + 3) * 128]))

    @blk.sync
    def _(s):
        dX1.wait_all(s)
        dZ2.wait_all(s)
    barrier()

    if "x1" in dumps:
        @blk.sync
        def _(s):
            dmisc.dma(s.dma_start(out=dump_t["x1"], in_=x1_dram))
            s.wait_ge(dmisc.h, dmisc.n)
        barrier()
    if "z2" in dumps:
        @blk.gpsimd
        def _(g_):
            dw_base.dma(g_.dma_start(out=dump_t["z2"], in_=z2bf_dram))
            g_.wait_ge(dw_base.h, dw_base.n)
        barrier()

    if stop_after == "C":
        _emit_dummy_out(blk, nc, out_t, bcast_pool, dmisc, vE)
        esC.close()
        es.close()
        return nc

    esC.close()
    ffn_dram = nc.dram_tensor("ffn_dram", [NOUT, V], BF16, kind="Internal").ap()

    esD1 = contextlib.ExitStack()

    def sbufD(name, shape, dtype):
        return esD1.enter_context(nc.sbuf_tensor(name, shape, dtype))

    HT = NOUT // 2
    NTT = HT // 128
    zdT = sbufD("zdT", [128, 32 * HT], BF16)
    gla = sbufD("gla", [128, 32 * 256], BF16)
    wsl = sbufD("wsl", [128, 2 * 32 * 512], BF16)
    lbTsb = sbufD("lbTsb", [128, 2 * V], BF16)
    bsb = sbufD("bsb", [128, 2 * 128], F32)
    bsbbf = sbufD("bsbbf", [128, 2 * 128], BF16)
    hAgT = sbufD("hAgT", [128, NTT * 256], BF16)
    hAgbf = sbufD("hAgbf", [128, 256], BF16)
    gha = sbufD("gha", [128, 256], F32)
    msk = sbufD("msk", [128, 2 * NE], F32)
    gsc = sbufD("gsc", [128, 8], F32)
    gt = sbufD("gt", [128, NE], F32)
    fevac = sbufD("fevac", [128, 2 * 512], BF16)

    @blk.sync
    def _(s):
        s.wait_ge(dw_gate.h, 16)
        s.wait_ge(dw_lora.h, 16)
        for dc in range(32):
            dPr.issue(s, lambda dc=dc: s.dma_start(
                out=gla[:, dc * 256: dc * 256 + 32],
                in_=gatebf_dram[:, dc * 128:(dc + 1) * 128], transpose=True))
            dPr.issue(s, lambda dc=dc: s.dma_start(
                out=gla[:, dc * 256 + 32:(dc + 1) * 256],
                in_=lorabf_dram[:, dc * 128:(dc + 1) * 128], transpose=True))

    lb_tE = tE.n
    lb_vE = vE.n
    lb_eZ = eZ.n
    lb_dXO = dXO.idx

    @blk.sync
    def _(s):
        for k in range(64):
            oc, ch = k // 2, k % 2
            e0 = ch * 16
            ne_ = 16 if ch == 0 else NE - 16
            srcb = dataclasses.replace(
                loraB_i.flatten(), offset=e0 * V * RK + oc * 128 * RK,
                ap=[[RK, 128], [V * RK, ne_], [1, RK]])
            dXO.issue(s, lambda k=k, srcb=srcb, ne_=ne_: s.dma_start(
                out=bsb[:, (k % 2) * 128: (k % 2) * 128 + ne_ * RK], in_=srcb))

    @blk.vector
    def _(v):
        v.memset(bsb[:], 0.0)
        v.drain()
        for k in range(64):
            dXO.wait_issue(v, lb_dXO + k)
            if k >= 2:
                v.wait_ge(tE.h, lb_tE + k - 1)
            ins = v.tensor_copy(bsbbf[:, (k % 2) * 128:(k % 2 + 1) * 128],
                                bsb[:, (k % 2) * 128:(k % 2 + 1) * 128])
            vE.inc(ins)

    @blk.tensor
    def _(t):
        for k in range(64):
            t.wait_ge(vE.h, lb_vE + k + 1)
            if k >= 2:
                t.wait_ge(eZ.h, lb_eZ + k - 1)
            pt = ptA if k % 2 == 0 else ptB
            ins = t.transpose(pt[:, 0:128],
                              bsbbf[:, (k % 2) * 128:(k % 2 + 1) * 128],
                              ident[:])
            tE.inc(ins)

    @blk.scalar
    def _(sc):
        for k in range(64):
            oc, ch = k // 2, k % 2
            sc.wait_ge(tE.h, lb_tE + k + 1)
            pt = ptA if k % 2 == 0 else ptB
            if ch == 0:
                ins = sc.copy(lbTsb[:, oc * 128:(oc + 1) * 128], pt[:, 0:128])
            else:
                ins = sc.copy(lbTsb[0:96, V + oc * 128: V + (oc + 1) * 128],
                              pt[0:96, 0:128])
            sc.drain()
            eZ.inc(ins)

    @blk.vector
    def _(v):
        vE.inc(v.memset(hAgbf[:], 0.0))

    barrier()

    dYW_D0 = dYW.idx
    wsl_base = {}
    lbT_base = {}

    for half in range(2):
        zd_base = dCk.idx

        @blk.sync
        def _(s, half=half):
            for dc in range(32):
                dCk.issue(s, lambda dc=dc, half=half: s.dma_start(
                    out=zdT[:, dc * HT:(dc + 1) * HT],
                    in_=z2bf_dram[half * HT: (half + 1) * HT,
                                  dc * 128:(dc + 1) * 128], transpose=True))
            if half == 0:
                s.wait_ge(dw_base.h, 16)
                wsl_base[(0, 0)] = dT.idx
                for dc in range(32):
                    dT.issue(s, lambda dc=dc: s.dma_start(
                        out=wsl[:, dc * 512:(dc + 1) * 512],
                        in_=basebf_dram[0:512, dc * 128:(dc + 1) * 128],
                        transpose=True))
            else:
                s.wait_ge(tE.h, tE.n)

        mm_done = {}
        tr_done = {}
        v1_done = {}
        v2_done = {}
        ez_done = {-1: eZ.n, -2: eZ.n}

        def emit_transpose(t, tt):
            t.wait_ge(vE.h, v2_done[tt])
            if tt >= 2:
                t.wait_ge(eZ.h, ez_done[tt - 2])
            for k in range(2):
                pt = ptA if k == 0 else ptB
                ins = t.transpose(pt[:, 0:128],
                                  hAgbf[:, k * 128:(k + 1) * 128], ident[:])
                tE.inc(ins)
            tr_done[tt] = tE.n

        def emit_evac(sc, tt):
            sc.wait_ge(tE.h, tr_done[tt] - 1)
            sc.copy(hAgT[:, tt * 256: tt * 256 + 128], ptA[:, 0:128])
            sc.wait_ge(tE.h, tr_done[tt])
            ins = sc.copy(hAgT[0:96, tt * 256 + 128: tt * 256 + 256],
                          ptB[0:96, 0:128])
            sc.drain()
            eZ.inc(ins)
            ez_done[tt] = eZ.n

        for tt in range(NTT):
            @blk.tensor
            def _(t, half=half, tt=tt, zd_base=zd_base):
                if tt == 0:
                    if half == 0:
                        dPr.wait_all(t)
                    dCk.wait_issue(t, zd_base + 31)
                if tt >= 2:
                    t.wait_ge(vE.h, v1_done[tt - 2])
                ps = pmA if tt % 2 == 0 else pmB
                for dc in range(32):
                    ins = t.matmul(ps[:, 0:256],
                                   zdT[:, dc * HT + tt * 128: dc * HT + (tt + 1) * 128],
                                   gla[:, dc * 256:(dc + 1) * 256],
                                   start=(dc == 0), stop=(dc == 31))
                tE.inc(ins)
                mm_done[tt] = tE.n
                if tt >= 1:
                    emit_transpose(t, tt - 1)

            @blk.vector
            def _(v, half=half, tt=tt):
                v.wait_ge(tE.h, mm_done[tt])
                ps = pmA if tt % 2 == 0 else pmB
                v.tensor_copy(gha[:], ps[:, 0:256])
                v.drain()
                v.memset(gha[:, NE:32], -1e30)
                v.drain()
                v.reduce_max(gsc[:, 0:1], gha[:, 0:32], axis=AX.X)
                v.drain()
                v.tensor_scalar(msk[:, 0:NE], gha[:, 0:NE], gsc[:, 0:1], None,
                                op0=ALU.is_ge)
                v.drain()
                v.scalar_tensor_tensor(msk[:, NE:2 * NE], msk[:, 0:NE], -1e30,
                                       gha[:, 0:NE], op0=ALU.mult, op1=ALU.add)
                v.drain()
                v.reduce_max(gsc[:, 1:2], msk[:, NE:2 * NE], axis=AX.X)
                v.drain()
                v.tensor_scalar(msk[:, NE:2 * NE], msk[:, NE:2 * NE],
                                gsc[:, 1:2], None, op0=ALU.is_ge)
                v.tensor_sub(gsc[:, 2:3], gsc[:, 0:1], gsc[:, 1:2])
                v.tensor_sub(gsc[:, 3:4], gsc[:, 1:2], gsc[:, 0:1])
                v.drain()
                vE.inc(v.tensor_copy(gsc[:, 6:7], gsc[:, 0:1]))
            v1_done[tt] = vE.n

            @blk.scalar
            def _(sc, half=half, tt=tt):
                sc.wait_ge(vE.h, v1_done[tt])
                sc.activation(gsc[:, 4:5], gsc[:, 2:3], AF.Sigmoid)
                ins = sc.activation(gsc[:, 5:6], gsc[:, 3:4], AF.Sigmoid)
                sc.drain()
                aE.inc(ins)
                if tt >= 1:
                    emit_evac(sc, tt - 1)

            @blk.vector
            def _(v, half=half, tt=tt):
                v.wait_ge(aE.h, aE.n)
                if tt >= 1:
                    v.wait_ge(tE.h, tr_done[tt - 1])
                v.tensor_scalar(gt[:], msk[:, 0:NE], gsc[:, 4:5], None,
                                op0=ALU.mult)
                v.tensor_scalar(msk[:, NE:2 * NE], msk[:, NE:2 * NE],
                                gsc[:, 5:6], None, op0=ALU.mult)
                v.drain()
                v.tensor_add(gt[:], gt[:], msk[:, NE:2 * NE])
                v.drain()
                v.tensor_scalar(gt[:], gt[:], SCALING, None, op0=ALU.mult)
                v.drain()
                gv = dataclasses.replace(
                    gt[:], ap=[list(gt[:].ap[0]), [1, NE], [0, RK]])
                v.tensor_mul(hAgbf[:, 0:ER], gha[:, 32:256], gv)
                v.drain()
                vE.inc(v.tensor_copy(gsc[:, 7:8], gsc[:, 0:1]))
            v2_done[tt] = vE.n

        @blk.tensor
        def _(t, half=half):
            emit_transpose(t, NTT - 1)

        @blk.scalar
        def _(sc, half=half):
            emit_evac(sc, NTT - 1)

        for og in range(8):
            par = og % 2
            o_tE = tE.n
            o_eC = eC.n

            @blk.tensor
            def _(t, half=half, og=og, o_eC=o_eC,
                  wb=None, lb=None):
                wb = wsl_base[(half, og)]
                if og == 0:
                    t.wait_ge(eZ.h, eZ.n)
                dT.wait_issue(t, wb + 31)
                for tt in range(NTT):
                    if tt >= 2:
                        t.wait_ge(eC.h, o_eC + tt - 1)
                    ps = pmC if tt % 2 == 0 else pmD
                    for dc in range(32):
                        t.matmul(ps[:, 0:512],
                                 zdT[:, dc * HT + tt * 128: dc * HT + (tt + 1) * 128],
                                 wsl[:, par * 32 * 512 + dc * 512:
                                     par * 32 * 512 + (dc + 1) * 512],
                                 start=(dc == 0), stop=False)
                    t.matmul(ps[:, 0:512], hAgT[:, tt * 256: tt * 256 + 128],
                             lbTsb[:, og * 512:(og + 1) * 512],
                             start=False, stop=False)
                    ins = t.matmul(ps[:, 0:512],
                                   hAgT[0:96, tt * 256 + 128: tt * 256 + 256],
                                   lbTsb[0:96, V + og * 512: V + (og + 1) * 512],
                                   start=False, stop=True)
                    tE.inc(ins)

            @blk.scalar
            def _(sc, half=half, og=og, o_tE=o_tE):
                for tt in range(NTT):
                    sc.wait_ge(tE.h, o_tE + tt + 1)
                    gidx = (half * 8 + og) * NTT + tt
                    if gidx >= 2:
                        dYW.wait_issue(sc, dYW_D0 + gidx - 2)
                    ps = pmC if tt % 2 == 0 else pmD
                    ins = sc.copy(fevac[:, (tt % 2) * 512:((tt % 2) + 1) * 512],
                                  ps[:, 0:512])
                    sc.drain()
                    eC.inc(ins)

            @blk.sync
            def _(s, half=half, og=og, o_eC=o_eC, o_tE=o_tE):
                for tt in range(NTT):
                    s.wait_ge(eC.h, o_eC + tt + 1)
                    dYW.issue(s, lambda tt=tt, half=half, og=og: s.dma_start(
                        out=ffn_dram[half * HT + tt * 128: half * HT + (tt + 1) * 128,
                                     og * 512:(og + 1) * 512],
                        in_=fevac[:, (tt % 2) * 512:((tt % 2) + 1) * 512]))
                nx = half * 8 + og + 1
                if nx < 16:
                    og2 = nx % 8
                    h2 = nx // 8
                    s.wait_ge(tE.h, o_tE)
                    wsl_base[(h2, og2)] = dT.idx
                    for dc in range(32):
                        dT.issue(s, lambda dc=dc, og2=og2: s.dma_start(
                            out=wsl[:, ((og + 1) % 2) * 32 * 512 + dc * 512:
                                    ((og + 1) % 2) * 32 * 512 + (dc + 1) * 512],
                            in_=basebf_dram[og2 * 512:(og2 + 1) * 512,
                                            dc * 128:(dc + 1) * 128],
                            transpose=True))


    @blk.sync
    def _(s):
        dYW.wait_all(s)
    barrier()
    esD1.close()

    if "ffn" in dumps:
        @blk.gpsimd
        def _(g_):
            dw_base.dma(g_.dma_start(out=dump_t["ffn"], in_=ffn_dram))
            g_.wait_ge(dw_base.h, dw_base.n)
        barrier()

    esD2 = contextlib.ExitStack()
    fbufbf = esD2.enter_context(nc.sbuf_tensor("fbufbf", [128, 2 * V], BF16))
    x1t = esD2.enter_context(nc.sbuf_tensor("x1t", [128, 2 * V], F32))
    obuf = esD2.enter_context(nc.sbuf_tensor("obuf", [128, V], F32))

    f_dHB = dHB.idx
    f_dXO = dXO.idx
    f_dY = dY.idx

    @blk.sync
    def _(s):
        for fi in range(2):
            dHB.issue(s, lambda fi=fi: s.dma_start(
                out=fbufbf[:, (fi % 2) * V:((fi % 2) + 1) * V],
                in_=ffn_dram[fi * 128:(fi + 1) * 128]))
            dXO.issue(s, lambda fi=fi: s.dma_start(
                out=x1t[:, (fi % 2) * V:((fi % 2) + 1) * V],
                in_=x1_dram[fi * 128:(fi + 1) * 128]))

    for fi in range(NCT):
        par = fi % 2
        sb_ = 16 + par * 40

        @blk.vector
        def _(v, fi=fi, par=par, sb_=sb_):
            dHB.wait_issue(v, f_dHB + fi)
            fv = fbufbf[:, par * V:(par + 1) * V]
            if fi >= 1:
                dY.wait_issue(v, f_dY + fi - 1)
            v.reduce_sum(stat[:, sb_:sb_ + 1], fv, axis=AX.X)
            v.scalar_tensor_tensor(obuf[:], fv, 1.0, fv, op0=ALU.mult,
                                   op1=ALU.mult, accum_out=stat[:, sb_ + 1:sb_ + 2])
            v.drain()
            v.tensor_scalar(stat[:, sb_ + 2:sb_ + 3], stat[:, sb_:sb_ + 1],
                            1.0 / V, None, op0=ALU.mult)
            v.drain()
            v.scalar_tensor_tensor(stat[:, sb_ + 3:sb_ + 4],
                                   stat[:, sb_ + 2:sb_ + 3], -1.0,
                                   stat[:, sb_ + 2:sb_ + 3],
                                   op0=ALU.mult, op1=ALU.mult)
            v.drain()
            vE.inc(v.scalar_tensor_tensor(stat[:, sb_ + 3:sb_ + 4],
                                          stat[:, sb_ + 1:sb_ + 2], 1.0 / V,
                                          stat[:, sb_ + 3:sb_ + 4],
                                          op0=ALU.mult, op1=ALU.add))

        @blk.scalar
        def _(sc, fi=fi, sb_=sb_):
            sc.wait_ge(vE.h, vE.n)
            sc.activation(stat[:, sb_ + 4:sb_ + 5], stat[:, sb_ + 3:sb_ + 4],
                          AF.Ln, bias=epscol[:, 0:1])
            sc.drain()
            ins = sc.activation(stat[:, sb_ + 5:sb_ + 6], stat[:, sb_ + 4:sb_ + 5],
                                AF.Exp, scale=-0.5)
            sc.drain()
            aE.inc(ins)

        @blk.vector
        def _(v, fi=fi, par=par, sb_=sb_):
            v.wait_ge(aE.h, aE.n)
            dXO.wait_issue(v, f_dXO + fi)
            fv = fbufbf[:, par * V:(par + 1) * V]
            nm = stat[:, sb_ + 6:sb_ + 7]
            v.scalar_tensor_tensor(nm, stat[:, sb_ + 2:sb_ + 3], -1.0,
                                   stat[:, sb_ + 5:sb_ + 6],
                                   op0=ALU.mult, op1=ALU.mult)
            v.drain()
            v.tensor_scalar(obuf[:], fv, stat[:, sb_ + 5:sb_ + 6], nm,
                            op0=ALU.mult, op1=ALU.add)
            v.drain()
            v.tensor_add(obuf[:], obuf[:], x1t[:, par * V:(par + 1) * V])
            v.drain()
            vE.inc(v.tensor_copy(stat[:, sb_:sb_ + 1], stat[:, sb_:sb_ + 1]))

        @blk.sync
        def _(s, fi=fi):
            s.wait_ge(vE.h, vE.n)
            dY.issue(s, lambda fi=fi: s.dma_start(
                out=out_t[fi * 128:(fi + 1) * 128], in_=obuf[:]))
            if fi + 2 < NCT:
                dHB.issue(s, lambda fi=fi: s.dma_start(
                    out=fbufbf[:, (fi % 2) * V:((fi % 2) + 1) * V],
                    in_=ffn_dram[(fi + 2) * 128:(fi + 3) * 128]))
                dXO.issue(s, lambda fi=fi: s.dma_start(
                    out=x1t[:, (fi % 2) * V:((fi % 2) + 1) * V],
                    in_=x1_dram[(fi + 2) * 128:(fi + 3) * 128]))

    @blk.sync
    def _(s):
        dY.wait_all(s)
    barrier()
    esD2.close()
    es.close()
    return nc


def _emit_dummy_out(blk, nc, out_t, scratch, dm, vE):
    @blk.vector
    def _(v):
        v.memset(scratch[:, 0:DIM], 0.0)
        v.drain()
        vE.inc(v.tensor_copy(scratch[:, 0:1], scratch[:, 0:1]))

    @blk.sync
    def _(s):
        s.wait_ge(vE.h, vE.n)
        for i in range(NOUT // 128):
            for j in range(V // DIM):
                dm.dma(s.dma_start(out=out_t[i * 128:(i + 1) * 128,
                                             j * DIM:(j + 1) * DIM],
                                   in_=scratch[:, 0:DIM]))
                s.wait_ge(dm.h, dm.n)


def _prep_inputs(inputs, nonce=None):
    x = np.ascontiguousarray(np.asarray(inputs["L0_attn_x"])[0])
    gate_pad = np.zeros((32, V), np.float32)
    gate_pad[:NE] = np.asarray(inputs["L0_ffn_gate_W"])
    loraA_flat = np.ascontiguousarray(
        np.asarray(inputs["L0_ffn_lora_A"]).reshape(ER, V))
    if nonce is None:
        nonce = int(np.random.randint(1 << 20, 1 << 30))
    common = dict(
        Win_i=inputs["L0_attn_Win"],
        bin_i=np.asarray(inputs["L0_attn_bin"])[None, :],
        ln1w_i=inputs["L0_attn_ln1_w"], ln1b_i=inputs["L0_attn_ln1_b"],
        phzr_i=inputs["L0_attn_phz_re"], phzi_i=inputs["L0_attn_phz_im"],
        lcr_i=inputs["L0_attn_lc_re"], lci_i=inputs["L0_attn_lc_im"],
        scl_i=inputs["L0_attn_sc_lin"],
        ln2w_i=inputs["L0_attn_ln2_w"], ln2b_i=inputs["L0_attn_ln2_b"],
        W1_i=inputs["L0_attn_W1"], b1_i=inputs["L0_attn_b1"],
        W2_i=inputs["L0_attn_W2"], b2_i=inputs["L0_attn_b2"],
        lnlw_i=np.asarray(inputs["L0_attn_lnl_w"])[None, :],
        lnlb_i=np.asarray(inputs["L0_attn_lnl_b"])[None, :],
        Wout_i=inputs["L0_attn_Wout"],
        bout_i=np.asarray(inputs["L0_attn_bout"])[None, :],
        baseW_i=inputs["L0_ffn_base_W"],
        gateW_i=gate_pad,
        loraA_i=loraA_flat,
        loraB_i=inputs["L0_ffn_lora_B"],
    )
    common = {k: np.ascontiguousarray(np.asarray(v, np.float32))
              for k, v in common.items()}
    in_maps = []
    for r in range(8):
        erole = (r % 2 == 0)
        inner_lo = 0 if erole else NIN
        outer_lo = NOUT * r
        own = inner_lo <= outer_lo and outer_lo + NOUT <= inner_lo + NIN
        myslot = 0 if erole else 1
        rolef = np.zeros((1, 8), np.float32)
        rolef[0, 0] = 0.0 if erole else 1.0
        rolef[0, 1] = 0.0 if own else 1.0
        rolef[0, 2] = float(myslot)
        flags = np.zeros((1, 16), np.int32)
        myflag = np.zeros((1, 16), np.int32)
        for l in range(DEPTH):
            flags[0, 2 * l + (1 - myslot)] = nonce + 2 * l + (1 - myslot)
            myflag[0, 2 * l + myslot] = nonce + 2 * l + myslot
        flags[0, 8 + (1 - myslot)] = nonce + 8 + (1 - myslot)
        flags[0, 8 + myslot] = nonce + 8 + myslot
        myflag[0, 8 + myslot] = nonce + 8 + myslot
        pollcol = (8 + myslot) if own else (8 + (1 - myslot))
        flags[0, 10] = 8 + myslot
        flags[0, 11] = pollcol
        flags[0, 12] = myslot
        flags[0, 13] = 0 if own else 1
        flags[0, 14] = NOUT * r
        flags[0, 15] = nonce + pollcol
        myflag[0, 14] = nonce + 8 + myslot
        myflag[0, 15] = NIN * myslot
        m = dict(common)
        m["x_inner"] = np.ascontiguousarray(x[inner_lo:inner_lo + NIN])
        m["x_outer"] = np.ascontiguousarray(x[outer_lo:outer_lo + NOUT])
        m["rolef_i"] = rolef
        m["flags_i"] = flags.copy()
        m["myflag_i"] = myflag.copy()
        in_maps.append(m)
    return in_maps


def kernel(**inputs):
    in_maps = _prep_inputs(inputs)
    nc = build_program()
    res = run_bass_kernel_spmd(nc, in_maps, core_ids=list(range(8)))
    out = np.concatenate([np.asarray(res.results[r]["out"]) for r in range(8)],
                         axis=0)
    return out[None, :, :].astype(np.float32)


# version 10
# speedup vs baseline: 1.0757x; 1.0088x over previous
import contextlib
import dataclasses
import numpy as np
import ml_dtypes

import concourse.bass as bass
import concourse.mybir as mybir
from concourse.bass_utils import run_bass_kernel_spmd

F32 = mybir.dt.float32
BF16 = mybir.dt.bfloat16
I32 = mybir.dt.int32
AF = mybir.ActivationFunctionType
PM2 = mybir.MatmulPerfMode.DoubleRow
ALU = mybir.AluOpType
AX = mybir.AxisListType
DS = bass.ds

L_FULL = 16384
V = 4096
DIM = 512
DFF = 2048
DEPTH = 4
EPS = 1e-5
NIN = 8192
NOUT = 2048
CC = 126
NCH = 66
GRID = NCH * CC
ATILE = 256
NAT = NIN // ATILE
NVC = V // 128
NDC = DIM // 128
NFC = DFF // 128
MODP = 2 * CC - 1
NREP = 33
BIGN = NREP * MODP
TGRP = 32
NTG = DIM // TGRP
NE = 28
RK = 8
ER = NE * RK
SCALING = 2.0


class Sem:
    def __init__(self, nc, name):
        self.h = nc.alloc_semaphore(name=name)
        self.n = 0

    def inc(self, instr, k=1):
        instr.then_inc(self.h, k)
        self.n += k
        return self.n

    def dma(self, instr):
        return self.inc(instr, 16)


class SemRing:

    def __init__(self, nc, name, k):
        self.sem = Sem(nc, name)
        self.k = k
        self.idx = 0

    def issue(self, eng, mk):
        self.sem.dma(mk())
        self.idx += 1
        return self.idx - 1

    def wait_issue(self, eng, j):
        eng.wait_ge(self.sem.h, 16 * (j + 1))

    def wait_all(self, eng):
        if self.idx:
            eng.wait_ge(self.sem.h, 16 * self.idx)


RACE_CHECK = False


def build_program(n_layers=DEPTH, stop_after=None, dumps=(), nat=NAT):
    nc = bass.Bass("TRN2", target_bir_lowering=False, debug=False,
                   detect_race_conditions=RACE_CHECK,
                   dynamic_dma_scratch_size=32768)
    es = contextlib.ExitStack()

    def din(name, shape, dtype=F32):
        return nc.dram_tensor(name, shape, dtype, kind="ExternalInput").ap()

    x_inner = din("x_inner", [NIN, V])
    x_outer = din("x_outer", [NOUT, V])
    Win_i = din("Win_i", [DIM, V])
    bin_i = din("bin_i", [1, DIM])
    ln1w_i = din("ln1w_i", [DEPTH, DIM]); ln1b_i = din("ln1b_i", [DEPTH, DIM])
    phzr_i = din("phzr_i", [DEPTH, DIM]); phzi_i = din("phzi_i", [DEPTH, DIM])
    lcr_i = din("lcr_i", [DEPTH, DIM]); lci_i = din("lci_i", [DEPTH, DIM])
    scl_i = din("scl_i", [DEPTH, DIM])
    ln2w_i = din("ln2w_i", [DEPTH, DIM]); ln2b_i = din("ln2b_i", [DEPTH, DIM])
    W1_i = din("W1_i", [DEPTH, DFF, DIM]); b1_i = din("b1_i", [DEPTH, DFF])
    W2_i = din("W2_i", [DEPTH, DIM, DFF]); b2_i = din("b2_i", [DEPTH, DIM])
    lnlw_i = din("lnlw_i", [1, DIM]); lnlb_i = din("lnlb_i", [1, DIM])
    Wout_i = din("Wout_i", [V, DIM]); bout_i = din("bout_i", [1, V])
    baseW_i = din("baseW_i", [V, V])
    gateW_i = din("gateW_i", [32, V])
    loraA_i = din("loraA_i", [ER, V])
    loraB_i = din("loraB_i", [NE, V, RK])
    rolef_i = din("rolef_i", [1, 8])
    flags_i = din("flags_i", [1, 16], I32)
    myflag_i = din("myflag_i", [1, 16], I32)

    out_t = nc.dram_tensor("out", [NOUT, V], F32, kind="ExternalOutput").ap()
    dump_t = {}
    for nm, shape in (("h", [NIN, DIM]), ("y", [GRID, DIM]), ("yf", [GRID, DIM]),
                      ("si", [128, 8 * NCH]), ("x1", [NOUT, V]), ("z2", [NOUT, V]),
                      ("ffn", [NOUT, V])):
        if nm in dumps:
            dump_t[nm] = nc.dram_tensor("dbg_" + nm, shape, F32,
                                        kind="ExternalOutput").ap()

    h_dram = nc.dram_tensor("h_dram", [NIN, DIM], F32, kind="Internal").ap()
    y_dram = nc.dram_tensor("y_dram", [GRID, DIM], BF16, kind="Internal").ap()
    zzbf_dram = nc.dram_tensor("zzbf_dram", [NIN, DIM], BF16, kind="Internal").ap()
    yf_dram = nc.dram_tensor("yf_dram", [GRID, DIM], F32, kind="Internal").ap()
    winbf_dram = nc.dram_tensor("winbf_dram", [DIM, V], BF16, kind="Internal").ap()
    xbf_dram = nc.dram_tensor("xbf_dram", [NIN, V], BF16, kind="Internal").ap()
    w1bf_dram = nc.dram_tensor("w1bf_dram", [DEPTH, DFF, DIM], BF16, kind="Internal").ap()
    w2bf_dram = nc.dram_tensor("w2bf_dram", [DEPTH, DIM, DFF], BF16, kind="Internal").ap()
    woutbf_dram = nc.dram_tensor("woutbf_dram", [V, DIM], BF16, kind="Internal").ap()
    basebf_dram = nc.dram_tensor("basebf_dram", [V, V], BF16, kind="Internal").ap()
    gatebf_dram = nc.dram_tensor("gatebf_dram", [32, V], BF16, kind="Internal").ap()
    lorabf_dram = nc.dram_tensor("lorabf_dram", [ER, V], BF16, kind="Internal").ap()
    powrow_dram = nc.dram_tensor("powrow_dram", [DIM, MODP], BF16, kind="Internal").ap()
    big_dram = nc.dram_tensor("big_dram", [DIM, BIGN], BF16, kind="Internal").ap()
    misc_dram = nc.dram_tensor("misc_dram", [128, 64], F32, kind="Internal").ap()

    sh_h = nc.dram_tensor("sh_h", [L_FULL, DIM], F32, kind="Internal",
                          addr_space="Shared").ap()
    sh_carry = nc.dram_tensor("sh_carry", [DEPTH, 2, 2, DIM], F32, kind="Internal",
                              addr_space="Shared").ap()
    sh_flags = nc.dram_tensor("sh_flags", [1, 16], I32, kind="Internal",
                              addr_space="Shared").ap()

    def sbuf(name, shape, dtype):
        return es.enter_context(nc.sbuf_tensor(name, shape, dtype))

    ident = sbuf("ident", [128, 128], BF16)
    onesf = sbuf("onesf", [65, 128], F32)
    onesbf = sbuf("onesbf", [1, 128], BF16)
    lnlw_row = sbuf("lnlw_row", [1, DIM], F32)
    lnlb_row = sbuf("lnlb_row", [1, DIM], F32)
    b1col = sbuf("b1col", [128, NFC], F32)
    b2_row = sbuf("b2_row", [1, DIM], BF16)
    rolef_sb = sbuf("rolef_sb", [1, 8], F32)
    flagv_sb = sbuf("flagv_sb", [1, 16], I32)
    myflag_sb = sbuf("myflag_sb", [1, 16], I32)
    polltmp = sbuf("polltmp", [1, 2], I32)
    wsel = sbuf("wsel", [128, 4], F32)
    epscol = sbuf("epscol", [128, 1], F32)

    NBC = 5
    NBT = NIN // 128
    NST = NBT // 4
    bcast_pool = sbuf("bcast_pool", [128, NBC * DIM], F32)
    BC_SCL, BC_L1W, BC_L1B, BC_L2W, BC_L2B = range(5)

    def bc(i):
        return bcast_pool[:, i * DIM:(i + 1) * DIM]

    stat = sbuf("stat", [128, 160], F32)

    esA = contextlib.ExitStack()

    def sbufA(name, shape, dtype):
        return esA.enter_context(nc.sbuf_tensor(name, shape, dtype))

    bin_row = sbufA("bin_row", [1, DIM], F32)
    wcs_row = sbufA("wcs_row", [1, DIM], F32)
    abc = sbufA("abc", [128, 2 * DIM], F32)
    xa = sbufA("xa", [128, 2 * 2 * V], F32)
    xsq = sbufA("xsq", [128, V], F32)
    xaT = sbufA("xaT", [128, 2 * NVC * ATILE], BF16)
    winT = sbufA("winT", [128, NVC * DIM], BF16)
    habuf = sbufA("habuf", [128, 2 * 2 * DIM], F32)

    ptA = es.enter_context(nc.psum_tensor("ptA", [128, 1024], BF16))
    ptB = es.enter_context(nc.psum_tensor("ptB", [128, 1024], BF16))
    pmA = es.enter_context(nc.psum_tensor("pmA", [128, 512], F32))
    pmB = es.enter_context(nc.psum_tensor("pmB", [128, 512], F32))
    pmC = es.enter_context(nc.psum_tensor("pmC", [128, 512], F32))
    pmD = es.enter_context(nc.psum_tensor("pmD", [128, 512], F32))
    pmE = es.enter_context(nc.psum_tensor("pmE", [128, 512], F32))
    pmF = es.enter_context(nc.psum_tensor("pmF", [128, 512], F32))

    dS = Sem(nc, "dS")
    tE = Sem(nc, "tE"); vE = Sem(nc, "vE"); aE = Sem(nc, "aE"); pE = Sem(nc, "pE")
    bar = Sem(nc, "bar")
    nbar = [0]

    blk = es.enter_context(nc.Block())

    def barrier():
        nbar[0] += 1
        tgt = 5 * nbar[0]
        for en in ("sync", "tensor", "vector", "scalar", "gpsimd"):
            def body(eng, tgt=tgt):
                eng.sem_inc(bar.h, 1)
                eng.wait_ge(bar.h, tgt)
            getattr(blk, en)(body)
        bar.n = tgt

    dw_win = Sem(nc, "dw_win")
    dw_w1 = [Sem(nc, f"dw_w1_{l}") for l in range(DEPTH)]
    dw_w2 = [Sem(nc, f"dw_w2_{l}") for l in range(DEPTH)]
    dw_wout = Sem(nc, "dw_wout")
    dw_gate = Sem(nc, "dw_gate")
    dw_lora = Sem(nc, "dw_lora")
    dw_bout = Sem(nc, "dw_bout")
    dw_base = Sem(nc, "dw_base")
    d0 = SemRing(nc, "d0", 4)
    dmisc = Sem(nc, "dmisc")
    dwT = SemRing(nc, "dwT", 4)

    dw_xbf = Sem(nc, "dw_xbf")

    @blk.gpsimd
    def _(g):
        for xc in range(8):
            dw_xbf.dma(g.dma_start(out=xbf_dram[xc * (NIN // 8):(xc + 1) * (NIN // 8)],
                                   in_=x_inner[xc * (NIN // 8):(xc + 1) * (NIN // 8)]))
        dw_win.dma(g.dma_start(out=winbf_dram, in_=Win_i))
        for l in range(n_layers):
            dw_w1[l].dma(g.dma_start(out=w1bf_dram[l], in_=W1_i[l]))
            dw_w2[l].dma(g.dma_start(out=w2bf_dram[l], in_=W2_i[l]))
        dw_wout.dma(g.dma_start(out=woutbf_dram, in_=Wout_i))
        dw_gate.dma(g.dma_start(out=gatebf_dram, in_=gateW_i))
        dw_lora.dma(g.dma_start(out=lorabf_dram, in_=loraA_i))
        dw_base.dma(g.dma_start(out=basebf_dram, in_=baseW_i))

    @blk.sync
    def _(s):
        for dst, srcp in ((bin_row[:], bin_i), (lnlw_row[:], lnlw_i),
                          (lnlb_row[:], lnlb_i), (rolef_sb[:], rolef_i),
                          (flagv_sb[:], flags_i), (myflag_sb[:], myflag_i)):
            d0.issue(s, lambda dst=dst, srcp=srcp: s.dma_start(out=dst, in_=srcp))
        for c in range(4):
            d0.issue(s, lambda c=c: s.dma_start(out=xa[:, c * V:(c + 1) * V],
                                                in_=Win_i[c * 128:(c + 1) * 128]))

    @blk.gpsimd
    def _(g):
        g.memset(ident[:], 0.0)
        g.drain()
        g.affine_select(out=ident[:], in_=ident[:], compare_op=ALU.not_equal,
                        fill=1.0, base=0, pattern=[[-1, 128]], channel_multiplier=1)
        ins = g.memset(onesbf[:], 1.0)
        pE.inc(ins)

    @blk.vector
    def _(v):
        d0.wait_all(v)
        v.memset(onesf[:], 1.0)
        v.memset(epscol[:], EPS)
        for c in range(4):
            v.reduce_sum(stat[:, c:c + 1], xa[:, c * V:(c + 1) * V], axis=AX.X)
        v.drain()
        vE.inc(v.tensor_copy(stat[:, 8:9], stat[:, 0:1]))

    @blk.sync
    def _(s):
        s.wait_ge(vE.h, vE.n)
        dmisc.dma(s.dma_start(out=misc_dram[:, 0:4], in_=stat[:, 0:4]))
        s.wait_ge(dmisc.h, dmisc.n)
        srcw = dataclasses.replace(misc_dram.flatten(), offset=0,
                                   ap=[[1, 1], [1, 4], [64, 128]])
        dstw = dataclasses.replace(wcs_row[:], offset=0, ap=[[DIM, 1], [1, DIM]])
        dmisc.dma(s.dma_start(out=dstw, in_=srcw, allow_slow_non_contiguous=True))
        s.wait_ge(dmisc.h, dmisc.n)

    @blk.tensor
    def _(t):
        t.wait_ge(dmisc.h, dmisc.n)
        t.wait_ge(pE.h, pE.n)
        t.wait_ge(vE.h, vE.n)
        t.matmul(pmA[:, 0:DIM], onesf[0:1], wcs_row[:], start=True, stop=True)
        t.matmul(pmB[:, 0:DIM], onesf[0:1], bin_row[:], start=True, stop=True)
        ins = t.matmul(pmC[:, 0:8], onesf[0:1], rolef_sb[:], start=True, stop=True)
        tE.inc(ins)

    @blk.vector
    def _(v):
        v.wait_ge(tE.h, tE.n)
        v.tensor_copy(abc[:, 0:DIM], pmA[:, 0:DIM])
        v.tensor_copy(abc[:, DIM:2 * DIM], pmB[:, 0:DIM])
        v.tensor_copy(wsel[:, 0:1], pmC[:, 0:1])
        v.tensor_scalar(wsel[:, 1:2], pmC[:, 0:1], -1.0, 1.0,
                        op0=ALU.mult, op1=ALU.add)
        v.drain()
        vE.inc(v.tensor_copy(stat[:, 9:10], stat[:, 0:1]))

    @blk.sync
    def _(s):
        s.wait_ge(dw_win.h, 16)
        for vck in range(NVC):
            dwT.issue(s, lambda vck=vck: s.dma_start(
                out=winT[:, vck * DIM:(vck + 1) * DIM],
                in_=winbf_dram[:, vck * 128:(vck + 1) * 128], transpose=True))

    barrier()

    AC = ATILE // 128
    NGR = AC * NVC // 4

    dX2 = SemRing(nc, "dX2", 2)
    dH2 = SemRing(nc, "dH2", 2)
    eT = Sem(nc, "eT")

    xaT_base = {}
    mmA_done = {}
    evacA_done = {}

    @blk.sync
    def _(s):
        dwT.wait_all(s)
        s.wait_ge(dw_xbf.h, 16)
        for c in range(AC):
            dX2.issue(s, lambda c=c: s.dma_start(
                out=xa[:, c * V:(c + 1) * V], in_=x_inner[c * 128:(c + 1) * 128]))
        xaT_base[0] = dwT.idx
        for vck in range(NVC):
            dwT.issue(s, lambda vck=vck: s.dma_start(
                out=xaT[:, vck * ATILE:(vck + 1) * ATILE],
                in_=xbf_dram[0:ATILE, vck * 128:(vck + 1) * 128], transpose=True))

    for ti in range(nat):
        par = ti % 2
        xoff = par * AC * V
        sbase = 16 + par * 40

        @blk.sync
        def _(s, ti=ti):
            if ti + 1 < nat:
                nb = (ti + 1) % 2
                if ti >= 1:
                    s.wait_ge(vE.h, vE.n)
                    s.wait_ge(tE.h, mmA_done[ti - 1])
                for c in range(AC):
                    dX2.issue(s, lambda ti=ti, nb=nb, c=c: s.dma_start(
                        out=xa[:, (nb * AC + c) * V:(nb * AC + c + 1) * V],
                        in_=x_inner[(ti + 1) * ATILE + c * 128:
                                    (ti + 1) * ATILE + (c + 1) * 128]))
                need = ((ti + 2) * ATILE - 1) // (NIN // 8) + 1
                s.wait_ge(dw_xbf.h, 16 * min(need, 8))
                xaT_base[ti + 1] = dwT.idx
                toff = nb * NVC * ATILE
                for vck in range(NVC):
                    dwT.issue(s, lambda ti=ti, vck=vck, toff=toff: s.dma_start(
                        out=xaT[:, toff + vck * ATILE: toff + (vck + 1) * ATILE],
                        in_=xbf_dram[(ti + 1) * ATILE:(ti + 2) * ATILE,
                                     vck * 128:(vck + 1) * 128], transpose=True))

        @blk.vector
        def _(v, ti=ti, par=par, xoff=xoff, sbase=sbase):
            dX2.wait_issue(v, 2 * ti)
            dX2.wait_issue(v, 2 * ti + 1)
            for c in range(AC):
                xv = xa[:, xoff + c * V: xoff + (c + 1) * V]
                v.reduce_sum(stat[:, sbase + c: sbase + c + 1], xv, axis=AX.X)
                v.scalar_tensor_tensor(
                    xsq[:], xv, 1.0, xv, op0=ALU.mult, op1=ALU.mult,
                    accum_out=stat[:, sbase + 2 + c: sbase + 3 + c])
            v.drain()
            v.tensor_scalar(stat[:, sbase + 4: sbase + 6], stat[:, sbase: sbase + 2],
                            1.0 / V, None, op0=ALU.mult)
            v.drain()
            v.scalar_tensor_tensor(stat[:, sbase + 6: sbase + 8],
                                   stat[:, sbase + 4: sbase + 6], -1.0,
                                   stat[:, sbase + 4: sbase + 6],
                                   op0=ALU.mult, op1=ALU.mult)
            v.drain()
            vE.inc(v.scalar_tensor_tensor(stat[:, sbase + 6: sbase + 8],
                                          stat[:, sbase + 2: sbase + 4], 1.0 / V,
                                          stat[:, sbase + 6: sbase + 8],
                                          op0=ALU.mult, op1=ALU.add))

        @blk.scalar
        def _(sc, ti=ti, sbase=sbase):
            sc.wait_ge(vE.h, vE.n)
            sc.activation(stat[:, sbase + 8: sbase + 10],
                          stat[:, sbase + 6: sbase + 8], AF.Ln, bias=epscol[:, 0:1])
            sc.drain()
            ins = sc.activation(stat[:, sbase + 10: sbase + 12],
                                stat[:, sbase + 8: sbase + 10], AF.Exp, scale=-0.5)
            sc.drain()
            aE.inc(ins)

        @blk.tensor
        def _(t, ti=ti, par=par):
            dwT.wait_issue(t, xaT_base[ti] + NVC - 1)
            if ti >= 1:
                t.wait_ge(aE.h, evacA_done[ti - 1])
            toff = par * NVC * ATILE
            for c in range(AC):
                ps = pmA if c == 0 else pmB
                for kc in range(NVC):
                    xt = xaT[:, toff + kc * ATILE + c * 128:
                             toff + kc * ATILE + c * 128 + 128]
                    ins = t.matmul(ps[:, 0:DIM], xt, winT[:, kc * DIM:(kc + 1) * DIM],
                                   start=(kc == 0), stop=(kc == NVC - 1))
            tE.inc(ins)
            mmA_done[ti] = tE.n

        @blk.scalar
        def _(sc, ti=ti, par=par, sbase=sbase):
            sc.wait_ge(tE.h, mmA_done[ti])
            sc.wait_ge(aE.h, aE.n)
            if ti >= 2:
                dH2.wait_issue(sc, 2 * (ti - 2))
                dH2.wait_issue(sc, 2 * (ti - 2) + 1)
            for c in range(AC):
                ps = pmA if c == 0 else pmB
                rs = stat[:, sbase + 10 + c: sbase + 11 + c]
                sc.activation(habuf[:, (par * AC + c) * DIM:(par * AC + c + 1) * DIM],
                              ps[:, 0:DIM], AF.Copy, scale=rs)
            sc.drain()
            aE.inc(sc.copy(stat[:, sbase + 14:sbase + 15], stat[:, sbase:sbase + 1]))
            evacA_done[ti] = aE.n

        @blk.vector
        def _(v, ti=ti, par=par, sbase=sbase):
            v.wait_ge(aE.h, aE.n)
            v.scalar_tensor_tensor(stat[:, sbase + 12: sbase + 14],
                                   stat[:, sbase + 4: sbase + 6], -1.0,
                                   stat[:, sbase + 10: sbase + 12],
                                   op0=ALU.mult, op1=ALU.mult)
            v.drain()
            for c in range(AC):
                hb = habuf[:, (par * AC + c) * DIM:(par * AC + c + 1) * DIM]
                v.scalar_tensor_tensor(hb, abc[:, 0:DIM],
                                       stat[:, sbase + 12 + c: sbase + 13 + c],
                                       hb, op0=ALU.mult, op1=ALU.add)
            v.drain()
            for c in range(AC):
                hb = habuf[:, (par * AC + c) * DIM:(par * AC + c + 1) * DIM]
                ins = v.tensor_add(hb, hb, abc[:, DIM:2 * DIM])
            vE.inc(ins)

        @blk.sync
        def _(s, ti=ti, par=par):
            s.wait_ge(vE.h, vE.n)
            for c in range(AC):
                dH2.issue(s, lambda ti=ti, par=par, c=c: s.dma_start(
                    out=h_dram[ti * ATILE + c * 128: ti * ATILE + (c + 1) * 128],
                    in_=habuf[:, (par * AC + c) * DIM:(par * AC + c + 1) * DIM]))

    barrier()

    if "h" in dumps:
        @blk.sync
        def _(s):
            nrow = nat * ATILE
            dH2.wait_all(s)
            dmisc.dma(s.dma_start(out=dump_t["h"][0:nrow], in_=h_dram[0:nrow]))
            s.wait_ge(dmisc.h, dmisc.n)
        barrier()

    if stop_after == "A":
        _emit_dummy_out(blk, nc, out_t, bcast_pool, dmisc, vE)
        esA.close()
        es.close()
        return nc

    @blk.sync
    def _(s):
        dH2.wait_all(s)
        dX2.wait_all(s)
    barrier()
    esA.close()

    esB = contextlib.ExitStack()

    def sbufB(name, shape, dtype):
        return esB.enter_context(nc.sbuf_tensor(name, shape, dtype))

    CCP = CC + 1
    hbt = sbufB("hbt", [128, 4 * DIM], F32)
    bscr = sbufB("bscr", [128, 4 * DIM], F32)
    y126 = sbufB("y126", [128, NCH * DIM], BF16)
    yfbt = sbufB("yfbt", [128, 4 * DIM], F32)
    zbf = sbufB("zbf", [128, 4 * DIM], BF16)
    zT = sbufB("zT", [128, 2 * 4 * DIM], BF16)
    pch = sbufB("pch", [128, 16], F32)
    powc = sbufB("powc", [128, 2 * 4 * CCP], F32)
    apowv = sbufB("apowv", [128, 2 * 4 * NCH], F32)
    lcvv = sbufB("lcvv", [128, 8], F32)
    carryv = sbufB("carryv", [128, 8], F32)
    recvv = sbufB("recvv", [128, 8], F32)
    statev = sbufB("statev", [128, 16], F32)
    sire = sbufB("sire", [128, 4 * NCH], F32)
    siim = sbufB("siim", [128, 4 * NCH], F32)
    si_inr = sbufB("si_inr", [128, 4 * NCH], F32)
    si_ini = sbufB("si_ini", [128, 4 * NCH], F32)
    sc1 = sbufB("sc1", [128, 4 * NCH], F32)
    sc2 = sbufB("sc2", [128, 4 * NCH], F32)


    tl2 = sbufB("tl2", [128, 8], F32)
    tl2bf = sbufB("tl2bf", [128, 8], BF16)
    zrow = sbufB("zrow", [128, DIM], BF16)

    dHB = SemRing(nc, "dHB", 4)
    dY = SemRing(nc, "dY", 4)
    dCk = SemRing(nc, "dCk", 4)
    dTh = SemRing(nc, "dTh", 4)
    dT = SemRing(nc, "dT", 2)
    dYW = SemRing(nc, "dYW", 2)
    dYF = SemRing(nc, "dYF", 4)
    dHW = SemRing(nc, "dHW", 4)
    dPr = SemRing(nc, "dPr", 4)
    dSt = SemRing(nc, "dSt", 2)
    eC = Sem(nc, "eC")
    sE = Sem(nc, "sE")
    eTH = Sem(nc, "eTH")
    eZ = Sem(nc, "eZ")
    thin_dram = nc.dram_tensor("thin_dram", [2, DIM * NCH], F32, kind="Internal").ap()
    si_dram = nc.dram_tensor("si_dram", [2, NCH * DIM], BF16, kind="Internal").ap()
    misc2_dram = nc.dram_tensor("misc2_dram", [2, DIM], BF16, kind="Internal").ap()

    sreg_slot = nc.alloc_register(mybir.EngineType.SP, "sreg_slot")
    sreg_flagv = nc.alloc_register(mybir.EngineType.SP, "sreg_flagv")
    sreg_tgt = nc.alloc_register(mybir.EngineType.SP, "sreg_tgt")
    sreg_cond = nc.alloc_register(mybir.EngineType.SP, "sreg_cond")
    sreg_dmc = nc.alloc_register(mybir.EngineType.SP, "sreg_dmc")
    sreg_mycol = nc.alloc_register(mybir.EngineType.SP, "sreg_mycol")
    sreg_pollc = nc.alloc_register(mybir.EngineType.SP, "sreg_pollc")
    sreg_hoff = nc.alloc_register(mybir.EngineType.SP, "sreg_hoff")
    sreg_pub = nc.alloc_register(mybir.EngineType.SP, "sreg_pub")
    dm = Sem(nc, "dm")

    d_b2 = SemRing(nc, "d_b2", 2)
    dZB = SemRing(nc, "dZB", 4)
    dZT = SemRing(nc, "dZT", 4)
    dHB2 = SemRing(nc, "dHB2", 4)
    dH2g = SemRing(nc, "dH2g", 4)
    dYg = SemRing(nc, "dYg", 4)
    fE = Sem(nc, "fE")
    hE = Sem(nc, "hE")
    zt_bases = {}
    f1_done = {}
    zbA = {}
    h2base = {}
    hb2base = {}
    hb4 = {}
    yf4 = {}
    he_done = {}
    b1v2 = {}
    dyb1 = {}

    def snap_pub():
        return nc.sync.snap(sreg_pub, min_val=0, max_val=NIN)

    def snap_hoff():
        return nc.sync.snap(sreg_hoff, min_val=0, max_val=L_FULL - NOUT)

    def snap_mycol():
        return nc.sync.snap(sreg_mycol, min_val=8, max_val=9)

    def snap_pollc():
        return nc.sync.snap(sreg_pollc, min_val=8, max_val=9)

    @blk.sync
    def _(s):
        nc.sync.reg_load(sreg_slot, flagv_sb[0:1, 12:13])
        nc.sync.reg_load(sreg_mycol, flagv_sb[0:1, 10:11])
        nc.sync.reg_load(sreg_pollc, flagv_sb[0:1, 11:12])
        nc.sync.reg_load(sreg_hoff, flagv_sb[0:1, 14:15])
        nc.sync.reg_load(sreg_pub, myflag_sb[0:1, 15:16])
        nc.sync.reg_mov(sreg_dmc, 0)

    @blk.vector
    def _(v):
        vE.inc(v.memset(recvv[:], 0.0))

    for l in range(n_layers):
        es23 = contextlib.ExitStack()
        tbuf = es23.enter_context(nc.sbuf_tensor(f"tbuf{l}", [128, 2 * TGRP * 128], BF16))
        yfst = es23.enter_context(nc.sbuf_tensor(f"yfst{l}", [128, TGRP * NCH], F32))
        thst = es23.enter_context(nc.sbuf_tensor(f"thst{l}", [2, 2 * 462], F32))
        swpair = es23.enter_context(nc.sbuf_tensor(f"swpair{l}", [128, 2 * DIM], BF16))
        swbf = es23.enter_context(nc.sbuf_tensor(f"swbf{l}", [128, 2 * DIM], BF16))
        crre_bf = es23.enter_context(nc.sbuf_tensor(f"crre{l}", [128, 4 * CC], BF16))
        powrow_sb = es23.enter_context(nc.sbuf_tensor(f"powrow{l}", [128, 4 * MODP], BF16))
        sirow = es23.enter_context(nc.sbuf_tensor(f"sirow{l}", [128, 2 * DIM], F32))
        crimn_bf = es23.enter_context(nc.sbuf_tensor(f"crimn{l}", [128, 4 * CC], BF16))
        sibf = es23.enter_context(nc.sbuf_tensor(f"sibf{l}", [128, 2 * DIM], BF16))

        @blk.sync
        def _(s, l=l):
            for (k, srcp, dstt, dcol) in ((0, phzr_i, pch, 0), (1, phzi_i, pch, 4),
                                          (2, lcr_i, lcvv, 0), (3, lci_i, lcvv, 4)):
                srcv = dataclasses.replace(srcp.flatten(), offset=l * DIM,
                                           ap=[[1, 128], [128, 4]])
                dstv = dataclasses.replace(
                    dstt[:, dcol:dcol + 4], ap=[[dstt[:].ap[0][0], 128], [1, 4]])
                dPr.issue(s, lambda dstv=dstv, srcv=srcv: s.dma_start(
                    out=dstv, in_=srcv, allow_slow_non_contiguous=True))
            dPr.issue(s, lambda l=l: s.dma_start(out=sirow[0:1, 0:DIM], in_=ln1w_i[l:l + 1]))
            dPr.issue(s, lambda l=l: s.dma_start(out=sirow[0:1, DIM:2 * DIM], in_=ln1b_i[l:l + 1]))
            dPr.issue(s, lambda l=l: s.dma_start(out=sirow[32:33, 0:DIM], in_=ln2w_i[l:l + 1]))
            dPr.issue(s, lambda l=l: s.dma_start(out=sirow[32:33, DIM:2 * DIM], in_=ln2b_i[l:l + 1]))
            dPr.issue(s, lambda l=l: s.dma_start(out=sirow[64:65, 0:DIM], in_=scl_i[l:l + 1]))
            srcb1 = dataclasses.replace(b1_i.flatten(), offset=l * DFF,
                                        ap=[[1, 128], [128, NFC]])
            dstb1 = dataclasses.replace(b1col[:], ap=[[NFC, 128], [1, NFC]])
            dPr.issue(s, lambda srcb1=srcb1, dstb1=dstb1: s.dma_start(
                out=dstb1, in_=srcb1, allow_slow_non_contiguous=True))

        @blk.gpsimd
        def _(g_, l=l):
            d_b2.issue(g_, lambda l=l: g_.dma_start(out=b2_row[:], in_=b2_i[l:l + 1]))

        @blk.tensor
        def _(t, l=l):
            dPr.wait_all(t)
            t.matmul(pmA[:, 0:DIM], onesf[0:1], sirow[0:1, 0:DIM], start=True, stop=True)
            t.matmul(pmB[:, 0:DIM], onesf[0:1], sirow[0:1, DIM:2 * DIM], start=True, stop=True)
            t.matmul(pmC[:, 0:DIM], onesf[32:33], sirow[32:33, 0:DIM], start=True, stop=True)
            t.matmul(pmD[:, 0:DIM], onesf[32:33], sirow[32:33, DIM:2 * DIM], start=True, stop=True)
            ins = t.matmul(pmE[:, 0:DIM], onesf[64:65], sirow[64:65, 0:DIM], start=True, stop=True)
            tE.inc(ins)

        @blk.vector
        def _(v, l=l):
            v.wait_ge(tE.h, tE.n)
            v.tensor_copy(bc(BC_L1W), pmA[:, 0:DIM])
            v.tensor_copy(bc(BC_L1B), pmB[:, 0:DIM])
            v.tensor_copy(bc(BC_L2W), pmC[:, 0:DIM])
            v.tensor_copy(bc(BC_L2B), pmD[:, 0:DIM])
            v.tensor_copy(bc(BC_SCL), pmE[:, 0:DIM])
            v.memset(zrow[:], 0.0)
            re_ = pch[:, 0:4]; im_ = pch[:, 4:8]
            r2 = pch[:, 8:12]
            v.drain()
            v.scalar_tensor_tensor(r2, re_, 1.0, re_, op0=ALU.mult, op1=ALU.mult)
            v.scalar_tensor_tensor(sc1[:, 0:4], im_, 1.0, im_, op0=ALU.mult, op1=ALU.mult)
            v.drain()
            vE.inc(v.tensor_add(r2, r2, sc1[:, 0:4]))

        @blk.scalar
        def _(sc, l=l):
            sc.wait_ge(vE.h, vE.n)
            sc.activation(pch[:, 12:16], pch[:, 8:12], AF.Ln)
            sc.drain()
            sc.activation(pch[:, 8:12], pch[:, 12:16], AF.Exp, scale=0.5)
            sc.drain()
            aE.inc(sc.activation(pch[:, 12:16], pch[:, 8:12], AF.Tanh))

        @blk.vector
        def _(v, l=l):
            v.wait_ge(aE.h, aE.n)
            rinv = sc1[:, 0:4]
            v.reciprocal(rinv, pch[:, 8:12])
            v.drain()
            tor = sc1[:, 4:8]
            v.tensor_mul(tor, pch[:, 12:16], rinv)
            v.drain()
            v.tensor_mul(pch[:, 0:4], pch[:, 0:4], tor)
            v.tensor_mul(pch[:, 4:8], pch[:, 4:8], tor)
            v.drain()
            PR, PI = 0, 4 * CCP
            for g in range(4):
                v.memset(powc[:, PR + g * CCP: PR + g * CCP + 1], 1.0)
                v.memset(powc[:, PI + g * CCP: PI + g * CCP + 1], 0.0)
            v.drain()
            for g in range(4):
                mr = pch[:, g:g + 1]; mi = pch[:, 4 + g:5 + g]
                br = powc[:, PR + g * CCP: PR + g * CCP + 1]
                bi = powc[:, PI + g * CCP: PI + g * CCP + 1]
                v.tensor_scalar(sc1[:, g * NCH: g * NCH + 1], br, mr, None, op0=ALU.mult)
                v.tensor_scalar(sc2[:, g * NCH: g * NCH + 1], bi, mi, None, op0=ALU.mult)
                v.tensor_scalar(si_inr[:, g * NCH: g * NCH + 1], bi, mr, None, op0=ALU.mult)
                v.tensor_scalar(si_ini[:, g * NCH: g * NCH + 1], br, mi, None, op0=ALU.mult)
            v.drain()
            for g in range(4):
                v.tensor_sub(powc[:, PR + g * CCP + 1: PR + g * CCP + 2],
                             sc1[:, g * NCH: g * NCH + 1],
                             sc2[:, g * NCH: g * NCH + 1])
                v.tensor_add(powc[:, PI + g * CCP + 1: PI + g * CCP + 2],
                             si_inr[:, g * NCH: g * NCH + 1],
                             si_ini[:, g * NCH: g * NCH + 1])
            v.drain()
            K = 2
            while K < CCP:
                w = min(K - 1, CCP - K)
                for g in range(4):
                    mr = powc[:, PR + g * CCP + K - 1: PR + g * CCP + K]
                    mi = powc[:, PI + g * CCP + K - 1: PI + g * CCP + K]
                    br = powc[:, PR + g * CCP + 1: PR + g * CCP + 1 + w]
                    bi = powc[:, PI + g * CCP + 1: PI + g * CCP + 1 + w]
                    v.tensor_scalar(sc1[:, g * NCH: g * NCH + w], br, mr, None, op0=ALU.mult)
                    v.tensor_scalar(sc2[:, g * NCH: g * NCH + w], bi, mi, None, op0=ALU.mult)
                    v.tensor_scalar(si_inr[:, g * NCH: g * NCH + w], bi, mr, None, op0=ALU.mult)
                    v.tensor_scalar(si_ini[:, g * NCH: g * NCH + w], br, mi, None, op0=ALU.mult)
                v.drain()
                for g in range(4):
                    v.tensor_sub(powc[:, PR + g * CCP + K: PR + g * CCP + K + w],
                                 sc1[:, g * NCH: g * NCH + w],
                                 sc2[:, g * NCH: g * NCH + w])
                    v.tensor_add(powc[:, PI + g * CCP + K: PI + g * CCP + K + w],
                                 si_inr[:, g * NCH: g * NCH + w],
                                 si_ini[:, g * NCH: g * NCH + w])
                v.drain()
                K += w
            for g in range(4):
                v.tensor_copy(crre_bf[:, g * CC:(g + 1) * CC],
                              powc[:, PR + g * CCP + 1: PR + g * CCP + 1 + CC])
                v.tensor_scalar(crimn_bf[:, g * CC:(g + 1) * CC],
                                powc[:, PI + g * CCP + 1: PI + g * CCP + 1 + CC],
                                -1.0, None, op0=ALU.mult)
            v.drain()
            AR, AI = 0, 4 * NCH
            for g in range(4):
                v.memset(apowv[:, AR + g * NCH: AR + g * NCH + 1], 1.0)
                v.memset(apowv[:, AI + g * NCH: AI + g * NCH + 1], 0.0)
            v.drain()
            for g in range(4):
                mr = powc[:, PR + g * CCP + CC: PR + g * CCP + CCP]
                mi = powc[:, PI + g * CCP + CC: PI + g * CCP + CCP]
                br = apowv[:, AR + g * NCH: AR + g * NCH + 1]
                bi = apowv[:, AI + g * NCH: AI + g * NCH + 1]
                v.tensor_scalar(sc1[:, g * NCH: g * NCH + 1], br, mr, None, op0=ALU.mult)
                v.tensor_scalar(sc2[:, g * NCH: g * NCH + 1], bi, mi, None, op0=ALU.mult)
                v.tensor_scalar(si_inr[:, g * NCH: g * NCH + 1], bi, mr, None, op0=ALU.mult)
                v.tensor_scalar(si_ini[:, g * NCH: g * NCH + 1], br, mi, None, op0=ALU.mult)
            v.drain()
            for g in range(4):
                v.tensor_sub(apowv[:, AR + g * NCH + 1: AR + g * NCH + 2],
                             sc1[:, g * NCH: g * NCH + 1],
                             sc2[:, g * NCH: g * NCH + 1])
                v.tensor_add(apowv[:, AI + g * NCH + 1: AI + g * NCH + 2],
                             si_inr[:, g * NCH: g * NCH + 1],
                             si_ini[:, g * NCH: g * NCH + 1])
            v.drain()
            K = 2
            while K < NCH:
                w = min(K - 1, NCH - K)
                for g in range(4):
                    mr = apowv[:, AR + g * NCH + K - 1: AR + g * NCH + K]
                    mi = apowv[:, AI + g * NCH + K - 1: AI + g * NCH + K]
                    br = apowv[:, AR + g * NCH + 1: AR + g * NCH + 1 + w]
                    bi = apowv[:, AI + g * NCH + 1: AI + g * NCH + 1 + w]
                    v.tensor_scalar(sc1[:, g * NCH: g * NCH + w], br, mr, None, op0=ALU.mult)
                    v.tensor_scalar(sc2[:, g * NCH: g * NCH + w], bi, mi, None, op0=ALU.mult)
                    v.tensor_scalar(si_inr[:, g * NCH: g * NCH + w], bi, mr, None, op0=ALU.mult)
                    v.tensor_scalar(si_ini[:, g * NCH: g * NCH + w], br, mi, None, op0=ALU.mult)
                v.drain()
                for g in range(4):
                    v.tensor_sub(apowv[:, AR + g * NCH + K: AR + g * NCH + K + w],
                                 sc1[:, g * NCH: g * NCH + w],
                                 sc2[:, g * NCH: g * NCH + w])
                    v.tensor_add(apowv[:, AI + g * NCH + K: AI + g * NCH + K + w],
                                 si_inr[:, g * NCH: g * NCH + w],
                                 si_ini[:, g * NCH: g * NCH + w])
                v.drain()
                K += w
            v.memset(powrow_sb[:], 0.0)
            v.drain()
            for g in range(4):
                v.tensor_copy(powrow_sb[:, g * MODP + CC - 1: g * MODP + CC - 1 + CC],
                              powc[:, PR + g * CCP: PR + g * CCP + CC])
            v.memset(swbf[:], 0.0)
            v.drain()
            for g in range(4):
                rsrc_r = dataclasses.replace(
                    powc[:], offset=PR + g * CCP + CC - 1,
                    ap=[list(powc[:].ap[0]), [-1, CC]])
                rsrc_i = dataclasses.replace(
                    powc[:], offset=PI + g * CCP + CC - 1,
                    ap=[list(powc[:].ap[0]), [-1, CC]])
                v.tensor_copy(swbf[:, g * 128: g * 128 + CC], rsrc_r)
                v.tensor_copy(swbf[:, DIM + g * 128: DIM + g * 128 + CC], rsrc_i)
            v.drain()
            vE.inc(v.tensor_copy(sc1[:, 0:1], sc1[:, 0:1]))

        @blk.gpsimd
        def _(g_, l=l, tbuf=tbuf, swpair=swpair):
            g_.memset(tbuf[:], 0.0)
            g_.drain()
            pE.inc(g_.memset(swpair[:], 0.0))

        eZ_base = eZ.n
        tE_base = tE.n

        @blk.tensor
        def _(t, l=l, eZ_base=eZ_base):
            t.wait_ge(vE.h, vE.n)
            t.wait_ge(pE.h, pE.n)
            for k in range(8):
                comp, g = k % 2, k // 2
                if k >= 2:
                    t.wait_ge(eZ.h, eZ_base + k - 1)
                pt = ptA if k % 2 == 0 else ptB
                ins = t.transpose(pt[:, 0:128],
                                  swbf[:, comp * DIM + g * 128: comp * DIM + g * 128 + 128],
                                  ident[:])
                tE.inc(ins)

        @blk.vector
        def _(v, l=l, tE_base=tE_base):
            for k in range(8):
                comp, g = k % 2, k // 2
                v.wait_ge(tE.h, tE_base + k + 1)
                pt = ptA if k % 2 == 0 else ptB
                dst = dataclasses.replace(
                    swpair[:], offset=g * 256 + comp,
                    ap=[list(swpair[:].ap[0]), [2, 128]])
                ins = v.tensor_copy(dst, pt[:, 0:128])
                eZ.inc(ins)

        @blk.sync
        def _(s, l=l):
            s.wait_ge(vE.h, vE.n)
            srcp = dataclasses.replace(
                powrow_sb[:], ap=[[4 * MODP, 128], [MODP, 4], [1, MODP]])
            dstp = dataclasses.replace(
                powrow_dram.flatten(), offset=0,
                ap=[[MODP, 128], [128 * MODP, 4], [1, MODP]])
            dmisc.dma(s.dma_start(out=dstp, in_=srcp))
            s.wait_ge(dmisc.h, dmisc.n)
            dmisc.dma(s.dma_start(out=big_dram[:, 0:MODP], in_=powrow_dram))
            s.wait_ge(dmisc.h, dmisc.n)
            n = MODP
            while n < BIGN:
                m_ = min(n, BIGN - n)
                dmisc.dma(s.dma_start(out=big_dram[:, n:n + m_],
                                      in_=big_dram[:, 0:m_]))
                s.wait_ge(dmisc.h, dmisc.n)
                n += m_

        barrier()

        if l == 0:
            @blk.sync
            def _(s, l=l):
                dmisc.dma(s.dma_start(out=y_dram[NIN:GRID], in_=zrow[0:GRID - NIN, :]))
                s.wait_ge(dmisc.h, dmisc.n)
                for ti in range(4):
                    dHB.issue(s, lambda ti=ti: s.dma_start(
                        out=hbt[:, ti * DIM:(ti + 1) * DIM],
                        in_=h_dram[ti * 128:(ti + 1) * 128]))

        for st_ in (range(NST) if l == 0 else ()):
            sb_ = 16 + (st_ % 2) * 40

            @blk.vector
            def _(v, l=l, st_=st_, sb_=sb_):
                dHB.wait_issue(v, 128 * l + st_ * 4 + 3)
                for q in range(4):
                    hv = hbt[:, q * DIM:(q + 1) * DIM]
                    v.reduce_sum(stat[:, sb_ + q:sb_ + q + 1], hv, axis=AX.X)
                    v.scalar_tensor_tensor(bscr[:, q * DIM:(q + 1) * DIM], hv, 1.0,
                                           hv, op0=ALU.mult, op1=ALU.mult,
                                           accum_out=stat[:, sb_ + 4 + q:sb_ + 5 + q])
                v.drain()
                v.tensor_scalar(stat[:, sb_ + 8:sb_ + 12], stat[:, sb_:sb_ + 4],
                                1.0 / DIM, None, op0=ALU.mult)
                v.drain()
                v.scalar_tensor_tensor(stat[:, sb_ + 12:sb_ + 16],
                                       stat[:, sb_ + 8:sb_ + 12], -1.0,
                                       stat[:, sb_ + 8:sb_ + 12],
                                       op0=ALU.mult, op1=ALU.mult)
                v.drain()
                vE.inc(v.scalar_tensor_tensor(stat[:, sb_ + 12:sb_ + 16],
                                              stat[:, sb_ + 4:sb_ + 8], 1.0 / DIM,
                                              stat[:, sb_ + 12:sb_ + 16],
                                              op0=ALU.mult, op1=ALU.add))

            @blk.scalar
            def _(sc, l=l, st_=st_, sb_=sb_):
                sc.wait_ge(vE.h, vE.n)
                sc.activation(stat[:, sb_ + 16:sb_ + 20], stat[:, sb_ + 12:sb_ + 16],
                              AF.Ln, bias=epscol[:, 0:1])
                sc.drain()
                ins = sc.activation(stat[:, sb_ + 20:sb_ + 24],
                                    stat[:, sb_ + 16:sb_ + 20], AF.Exp, scale=-0.5)
                sc.drain()
                aE.inc(ins)

            @blk.vector
            def _(v, l=l, st_=st_, sb_=sb_):
                v.wait_ge(aE.h, aE.n)
                v.scalar_tensor_tensor(stat[:, sb_ + 24:sb_ + 28],
                                       stat[:, sb_ + 8:sb_ + 12], -1.0,
                                       stat[:, sb_ + 20:sb_ + 24],
                                       op0=ALU.mult, op1=ALU.mult)
                v.drain()
                for q in range(4):
                    v.tensor_scalar(bscr[:, q * DIM:(q + 1) * DIM],
                                    hbt[:, q * DIM:(q + 1) * DIM],
                                    stat[:, sb_ + 20 + q:sb_ + 21 + q],
                                    stat[:, sb_ + 24 + q:sb_ + 25 + q],
                                    op0=ALU.mult, op1=ALU.add)
                v.drain()
                for q in range(4):
                    v.tensor_mul(bscr[:, q * DIM:(q + 1) * DIM],
                                 bscr[:, q * DIM:(q + 1) * DIM], bc(BC_L1W))
                v.drain()
                for q in range(4):
                    ins = v.tensor_add(bscr[:, q * DIM:(q + 1) * DIM],
                                       bscr[:, q * DIM:(q + 1) * DIM], bc(BC_L1B))
                vE.inc(ins)

            @blk.scalar
            def _(sc, l=l, st_=st_):
                sc.wait_ge(vE.h, vE.n)
                for q in range(4):
                    ti = st_ * 4 + q
                    if l * NBT + ti >= 4:
                        dY.wait_issue(sc, l * NBT + ti - 4)
                    ins = sc.activation(zbf[:, q * DIM:(q + 1) * DIM],
                                        bscr[:, q * DIM:(q + 1) * DIM], AF.Silu)
                    aE.inc(ins)

            @blk.sync
            def _(s, l=l, st_=st_):
                base_aE = aE.n - 4
                for q in range(4):
                    ti = st_ * 4 + q
                    s.wait_ge(aE.h, base_aE + q + 1)
                    dY.issue(s, lambda ti=ti, q=q: s.dma_start(
                        out=y_dram[ti * 128:(ti + 1) * 128],
                        in_=zbf[:, q * DIM:(q + 1) * DIM]))
                    if ti + 4 < NBT:
                        dHB.issue(s, lambda ti=ti, q=q: s.dma_start(
                            out=hbt[:, q * DIM:(q + 1) * DIM],
                            in_=h_dram[(ti + 4) * 128:(ti + 5) * 128]))

        barrier()

        @blk.sync
        def _(s, l=l):
            dY.wait_all(s)
            dYg.wait_all(s)
            for c in range(NCH):
                dCk.issue(s, lambda c=c: s.dma_start(
                    out=y126[0:126, c * DIM:(c + 1) * DIM],
                    in_=y_dram[c * CC: c * CC + CC]))
            if l == 0:
                for rep in range(NCH):
                    dmisc.dma(s.dma_start(
                        out=y126[126:128, rep * DIM:(rep + 1) * DIM],
                        in_=zrow[0:2, 0:DIM]))
                    s.wait_ge(dmisc.h, dmisc.n)

        thin_tE = tE.n
        thin_eTH = eTH.n

        @blk.tensor
        def _(t, l=l, thin_eTH=thin_eTH):
            dCk.wait_all(t)
            t.wait_ge(dmisc.h, dmisc.n)
            t.wait_ge(eZ.h, eZ.n)
            for ch in range(DIM):
                b = ch // 7
                if ch % 7 == 0 and b >= 2:
                    t.wait_ge(eTH.h, thin_eTH + b - 1)
                ps = pmE if b % 2 == 0 else pmF
                rhs = dataclasses.replace(
                    y126[:], offset=ch, ap=[list(y126[:].ap[0]), [DIM, NCH]])
                ins = t.matmul(ps[0:2, (ch % 7) * 66:(ch % 7) * 66 + NCH],
                               swpair[:, ch * 2: ch * 2 + 2], rhs,
                               start=True, stop=True)
                if ch % 7 == 6 or ch == DIM - 1:
                    tE.inc(ins)

        NBATCH = (DIM + 6) // 7

        thin_dTh0 = dTh.idx

        @blk.vector
        def _(v, l=l, thin_tE=thin_tE, thin_dTh0=thin_dTh0):
            for b in range(NBATCH):
                v.wait_ge(tE.h, thin_tE + b + 1)
                if b >= 2:
                    dTh.wait_issue(v, thin_dTh0 + b - 2)
                ps = pmE if b % 2 == 0 else pmF
                nch_b = min(7, DIM - b * 7)
                ins = v.tensor_copy(thst[0:2, (b % 2) * 462:(b % 2) * 462 + nch_b * 66],
                                    ps[0:2, 0:nch_b * 66])
                eTH.inc(ins)

        @blk.sync
        def _(s, l=l, thin_eTH=thin_eTH):
            for b in range(NBATCH):
                s.wait_ge(eTH.h, thin_eTH + b + 1)
                nch_b = min(7, DIM - b * 7)
                src = thst[0:2, (b % 2) * 462:(b % 2) * 462 + nch_b * 66]
                dst = dataclasses.replace(
                    thin_dram.flatten(), offset=b * 7 * 66,
                    ap=[[DIM * NCH, 2], [1, nch_b * 66]])
                dTh.issue(s, lambda src=src, dst=dst: s.dma_start(out=dst, in_=src))
            dTh.wait_all(s)
            for comp, dstt in ((0, sire), (1, siim)):
                srcv = dataclasses.replace(
                    thin_dram.flatten(), offset=comp * DIM * NCH,
                    ap=[[NCH, 128], [128 * NCH, 4], [1, NCH]])
                dstv = dataclasses.replace(
                    dstt[:], ap=[[4 * NCH, 128], [NCH, 4], [1, NCH]])
                dmisc.dma(s.dma_start(out=dstv, in_=srcv))
                s.wait_ge(dmisc.h, dmisc.n)
            dmisc.dma(s.dma_start(out=misc2_dram, in_=y126[0:2, 65 * DIM:66 * DIM]))
            s.wait_ge(dmisc.h, dmisc.n)
            srct = dataclasses.replace(misc2_dram.flatten(), offset=0,
                                       ap=[[1, 128], [DIM, 2], [128, 4]])
            dstt = dataclasses.replace(tl2bf[:], ap=[[8, 128], [4, 2], [1, 4]])
            dmisc.dma(s.dma_start(out=dstt, in_=srct, allow_slow_non_contiguous=True))
            s.wait_ge(dmisc.h, dmisc.n)

        barrier()

        @blk.vector
        def _(v, l=l):
            AR, AI = 0, 4 * NCH
            lvl = 1
            while lvl < NCH:
                for g in range(4):
                    ar = apowv[:, AR + g * NCH + lvl: AR + g * NCH + lvl + 1]
                    ai = apowv[:, AI + g * NCH + lvl: AI + g * NCH + lvl + 1]
                    w = NCH - lvl
                    srs = sire[:, g * NCH: g * NCH + w]
                    sis = siim[:, g * NCH: g * NCH + w]
                    v.tensor_scalar(sc1[:, g * NCH: g * NCH + w], srs, ar, None, op0=ALU.mult)
                    v.tensor_scalar(sc2[:, g * NCH: g * NCH + w], sis, ai, None, op0=ALU.mult)
                    v.tensor_scalar(si_inr[:, g * NCH: g * NCH + w], sis, ar, None, op0=ALU.mult)
                    v.tensor_scalar(si_ini[:, g * NCH: g * NCH + w], srs, ai, None, op0=ALU.mult)
                v.drain()
                for g in range(4):
                    w = NCH - lvl
                    dr = sire[:, g * NCH + lvl: g * NCH + NCH]
                    di = siim[:, g * NCH + lvl: g * NCH + NCH]
                    v.tensor_add(dr, dr, sc1[:, g * NCH: g * NCH + w])
                    v.tensor_add(di, di, si_inr[:, g * NCH: g * NCH + w])
                v.drain()
                for g in range(4):
                    w = NCH - lvl
                    dr = sire[:, g * NCH + lvl: g * NCH + NCH]
                    di = siim[:, g * NCH + lvl: g * NCH + NCH]
                    v.tensor_sub(dr, dr, sc2[:, g * NCH: g * NCH + w])
                    v.tensor_add(di, di, si_ini[:, g * NCH: g * NCH + w])
                v.drain()
                lvl *= 2
            v.tensor_copy(tl2[:, 0:8], tl2bf[:, 0:8])
            v.drain()
            PR, PI = 0, 4 * CCP
            for g in range(4):
                a2r = powc[:, PR + g * CCP + 2: PR + g * CCP + 3]
                a2i = powc[:, PI + g * CCP + 2: PI + g * CCP + 3]
                s64r = sire[:, g * NCH + 64: g * NCH + 65]
                s64i = siim[:, g * NCH + 64: g * NCH + 65]
                y0 = tl2[:, g:g + 1]
                y1 = tl2[:, 4 + g:5 + g]
                pr_ = pch[:, g:g + 1]
                pi_ = pch[:, 4 + g:5 + g]
                er = statev[:, g:g + 1]
                ei = statev[:, 4 + g:5 + g]
                v.tensor_scalar(er, s64r, a2r, None, op0=ALU.mult)
                v.tensor_scalar(sc1[:, 0:1], s64i, a2i, None, op0=ALU.mult)
                v.tensor_scalar(sc2[:, 0:1], y0, pr_, None, op0=ALU.mult)
                v.drain()
                v.tensor_sub(er, er, sc1[:, 0:1])
                v.drain()
                v.tensor_add(er, er, sc2[:, 0:1])
                v.drain()
                v.tensor_add(er, er, y1)
                v.tensor_scalar(ei, s64i, a2r, None, op0=ALU.mult)
                v.tensor_scalar(sc1[:, 0:1], s64r, a2i, None, op0=ALU.mult)
                v.tensor_scalar(sc2[:, 0:1], y0, pi_, None, op0=ALU.mult)
                v.drain()
                v.tensor_add(ei, ei, sc1[:, 0:1])
                v.drain()
                v.tensor_add(ei, ei, sc2[:, 0:1])
                v.drain()
            v.drain()
            vE.inc(v.tensor_copy(statev[:, 8:9], statev[:, 0:1]))

        @blk.sync
        def _(s, l=l):
            s.wait_ge(vE.h, vE.n)
            def cdma(mk):
                mk().then_inc(dm.h, 16)
                nc.sync.reg_add(sreg_dmc, sreg_dmc, 16)
                s.wait_ge(dm.h, sreg_dmc)
            with nc.sync.If(sreg_slot):
                srcv = dataclasses.replace(statev[:, 0:8],
                                           ap=[[statev[:].ap[0][0], 128], [4, 2], [1, 4]])
                dstv = dataclasses.replace(sh_carry.flatten(),
                                           offset=(l * 4 + 2) * DIM,
                                           ap=[[1, 128], [DIM, 2], [128, 4]])
                cdma(lambda srcv=srcv, dstv=dstv: s.dma_start(
                    out=dstv, in_=srcv, allow_slow_non_contiguous=True))
                cdma(lambda l=l: s.dma_start(
                    out=sh_flags[0:1, 2 * l + 1: 2 * l + 2],
                    in_=myflag_sb[0:1, 2 * l + 1: 2 * l + 2],
                    allow_slow_non_contiguous=True))
                nc.sync.reg_mov(sreg_cond, 1)
                nc.sync.reg_load(sreg_tgt, flagv_sb[0:1, 2 * l: 2 * l + 1])
                with nc.sync.While(sreg_cond):
                    cdma(lambda l=l: s.dma_start(
                        out=polltmp[0:1, 0:1],
                        in_=sh_flags[0:1, 2 * l: 2 * l + 1],
                        allow_slow_non_contiguous=True))
                    nc.sync.reg_load(sreg_flagv, polltmp[0:1, 0:1])
                    nc.sync.reg_alu(sreg_cond, sreg_flagv, sreg_tgt, op=ALU.not_equal)
                srcv2 = dataclasses.replace(sh_carry.flatten(),
                                            offset=(l * 4 + 0) * DIM,
                                            ap=[[1, 128], [DIM, 2], [128, 4]])
                dstv2 = dataclasses.replace(recvv[:, 0:8],
                                            ap=[[recvv[:].ap[0][0], 128], [4, 2], [1, 4]])
                cdma(lambda srcv2=srcv2, dstv2=dstv2: s.dma_start(
                    out=dstv2, in_=srcv2, allow_slow_non_contiguous=True))
            with nc.sync.Else():
                srcv = dataclasses.replace(statev[:, 0:8],
                                           ap=[[statev[:].ap[0][0], 128], [4, 2], [1, 4]])
                dstv = dataclasses.replace(sh_carry.flatten(),
                                           offset=(l * 4 + 0) * DIM,
                                           ap=[[1, 128], [DIM, 2], [128, 4]])
                cdma(lambda srcv=srcv, dstv=dstv: s.dma_start(
                    out=dstv, in_=srcv, allow_slow_non_contiguous=True))
                cdma(lambda l=l: s.dma_start(
                    out=sh_flags[0:1, 2 * l: 2 * l + 1],
                    in_=myflag_sb[0:1, 2 * l: 2 * l + 1],
                    allow_slow_non_contiguous=True))
            nc.sync.sem_inc(sE.h, 1)
        sE.n += 1

        @blk.vector
        def _(v, l=l):
            v.wait_ge(sE.h, sE.n)
            v.tensor_scalar(carryv[:, 0:8], recvv[:, 0:8], wsel[:, 0:1], None,
                            op0=ALU.mult)
            v.tensor_scalar(sc1[:, 0:8], lcvv[:, 0:8], wsel[:, 1:2], None,
                            op0=ALU.mult)
            v.drain()
            v.tensor_add(carryv[:, 0:8], carryv[:, 0:8], sc1[:, 0:8])
            v.drain()
            AR, AI = 0, 4 * NCH
            for g in range(4):
                cr = carryv[:, g:g + 1]
                ci = carryv[:, 4 + g:5 + g]
                apr = apowv[:, AR + g * NCH: AR + (g + 1) * NCH]
                api = apowv[:, AI + g * NCH: AI + (g + 1) * NCH]
                tr = si_inr[:, g * NCH:(g + 1) * NCH]
                ti_ = si_ini[:, g * NCH:(g + 1) * NCH]
                v.tensor_scalar(tr, apr, cr, None, op0=ALU.mult)
                v.tensor_scalar(sc1[:, g * NCH:(g + 1) * NCH], api, ci, None, op0=ALU.mult)
                v.tensor_scalar(ti_, api, cr, None, op0=ALU.mult)
                v.tensor_scalar(sc2[:, g * NCH:(g + 1) * NCH], apr, ci, None, op0=ALU.mult)
            v.drain()
            for g in range(4):
                tr = si_inr[:, g * NCH:(g + 1) * NCH]
                ti_ = si_ini[:, g * NCH:(g + 1) * NCH]
                v.tensor_sub(tr, tr, sc1[:, g * NCH:(g + 1) * NCH])
                v.tensor_add(ti_, ti_, sc2[:, g * NCH:(g + 1) * NCH])
            v.drain()
            for g in range(4):
                v.tensor_add(si_inr[:, g * NCH + 1:(g + 1) * NCH],
                             si_inr[:, g * NCH + 1:(g + 1) * NCH],
                             sire[:, g * NCH:(g + 1) * NCH - 1])
                v.tensor_add(si_ini[:, g * NCH + 1:(g + 1) * NCH],
                             si_ini[:, g * NCH + 1:(g + 1) * NCH],
                             siim[:, g * NCH:(g + 1) * NCH - 1])
            v.drain()
            for g in range(4):
                v.tensor_copy(zbf[:, g * 128: g * 128 + NCH],
                              si_inr[:, g * NCH:(g + 1) * NCH])
                v.tensor_copy(zbf[:, DIM + g * 128: DIM + g * 128 + NCH],
                              si_ini[:, g * NCH:(g + 1) * NCH])
            v.drain()
            vE.inc(v.tensor_copy(sc1[:, 0:1], sc1[:, 0:1]))

        siz_eZ = eZ.n
        siz_tE = tE.n

        @blk.tensor
        def _(t, l=l, siz_eZ=siz_eZ):
            t.wait_ge(vE.h, vE.n)
            for k in range(8):
                comp, g = k // 4, k % 4
                if k >= 2:
                    t.wait_ge(eZ.h, siz_eZ + k - 1)
                pt = ptA if k % 2 == 0 else ptB
                ins = t.transpose(pt[:, 0:128],
                                  zbf[:, comp * DIM + g * 128: comp * DIM + g * 128 + 128],
                                  ident[:])
                tE.inc(ins)

        @blk.vector
        def _(v, l=l, siz_tE=siz_tE):
            for k in range(8):
                comp, g = k // 4, k % 4
                v.wait_ge(tE.h, siz_tE + k + 1)
                pt = ptA if k % 2 == 0 else ptB
                ins = v.tensor_copy(sibf[0:NCH, comp * DIM + g * 128:
                                         comp * DIM + g * 128 + 128],
                                    pt[0:NCH, 0:128])
                eZ.inc(ins)

        @blk.sync
        def _(s, l=l):
            s.wait_ge(eZ.h, eZ.n)
            for comp in range(2):
                dmisc.dma(s.dma_start(
                    out=dataclasses.replace(si_dram.flatten(),
                                            offset=comp * NCH * DIM,
                                            ap=[[DIM, NCH], [1, DIM]]),
                    in_=sibf[0:NCH, comp * DIM:(comp + 1) * DIM]))
                s.wait_ge(dmisc.h, dmisc.n)
            for comp in range(2):
                dmisc.dma(s.dma_start(
                    out=y126[126 + comp:127 + comp, :],
                    in_=dataclasses.replace(si_dram.flatten(),
                                            offset=comp * NCH * DIM,
                                            ap=[[1, 1], [1, NCH * DIM]])))
                s.wait_ge(dmisc.h, dmisc.n)

        barrier()

        conv_eC = eC.n
        conv_tE = tE.n
        SLOT = 128

        @blk.scalar
        def _(sc, l=l):
            for g in range(NTG):
                par = g % 2
                if g >= 2:
                    sc.wait_ge(tE.h, conv_tE + (g - 1) * (TGRP // 4))
                toff = par * TGRP * 128
                for b in range(4):
                    k0 = 32 * b
                    rows = 32 if b < 3 else 30
                    s_b = (125 - k0) % MODP
                    srcv = dataclasses.replace(
                        big_dram.flatten(), offset=g * TGRP * BIGN + s_b,
                        ap=[[MODP - 1, rows], [BIGN, TGRP], [1, CC]])
                    base = tbuf[k0:k0 + rows, toff:toff + TGRP * 128]
                    dstv = dataclasses.replace(
                        base, ap=[[tbuf[:].ap[0][0], rows], [128, TGRP], [1, CC]])
                    dT.issue(sc, lambda dstv=dstv, srcv=srcv: sc.dma_start(
                        out=dstv, in_=srcv))
                ch0 = g * TGRP
                p0, gr = ch0 % 128, ch0 // 128
                for comp, srct in ((0, crre_bf), (1, crimn_bf)):
                    srcv = srct[p0:p0 + TGRP, gr * CC: gr * CC + CC]
                    base = tbuf[126 + comp:127 + comp, toff:toff + TGRP * 128]
                    dstv = dataclasses.replace(
                        base, ap=[[tbuf[:].ap[0][0], 1], [128, TGRP], [1, CC]])
                    dT.issue(sc, lambda dstv=dstv, srcv=srcv: sc.dma_start(
                        out=dstv, in_=srcv))

        @blk.tensor
        def _(t, l=l, conv_eC=conv_eC):
            t.wait_ge(dmisc.h, dmisc.n)
            t.wait_ge(eTH.h, eTH.n)
            for g in range(NTG):
                par = g % 2
                dT.wait_issue(t, l * NTG * 6 + g * 6 + 5)
                toff = par * TGRP * 128
                for ci in range(TGRP):
                    ch = g * TGRP + ci
                    bk = (g * TGRP + ci) // 4
                    if ci % 4 == 0 and bk >= 4:
                        t.wait_ge(eC.h, conv_eC + bk - 3)
                    ps = (pmA, pmB, pmE, pmF)[bk % 4]
                    rhs = dataclasses.replace(
                        y126[:], offset=ch, ap=[list(y126[:].ap[0]), [DIM, NCH]])
                    ins = t.matmul(ps[:, (ci % 4) * SLOT:(ci % 4) * SLOT + NCH],
                                   tbuf[:, toff + ci * 128: toff + (ci + 1) * 128],
                                   rhs, start=True, stop=True)
                    if ci % 4 == 3:
                        tE.inc(ins)

        @blk.vector
        def _(v, l=l, conv_tE=conv_tE, conv_eC=conv_eC):
            yw0 = l * 2 * NTG
            for bk in range(DIM // 4):
                v.wait_ge(tE.h, conv_tE + bk + 1)
                ps = (pmA, pmB, pmE, pmF)[bk % 4]
                g = (bk * 4) // TGRP
                cb = (bk * 4) % TGRP
                if cb == 0 and g >= 1:
                    dYW.wait_issue(v, yw0 + (g - 1) * 2 + 1)
                dst = dataclasses.replace(
                    yfst[:], offset=cb,
                    ap=[list(yfst[:].ap[0]), [1, 4], [TGRP, NCH]])
                src = dataclasses.replace(
                    ps[:], ap=[list(ps[:].ap[0]), [SLOT, 4], [1, NCH]])
                ins = v.tensor_copy(dst, src)
                eC.inc(ins)

        @blk.sync
        def _(s, l=l, conv_eC=conv_eC):
            for g in range(NTG):
                s.wait_ge(eC.h, conv_eC + (g + 1) * (TGRP // 4))
                for half in range(2):
                    c0 = half * (NCH // 2)
                    ncs = NCH // 2 if half == 0 else NCH - NCH // 2
                    srcv = dataclasses.replace(
                        yfst[0:126, :], offset=yfst[0:126, :].offset + c0 * TGRP,
                        ap=[[yfst[:].ap[0][0], 126], [TGRP, ncs], [1, TGRP]])
                    dstv = dataclasses.replace(
                        yf_dram.flatten(), offset=(c0 * CC) * DIM + g * TGRP,
                        ap=[[DIM, 126], [CC * DIM, ncs], [1, TGRP]])
                    dYW.issue(s, lambda dstv=dstv, srcv=srcv: s.dma_start(
                        out=dstv, in_=srcv))

        barrier()

        es23.close()
        esl2 = contextlib.ExitStack()
        w1T = esl2.enter_context(nc.sbuf_tensor(f"w1T{l}", [128, NDC * DFF], BF16))
        w2T = esl2.enter_context(nc.sbuf_tensor(f"w2T{l}", [128, NFC * DIM], BF16))
        y1b = esl2.enter_context(nc.sbuf_tensor(f"y1b{l}", [128, NFC * 512], BF16))
        hb2 = esl2.enter_context(nc.sbuf_tensor(f"hb2{l}", [128, 4 * DIM], F32))

        @blk.sync
        def _(s, l=l, w1T=w1T, w2T=w2T):
            s.wait_ge(dw_w1[l].h, 16)
            for dc in range(NDC):
                dPr.issue(s, lambda l=l, dc=dc, w1T=w1T: s.dma_start(
                    out=w1T[:, dc * DFF:(dc + 1) * DFF],
                    in_=w1bf_dram[l][:, dc * 128:(dc + 1) * 128], transpose=True))
            s.wait_ge(dw_w2[l].h, 16)
            for fc in range(NFC):
                dPr.issue(s, lambda l=l, fc=fc, w2T=w2T: s.dma_start(
                    out=w2T[:, fc * DIM:(fc + 1) * DIM],
                    in_=w2bf_dram[l][:, fc * 128:(fc + 1) * 128], transpose=True))

        @blk.tensor
        def _(t):
            dPr.wait_all(t)
            d_b2.wait_all(t)

        @blk.sync
        def _(s, l=l):
            dYW.wait_all(s)
            for ti in range(4):
                hb4[(l, ti)] = dHB.issue(s, lambda ti=ti: s.dma_start(
                    out=hbt[:, ti * DIM:(ti + 1) * DIM],
                    in_=h_dram[ti * 128:(ti + 1) * 128]))
                yf4[(l, ti)] = dYF.issue(s, lambda ti=ti: s.dma_start(
                    out=yfbt[:, ti * DIM:(ti + 1) * DIM],
                    in_=yf_dram[ti * 128:(ti + 1) * 128]))

        if l + 1 < n_layers:
            @blk.sync
            def _(s, l=l):
                dPr.issue(s, lambda l=l: s.dma_start(
                    out=lnlw_row[:], in_=ln1w_i[l + 1:l + 2]))
                dPr.issue(s, lambda l=l: s.dma_start(
                    out=lnlb_row[:], in_=ln1b_i[l + 1:l + 2]))

            @blk.tensor
            def _(t, l=l):
                dPr.wait_all(t)
                t.wait_ge(eC.h, eC.n)
                t.matmul(pmA[:, 0:DIM], onesf[0:1], lnlw_row[:],
                         start=True, stop=True)
                ins = t.matmul(pmB[:, 0:DIM], onesf[0:1], lnlb_row[:],
                               start=True, stop=True)
                tE.inc(ins)

            @blk.vector
            def _(v, l=l):
                v.wait_ge(tE.h, tE.n)
                v.tensor_copy(bc(BC_L1W), pmA[:, 0:DIM])
                vE.inc(v.tensor_copy(bc(BC_L1B), pmB[:, 0:DIM]))

        def emit_B4a(st_, l=None):
            sb_ = 16 + (st_ % 2) * 40

            @blk.vector
            def _(v, l=l, st_=st_, sb_=sb_):
                dYF.wait_issue(v, yf4[(l, st_ * 4 + 3)])
                dHB.wait_issue(v, hb4[(l, st_ * 4 + 3)])
                for q in range(4):
                    v.tensor_mul(yfbt[:, q * DIM:(q + 1) * DIM],
                                 yfbt[:, q * DIM:(q + 1) * DIM], bc(BC_SCL))
                v.drain()
                for q in range(4):
                    hv = hbt[:, q * DIM:(q + 1) * DIM]
                    v.tensor_add(hv, hv, yfbt[:, q * DIM:(q + 1) * DIM])
                v.drain()
                for q in range(4):
                    hv = hbt[:, q * DIM:(q + 1) * DIM]
                    v.reduce_sum(stat[:, sb_ + q:sb_ + q + 1], hv, axis=AX.X)
                    v.scalar_tensor_tensor(bscr[:, q * DIM:(q + 1) * DIM],
                                           hv, 1.0, hv, op0=ALU.mult, op1=ALU.mult,
                                           accum_out=stat[:, sb_ + 4 + q:sb_ + 5 + q])
                v.drain()
                v.tensor_scalar(stat[:, sb_ + 8:sb_ + 12], stat[:, sb_:sb_ + 4],
                                1.0 / DIM, None, op0=ALU.mult)
                v.drain()
                v.scalar_tensor_tensor(stat[:, sb_ + 12:sb_ + 16],
                                       stat[:, sb_ + 8:sb_ + 12], -1.0,
                                       stat[:, sb_ + 8:sb_ + 12],
                                       op0=ALU.mult, op1=ALU.mult)
                v.drain()
                vE.inc(v.scalar_tensor_tensor(stat[:, sb_ + 12:sb_ + 16],
                                              stat[:, sb_ + 4:sb_ + 8], 1.0 / DIM,
                                              stat[:, sb_ + 12:sb_ + 16],
                                              op0=ALU.mult, op1=ALU.add))

            @blk.scalar
            def _(sc, l=l, st_=st_, sb_=sb_):
                sc.wait_ge(vE.h, vE.n)
                sc.activation(stat[:, sb_ + 16:sb_ + 20], stat[:, sb_ + 12:sb_ + 16],
                              AF.Ln, bias=epscol[:, 0:1])
                sc.drain()
                ins = sc.activation(stat[:, sb_ + 20:sb_ + 24],
                                    stat[:, sb_ + 16:sb_ + 20], AF.Exp, scale=-0.5)
                sc.drain()
                aE.inc(ins)

            @blk.vector
            def _(v, l=l, st_=st_, sb_=sb_):
                v.wait_ge(aE.h, aE.n)
                if st_ >= 1:
                    dZB.wait_issue(v, zbA[l] + (st_ - 1) * 4 + 3)
                v.scalar_tensor_tensor(stat[:, sb_ + 24:sb_ + 28],
                                       stat[:, sb_ + 8:sb_ + 12], -1.0,
                                       stat[:, sb_ + 20:sb_ + 24],
                                       op0=ALU.mult, op1=ALU.mult)
                v.drain()
                for q in range(4):
                    v.tensor_scalar(bscr[:, q * DIM:(q + 1) * DIM],
                                    hbt[:, q * DIM:(q + 1) * DIM],
                                    stat[:, sb_ + 20 + q:sb_ + 21 + q],
                                    stat[:, sb_ + 24 + q:sb_ + 25 + q],
                                    op0=ALU.mult, op1=ALU.add)
                v.drain()
                for q in range(4):
                    v.tensor_mul(bscr[:, q * DIM:(q + 1) * DIM],
                                 bscr[:, q * DIM:(q + 1) * DIM], bc(BC_L2W))
                v.drain()
                for q in range(4):
                    ins = v.tensor_add(zbf[:, q * DIM:(q + 1) * DIM],
                                       bscr[:, q * DIM:(q + 1) * DIM], bc(BC_L2B))
                vE.inc(ins)

            @blk.scalar
            def _(sc, l=l, st_=st_):
                sc.wait_ge(vE.h, vE.n)
                for q in range(4):
                    ti = st_ * 4 + q
                    dZB.issue(sc, lambda ti=ti, q=q: sc.dma_start(
                        out=zzbf_dram[ti * 128:(ti + 1) * 128],
                        in_=zbf[:, q * DIM:(q + 1) * DIM]))
                for q in range(4):
                    ti = st_ * 4 + q
                    dHW.issue(sc, lambda ti=ti, q=q: sc.dma_start(
                        out=h_dram[ti * 128:(ti + 1) * 128],
                        in_=hbt[:, q * DIM:(q + 1) * DIM]))

            @blk.sync
            def _(s, l=l, st_=st_):
                for q in range(4):
                    ti = st_ * 4 + q
                    if ti + 4 < NBT:
                        dHW.wait_issue(s, NBT * l + ti)
                        hb4[(l, ti + 4)] = dHB.issue(s, lambda ti=ti: s.dma_start(
                            out=hbt[:, (ti % 4) * DIM:((ti % 4) + 1) * DIM],
                            in_=h_dram[(ti + 4) * 128:(ti + 5) * 128]))
                        yf4[(l, ti + 4)] = dYF.issue(s, lambda ti=ti: s.dma_start(
                            out=yfbt[:, (ti % 4) * DIM:((ti % 4) + 1) * DIM],
                            in_=yf_dram[(ti + 4) * 128:(ti + 5) * 128]))

        def emit_B4b(st_, l=None):
            zpar = st_ % 2

            @blk.sync
            def _(s, l=l, st_=st_, zpar=zpar):
                dZB.wait_issue(s, zbA[l] + st_ * 4 + 3)
                if st_ >= 2:
                    s.wait_ge(tE.h, f1_done[(l, st_ - 2)])
                zt_bases[(l, st_)] = dZT.idx
                for dc in range(NDC):
                    dZT.issue(s, lambda st_=st_, dc=dc, zpar=zpar: s.dma_start(
                        out=zT[:, zpar * 4 * DIM + dc * DIM:
                               zpar * 4 * DIM + (dc + 1) * DIM],
                        in_=zzbf_dram[st_ * 512:(st_ + 1) * 512,
                                      dc * 128:(dc + 1) * 128], transpose=True))
                if st_ >= 1:
                    dH2g.wait_issue(s, h2base[(l, st_ - 1)] + 3)
                if l + 1 < n_layers and st_ >= 1:
                    s.wait_ge(vE.h, b1v2[(l + 1, st_ - 1)])
                hb2base[(l, st_)] = dHB2.idx
                dHW.wait_issue(s, NBT * l + st_ * 4 + 3)
                for q in range(4):
                    ti = st_ * 4 + q
                    dHB2.issue(s, lambda ti=ti, q=q: s.dma_start(
                        out=hb2[:, q * DIM:(q + 1) * DIM],
                        in_=h_dram[ti * 128:(ti + 1) * 128]))

            f1_tE = tE.n
            f1_fE = fE.n

            @blk.tensor
            def _(t, l=l, st_=st_, f1_fE=f1_fE, zpar=zpar):
                dZT.wait_issue(t, zt_bases[(l, st_)] + NDC - 1)
                for fc in range(NFC):
                    if fc >= 2:
                        t.wait_ge(fE.h, f1_fE + fc - 1)
                    ps = pmC if fc % 2 == 0 else pmD
                    for dc in range(NDC):
                        ins = t.matmul(ps[:, 0:512],
                                       w1T[:, dc * DFF + fc * 128: dc * DFF + (fc + 1) * 128],
                                       zT[:, zpar * 4 * DIM + dc * DIM:
                                          zpar * 4 * DIM + (dc + 1) * DIM],
                                       start=(dc == 0), stop=(dc == NDC - 1))
                    tE.inc(ins)
                f1_done[(l, st_)] = tE.n

            @blk.scalar
            def _(sc, l=l, st_=st_, f1_tE=f1_tE):
                for fc in range(NFC):
                    sc.wait_ge(tE.h, f1_tE + fc + 1)
                    ps = pmC if fc % 2 == 0 else pmD
                    ins = sc.activation(y1b[:, fc * 512:(fc + 1) * 512], ps[:, 0:512],
                                        AF.Silu, bias=b1col[:, fc:fc + 1])
                    fE.inc(ins)

            f2_tE = tE.n
            f2_hE = hE.n

            @blk.tensor
            def _(t, l=l, st_=st_, f2_hE=f2_hE):
                t.wait_ge(fE.h, fE.n)
                for tc in range(4):
                    if tc >= 2:
                        t.wait_ge(hE.h, f2_hE + tc - 1)
                    ps = pmE if tc % 2 == 0 else pmF
                    for fc in range(NFC):
                        t.matmul(ps[:, 0:DIM],
                                 y1b[:, fc * 512 + tc * 128: fc * 512 + (tc + 1) * 128],
                                 w2T[:, fc * DIM:(fc + 1) * DIM],
                                 start=(fc == 0), stop=False)
                    ins = t.matmul(ps[:, 0:DIM], onesbf[:], b2_row[:],
                                   start=False, stop=True)
                    tE.inc(ins)

            @blk.vector
            def _(v, l=l, st_=st_, f2_tE=f2_tE):
                for tc in range(4):
                    v.wait_ge(tE.h, f2_tE + tc + 1)
                    dHB2.wait_issue(v, hb2base[(l, st_)] + tc)
                    ps = pmE if tc % 2 == 0 else pmF
                    ins = v.tensor_add(hb2[:, tc * DIM:(tc + 1) * DIM],
                                       hb2[:, tc * DIM:(tc + 1) * DIM], ps[:, 0:DIM])
                    hE.inc(ins)
            he_done[(l, st_)] = hE.n

            @blk.gpsimd
            def _(g_, l=l, st_=st_):
                h2base[(l, st_)] = dH2g.idx
                he0 = hE.n - 4
                for q in range(4):
                    ti = st_ * 4 + q
                    g_.wait_ge(hE.h, he0 + q + 1)
                    dH2g.issue(g_, lambda ti=ti, q=q: g_.dma_start(
                        out=h_dram[ti * 128:(ti + 1) * 128],
                        in_=hb2[:, q * DIM:(q + 1) * DIM]))

        def emit_B1p(k, l=l):
            lp = l + 1
            sb_ = 96 + (k % 2) * 28
            ycol = (k % 2) * 2 * 4 * DIM

            @blk.vector
            def _(v, k=k, lp=lp, sb_=sb_, ycol=ycol):
                v.wait_ge(hE.h, he_done[(l, k)])
                if k >= 2:
                    dYg.wait_issue(v, dyb1[(lp, (k - 2) * 4 + 3)])
                for q in range(4):
                    hv = hb2[:, q * DIM:(q + 1) * DIM]
                    v.reduce_sum(stat[:, sb_ + q:sb_ + q + 1], hv, axis=AX.X)
                    v.scalar_tensor_tensor(
                        y126[:, ycol + q * DIM: ycol + (q + 1) * DIM],
                        hv, 1.0, hv, op0=ALU.mult, op1=ALU.mult,
                        accum_out=stat[:, sb_ + 4 + q:sb_ + 5 + q])
                v.drain()
                v.tensor_scalar(stat[:, sb_ + 8:sb_ + 12], stat[:, sb_:sb_ + 4],
                                1.0 / DIM, None, op0=ALU.mult)
                v.drain()
                v.scalar_tensor_tensor(stat[:, sb_ + 12:sb_ + 16],
                                       stat[:, sb_ + 8:sb_ + 12], -1.0,
                                       stat[:, sb_ + 8:sb_ + 12],
                                       op0=ALU.mult, op1=ALU.mult)
                v.drain()
                vE.inc(v.scalar_tensor_tensor(stat[:, sb_ + 12:sb_ + 16],
                                              stat[:, sb_ + 4:sb_ + 8], 1.0 / DIM,
                                              stat[:, sb_ + 12:sb_ + 16],
                                              op0=ALU.mult, op1=ALU.add))

            @blk.scalar
            def _(sc, k=k, sb_=sb_):
                sc.wait_ge(vE.h, vE.n)
                sc.activation(stat[:, sb_ + 16:sb_ + 20], stat[:, sb_ + 12:sb_ + 16],
                              AF.Ln, bias=epscol[:, 0:1])
                sc.drain()
                ins = sc.activation(stat[:, sb_ + 20:sb_ + 24],
                                    stat[:, sb_ + 16:sb_ + 20], AF.Exp, scale=-0.5)
                sc.drain()
                aE.inc(ins)

            @blk.vector
            def _(v, k=k, lp=lp, sb_=sb_, ycol=ycol):
                v.wait_ge(aE.h, aE.n)
                v.scalar_tensor_tensor(stat[:, sb_ + 24:sb_ + 28],
                                       stat[:, sb_ + 8:sb_ + 12], -1.0,
                                       stat[:, sb_ + 20:sb_ + 24],
                                       op0=ALU.mult, op1=ALU.mult)
                v.drain()
                for q in range(4):
                    v.tensor_scalar(y126[:, ycol + q * DIM: ycol + (q + 1) * DIM],
                                    hb2[:, q * DIM:(q + 1) * DIM],
                                    stat[:, sb_ + 20 + q:sb_ + 21 + q],
                                    stat[:, sb_ + 24 + q:sb_ + 25 + q],
                                    op0=ALU.mult, op1=ALU.add)
                v.drain()
                for q in range(4):
                    v.tensor_mul(y126[:, ycol + q * DIM: ycol + (q + 1) * DIM],
                                 y126[:, ycol + q * DIM: ycol + (q + 1) * DIM],
                                 bc(BC_L1W))
                v.drain()
                for q in range(4):
                    ins = v.tensor_add(
                        y126[:, ycol + (4 + q) * DIM: ycol + (5 + q) * DIM],
                        y126[:, ycol + q * DIM: ycol + (q + 1) * DIM],
                        bc(BC_L1B))
                vE.inc(ins)
                b1v2[(lp, k)] = vE.n

            @blk.scalar
            def _(sc, k=k, lp=lp, ycol=ycol):
                sc.wait_ge(vE.h, b1v2[(lp, k)])
                for q in range(4):
                    ins = sc.activation(
                        y126[:, ycol + q * DIM: ycol + (q + 1) * DIM],
                        y126[:, ycol + (4 + q) * DIM: ycol + (5 + q) * DIM],
                        AF.Silu)
                    aE.inc(ins)

            @blk.gpsimd
            def _(g_, k=k, lp=lp, ycol=ycol):
                a0 = aE.n - 4
                for q in range(4):
                    ti = k * 4 + q
                    g_.wait_ge(aE.h, a0 + q + 1)
                    dyb1[(lp, ti)] = dYg.issue(g_, lambda ti=ti, q=q, ycol=ycol: g_.dma_start(
                        out=y_dram[ti * 128:(ti + 1) * 128],
                        in_=y126[:, ycol + q * DIM: ycol + (q + 1) * DIM]))

        zbA[l] = dZB.idx
        do_b1p = (l + 1 < n_layers)
        for k in range(NST):
            emit_B4a(k, l=l)
            if k >= 2:
                emit_B4b(k - 2, l=l)
                if do_b1p:
                    emit_B1p(k - 2, l=l)
        for k in (NST - 2, NST - 1):
            emit_B4b(k, l=l)
            if do_b1p:
                emit_B1p(k, l=l)

        @blk.sync
        def _(s, l=l):
            dH2.wait_all(s)
            dH2g.wait_all(s)
            dYg.wait_all(s)

        barrier()
        esl2.close()


    @blk.sync
    def _(s):
        dHW.wait_all(s)
        dY.wait_all(s)
    barrier()

    if "y" in dumps:
        @blk.gpsimd
        def _(g_):
            dw_base.dma(g_.dma_start(out=dump_t["y"], in_=y_dram))
            g_.wait_ge(dw_base.h, dw_base.n)
        barrier()
    if "yf" in dumps:
        @blk.sync
        def _(s):
            dmisc.dma(s.dma_start(out=dump_t["yf"], in_=yf_dram))
            s.wait_ge(dmisc.h, dmisc.n)
        barrier()
    if "h" in dumps:
        @blk.sync
        def _(s):
            dmisc.dma(s.dma_start(out=dump_t["h"], in_=h_dram))
            s.wait_ge(dmisc.h, dmisc.n)
        barrier()

    if n_layers < DEPTH or stop_after == "B":
        _emit_dummy_out(blk, nc, out_t, bcast_pool, dmisc, vE)
        esB.close()
        es.close()
        return nc

    esB.close()
    x1_dram = nc.dram_tensor("x1_dram", [NOUT, V], F32, kind="Internal").ap()
    houter_dram = nc.dram_tensor("houter_dram", [NOUT, DIM], F32, kind="Internal").ap()
    z2bf_dram = nc.dram_tensor("z2bf_dram", [NOUT, V], BF16, kind="Internal").ap()

    esC = contextlib.ExitStack()

    def sbufC(name, shape, dtype):
        return esC.enter_context(nc.sbuf_tensor(name, shape, dtype))

    houtb = sbufC("houtb", [128, 2 * DIM], F32)
    zcf = sbufC("zcf", [128, DIM], F32)
    zcbf = sbufC("zcbf", [128, DIM], BF16)
    zcT = sbufC("zcT", [128, DIM], BF16)
    woutT_sb = sbufC("woutT_sb", [128, NDC * V], BF16)
    boutbf = sbufC("boutbf", [1, V], BF16)
    abuf = sbufC("abuf", [128, V], F32)
    xob = sbufC("xob", [128, 2 * V], F32)
    z2bf = sbufC("z2bf", [128, V], BF16)

    dHC = SemRing(nc, "dHC", 2)
    dXO = SemRing(nc, "dXO", 2)
    dX1 = SemRing(nc, "dX1", 2)
    dZ2 = SemRing(nc, "dZ2", 2)
    NCT = NOUT // 128

    @blk.sync
    def _(s):
        def cdma(mk):
            mk().then_inc(dm.h, 16)
            nc.sync.reg_add(sreg_dmc, sreg_dmc, 16)
            s.wait_ge(dm.h, sreg_dmc)
        rvp = snap_pub()
        for blkrow in range(4):
            cdma(lambda blkrow=blkrow: s.dma_start(
                out=sh_h[bass.ds(rvp + blkrow * (NIN // 4), NIN // 4), :],
                in_=h_dram[blkrow * (NIN // 4):(blkrow + 1) * (NIN // 4)]))
        rvm = snap_mycol()
        cdma(lambda: s.dma_start(out=sh_flags[0:1, bass.ds(rvm, 1)],
                                 in_=myflag_sb[0:1, 14:15],
                                 allow_slow_non_contiguous=True))
        nc.sync.reg_mov(sreg_cond, 1)
        nc.sync.reg_load(sreg_tgt, flagv_sb[0:1, 15:16])
        with nc.sync.While(sreg_cond):
            rvpc = snap_pollc()
            cdma(lambda: s.dma_start(out=polltmp[0:1, 0:1],
                                     in_=sh_flags[0:1, bass.ds(rvpc, 1)],
                                     allow_slow_non_contiguous=True))
            nc.sync.reg_load(sreg_flagv, polltmp[0:1, 0:1])
            nc.sync.reg_alu(sreg_cond, sreg_flagv, sreg_tgt, op=ALU.not_equal)
        rvh = snap_hoff()
        cdma(lambda: s.dma_start(out=houter_dram,
                                 in_=sh_h[bass.ds(rvh, NOUT), :]))
        dPr.issue(s, lambda: s.dma_start(out=lnlw_row[:], in_=lnlw_i))
        dPr.issue(s, lambda: s.dma_start(out=lnlb_row[:], in_=lnlb_i))
        s.wait_ge(dw_wout.h, 16)
        for dc in range(NDC):
            dPr.issue(s, lambda dc=dc: s.dma_start(
                out=woutT_sb[:, dc * V:(dc + 1) * V],
                in_=woutbf_dram[:, dc * 128:(dc + 1) * 128], transpose=True))
        for ti in range(2):
            dHC.issue(s, lambda ti=ti: s.dma_start(
                out=houtb[:, (ti % 2) * DIM:((ti % 2) + 1) * DIM],
                in_=houter_dram[ti * 128:(ti + 1) * 128]))
            dXO.issue(s, lambda ti=ti: s.dma_start(
                out=xob[:, (ti % 2) * V:((ti % 2) + 1) * V],
                in_=x_outer[ti * 128:(ti + 1) * 128]))

    @blk.gpsimd
    def _(g_):
        d_b2.issue(g_, lambda: g_.dma_start(out=boutbf[:], in_=bout_i))

    @blk.tensor
    def _(t):
        dPr.wait_all(t)
        d_b2.wait_all(t)
        t.matmul(pmA[:, 0:DIM], onesf[0:1], lnlw_row[:], start=True, stop=True)
        ins = t.matmul(pmB[:, 0:DIM], onesf[0:1], lnlb_row[:], start=True, stop=True)
        tE.inc(ins)

    @blk.vector
    def _(v):
        v.wait_ge(tE.h, tE.n)
        v.tensor_copy(bc(BC_L1W), pmA[:, 0:DIM])
        vE.inc(v.tensor_copy(bc(BC_L1B), pmB[:, 0:DIM]))

    barrier()

    for ti in range(NCT):
        par = ti % 2
        sb_ = 16 + par * 40

        @blk.vector
        def _(v, ti=ti, par=par, sb_=sb_):
            dHC.wait_issue(v, ti)
            hv = houtb[:, par * DIM:(par + 1) * DIM]
            v.reduce_sum(stat[:, sb_:sb_ + 1], hv, axis=AX.X)
            v.scalar_tensor_tensor(zcf[:], hv, 1.0, hv, op0=ALU.mult,
                                   op1=ALU.mult, accum_out=stat[:, sb_ + 1:sb_ + 2])
            v.drain()
            v.tensor_scalar(stat[:, sb_ + 2:sb_ + 3], stat[:, sb_:sb_ + 1],
                            1.0 / DIM, None, op0=ALU.mult)
            v.drain()
            v.scalar_tensor_tensor(stat[:, sb_ + 3:sb_ + 4],
                                   stat[:, sb_ + 2:sb_ + 3], -1.0,
                                   stat[:, sb_ + 2:sb_ + 3],
                                   op0=ALU.mult, op1=ALU.mult)
            v.drain()
            vE.inc(v.scalar_tensor_tensor(stat[:, sb_ + 3:sb_ + 4],
                                          stat[:, sb_ + 1:sb_ + 2], 1.0 / DIM,
                                          stat[:, sb_ + 3:sb_ + 4],
                                          op0=ALU.mult, op1=ALU.add))

        @blk.scalar
        def _(sc, ti=ti, sb_=sb_):
            sc.wait_ge(vE.h, vE.n)
            sc.activation(stat[:, sb_ + 4:sb_ + 5], stat[:, sb_ + 3:sb_ + 4],
                          AF.Ln, bias=epscol[:, 0:1])
            sc.drain()
            ins = sc.activation(stat[:, sb_ + 5:sb_ + 6], stat[:, sb_ + 4:sb_ + 5],
                                AF.Exp, scale=-0.5)
            sc.drain()
            aE.inc(ins)

        @blk.vector
        def _(v, ti=ti, par=par, sb_=sb_):
            v.wait_ge(aE.h, aE.n)
            hv = houtb[:, par * DIM:(par + 1) * DIM]
            nm = stat[:, sb_ + 6:sb_ + 7]
            v.scalar_tensor_tensor(nm, stat[:, sb_ + 2:sb_ + 3], -1.0,
                                   stat[:, sb_ + 5:sb_ + 6],
                                   op0=ALU.mult, op1=ALU.mult)
            v.drain()
            v.tensor_scalar(zcf[:], hv, stat[:, sb_ + 5:sb_ + 6], nm,
                            op0=ALU.mult, op1=ALU.add)
            v.drain()
            v.tensor_mul(zcf[:], zcf[:], bc(BC_L1W))
            v.drain()
            v.tensor_add(zcf[:], zcf[:], bc(BC_L1B))
            v.drain()
            vE.inc(v.tensor_copy(zcbf[:], zcf[:]))

        ct_eZ = eZ.n
        ct_tE = tE.n

        @blk.tensor
        def _(t, ti=ti, ct_eZ=ct_eZ):
            t.wait_ge(vE.h, vE.n)
            for k in range(4):
                if k >= 2:
                    t.wait_ge(eZ.h, ct_eZ + k - 1)
                pt = ptA if k % 2 == 0 else ptB
                ins = t.transpose(pt[:, 0:128], zcbf[:, k * 128:(k + 1) * 128],
                                  ident[:])
                tE.inc(ins)

        @blk.scalar
        def _(sc, ti=ti, ct_tE=ct_tE):
            for k in range(4):
                sc.wait_ge(tE.h, ct_tE + k + 1)
                pt = ptA if k % 2 == 0 else ptB
                ins = sc.copy(zcT[:, k * 128:(k + 1) * 128], pt[:, 0:128])
                eZ.inc(ins)

        co_tE = tE.n
        co_aE = aE.n

        @blk.tensor
        def _(t, ti=ti, co_aE=co_aE):
            t.wait_ge(eZ.h, eZ.n)
            for og in range(8):
                if og >= 2:
                    t.wait_ge(aE.h, co_aE + og - 1)
                ps = pmC if og % 2 == 0 else pmD
                for dc in range(NDC):
                    t.matmul(ps[:, 0:512],
                             zcT[:, dc * 128:(dc + 1) * 128],
                             woutT_sb[:, dc * V + og * 512: dc * V + og * 512 + 512],
                             start=(dc == 0), stop=False)
                ins = t.matmul(ps[:, 0:512], onesbf[:],
                               boutbf[0:1, og * 512:(og + 1) * 512],
                               start=False, stop=True)
                tE.inc(ins)

        @blk.scalar
        def _(sc, ti=ti, co_tE=co_tE):
            if ti >= 1:
                dX1.wait_issue(sc, ti - 1)
            for og in range(8):
                sc.wait_ge(tE.h, co_tE + og + 1)
                ps = pmC if og % 2 == 0 else pmD
                ins = sc.copy(abuf[:, og * 512:(og + 1) * 512], ps[:, 0:512])
                aE.inc(ins)

        @blk.vector
        def _(v, ti=ti, par=par, sb_=sb_):
            v.wait_ge(aE.h, aE.n)
            dXO.wait_issue(v, ti)
            if ti >= 1:
                dZ2.wait_issue(v, ti - 1)
            v.reduce_sum(stat[:, sb_ + 7:sb_ + 8], abuf[:], axis=AX.X)
            v.scalar_tensor_tensor(z2bf[:], abuf[:], 1.0, abuf[:], op0=ALU.mult,
                                   op1=ALU.mult, accum_out=stat[:, sb_ + 8:sb_ + 9])
            v.drain()
            v.tensor_scalar(stat[:, sb_ + 9:sb_ + 10], stat[:, sb_ + 7:sb_ + 8],
                            1.0 / V, None, op0=ALU.mult)
            v.drain()
            v.scalar_tensor_tensor(stat[:, sb_ + 10:sb_ + 11],
                                   stat[:, sb_ + 9:sb_ + 10], -1.0,
                                   stat[:, sb_ + 9:sb_ + 10],
                                   op0=ALU.mult, op1=ALU.mult)
            v.drain()
            vE.inc(v.scalar_tensor_tensor(stat[:, sb_ + 10:sb_ + 11],
                                          stat[:, sb_ + 8:sb_ + 9], 1.0 / V,
                                          stat[:, sb_ + 10:sb_ + 11],
                                          op0=ALU.mult, op1=ALU.add))

        @blk.scalar
        def _(sc, ti=ti, sb_=sb_):
            sc.wait_ge(vE.h, vE.n)
            sc.activation(stat[:, sb_ + 11:sb_ + 12], stat[:, sb_ + 10:sb_ + 11],
                          AF.Ln, bias=epscol[:, 0:1])
            sc.drain()
            ins = sc.activation(stat[:, sb_ + 12:sb_ + 13], stat[:, sb_ + 11:sb_ + 12],
                                AF.Exp, scale=-0.5)
            sc.drain()
            aE.inc(ins)

        @blk.vector
        def _(v, ti=ti, par=par, sb_=sb_):
            v.wait_ge(aE.h, aE.n)
            nm = stat[:, sb_ + 13:sb_ + 14]
            v.scalar_tensor_tensor(nm, stat[:, sb_ + 9:sb_ + 10], -1.0,
                                   stat[:, sb_ + 12:sb_ + 13],
                                   op0=ALU.mult, op1=ALU.mult)
            v.drain()
            v.tensor_scalar(abuf[:], abuf[:], stat[:, sb_ + 12:sb_ + 13], nm,
                            op0=ALU.mult, op1=ALU.add)
            v.drain()
            v.tensor_add(abuf[:], abuf[:], xob[:, par * V:(par + 1) * V])
            v.drain()
            v.reduce_sum(stat[:, sb_ + 14:sb_ + 15], abuf[:], axis=AX.X)
            v.scalar_tensor_tensor(z2bf[:], abuf[:], 1.0, abuf[:], op0=ALU.mult,
                                   op1=ALU.mult, accum_out=stat[:, sb_ + 15:sb_ + 16])
            v.drain()
            v.tensor_scalar(stat[:, sb_ + 16:sb_ + 17], stat[:, sb_ + 14:sb_ + 15],
                            1.0 / V, None, op0=ALU.mult)
            v.drain()
            v.scalar_tensor_tensor(stat[:, sb_ + 17:sb_ + 18],
                                   stat[:, sb_ + 16:sb_ + 17], -1.0,
                                   stat[:, sb_ + 16:sb_ + 17],
                                   op0=ALU.mult, op1=ALU.mult)
            v.drain()
            vE.inc(v.scalar_tensor_tensor(stat[:, sb_ + 17:sb_ + 18],
                                          stat[:, sb_ + 15:sb_ + 16], 1.0 / V,
                                          stat[:, sb_ + 17:sb_ + 18],
                                          op0=ALU.mult, op1=ALU.add))

        @blk.scalar
        def _(sc, ti=ti, sb_=sb_):
            sc.wait_ge(vE.h, vE.n)
            sc.activation(stat[:, sb_ + 18:sb_ + 19], stat[:, sb_ + 17:sb_ + 18],
                          AF.Ln, bias=epscol[:, 0:1])
            sc.drain()
            ins = sc.activation(stat[:, sb_ + 19:sb_ + 20], stat[:, sb_ + 18:sb_ + 19],
                                AF.Exp, scale=-0.5)
            sc.drain()
            aE.inc(ins)

        @blk.vector
        def _(v, ti=ti, sb_=sb_):
            v.wait_ge(aE.h, aE.n)
            nm = stat[:, sb_ + 20:sb_ + 21]
            v.scalar_tensor_tensor(nm, stat[:, sb_ + 16:sb_ + 17], -1.0,
                                   stat[:, sb_ + 19:sb_ + 20],
                                   op0=ALU.mult, op1=ALU.mult)
            v.drain()
            v.tensor_scalar(z2bf[:], abuf[:], stat[:, sb_ + 19:sb_ + 20], nm,
                            op0=ALU.mult, op1=ALU.add)
            v.drain()
            vE.inc(v.tensor_copy(stat[:, sb_:sb_ + 1], stat[:, sb_:sb_ + 1]))

        @blk.sync
        def _(s, ti=ti):
            s.wait_ge(vE.h, vE.n)
            dX1.issue(s, lambda ti=ti: s.dma_start(
                out=x1_dram[ti * 128:(ti + 1) * 128], in_=abuf[:]))
            dZ2.issue(s, lambda ti=ti: s.dma_start(
                out=z2bf_dram[ti * 128:(ti + 1) * 128], in_=z2bf[:]))
            if ti + 2 < NCT:
                dHC.issue(s, lambda ti=ti: s.dma_start(
                    out=houtb[:, (ti % 2) * DIM:((ti % 2) + 1) * DIM],
                    in_=houter_dram[(ti + 2) * 128:(ti + 3) * 128]))
                dXO.issue(s, lambda ti=ti: s.dma_start(
                    out=xob[:, (ti % 2) * V:((ti % 2) + 1) * V],
                    in_=x_outer[(ti + 2) * 128:(ti + 3) * 128]))

    @blk.sync
    def _(s):
        dX1.wait_all(s)
        dZ2.wait_all(s)
    barrier()

    if "x1" in dumps:
        @blk.sync
        def _(s):
            dmisc.dma(s.dma_start(out=dump_t["x1"], in_=x1_dram))
            s.wait_ge(dmisc.h, dmisc.n)
        barrier()
    if "z2" in dumps:
        @blk.gpsimd
        def _(g_):
            dw_base.dma(g_.dma_start(out=dump_t["z2"], in_=z2bf_dram))
            g_.wait_ge(dw_base.h, dw_base.n)
        barrier()

    if stop_after == "C":
        _emit_dummy_out(blk, nc, out_t, bcast_pool, dmisc, vE)
        esC.close()
        es.close()
        return nc

    esC.close()
    ffn_dram = nc.dram_tensor("ffn_dram", [NOUT, V], BF16, kind="Internal").ap()

    esD1 = contextlib.ExitStack()

    def sbufD(name, shape, dtype):
        return esD1.enter_context(nc.sbuf_tensor(name, shape, dtype))

    HT = NOUT // 2
    NTT = HT // 128
    zdT = sbufD("zdT", [128, 32 * HT], BF16)
    gla = sbufD("gla", [128, 32 * 256], BF16)
    wsl = sbufD("wsl", [128, 2 * 32 * 512], BF16)
    lbTsb = sbufD("lbTsb", [128, 2 * V], BF16)
    bsb = sbufD("bsb", [128, 2 * 128], F32)
    bsbbf = sbufD("bsbbf", [128, 2 * 128], BF16)
    hAgT = sbufD("hAgT", [128, NTT * 256], BF16)
    hAgbf = sbufD("hAgbf", [128, 256], BF16)
    gha = sbufD("gha", [128, 256], F32)
    msk = sbufD("msk", [128, 2 * NE], F32)
    gsc = sbufD("gsc", [128, 8], F32)
    gt = sbufD("gt", [128, NE], F32)
    fevac = sbufD("fevac", [128, 2 * 512], BF16)

    @blk.sync
    def _(s):
        s.wait_ge(dw_gate.h, 16)
        s.wait_ge(dw_lora.h, 16)
        for dc in range(32):
            dPr.issue(s, lambda dc=dc: s.dma_start(
                out=gla[:, dc * 256: dc * 256 + 32],
                in_=gatebf_dram[:, dc * 128:(dc + 1) * 128], transpose=True))
            dPr.issue(s, lambda dc=dc: s.dma_start(
                out=gla[:, dc * 256 + 32:(dc + 1) * 256],
                in_=lorabf_dram[:, dc * 128:(dc + 1) * 128], transpose=True))

    lb_tE = tE.n
    lb_vE = vE.n
    lb_eZ = eZ.n
    lb_dXO = dXO.idx

    @blk.sync
    def _(s):
        for k in range(64):
            oc, ch = k // 2, k % 2
            e0 = ch * 16
            ne_ = 16 if ch == 0 else NE - 16
            srcb = dataclasses.replace(
                loraB_i.flatten(), offset=e0 * V * RK + oc * 128 * RK,
                ap=[[RK, 128], [V * RK, ne_], [1, RK]])
            dXO.issue(s, lambda k=k, srcb=srcb, ne_=ne_: s.dma_start(
                out=bsb[:, (k % 2) * 128: (k % 2) * 128 + ne_ * RK], in_=srcb))

    @blk.vector
    def _(v):
        v.memset(bsb[:], 0.0)
        v.drain()
        for k in range(64):
            dXO.wait_issue(v, lb_dXO + k)
            if k >= 2:
                v.wait_ge(tE.h, lb_tE + k - 1)
            ins = v.tensor_copy(bsbbf[:, (k % 2) * 128:(k % 2 + 1) * 128],
                                bsb[:, (k % 2) * 128:(k % 2 + 1) * 128])
            vE.inc(ins)

    @blk.tensor
    def _(t):
        for k in range(64):
            t.wait_ge(vE.h, lb_vE + k + 1)
            if k >= 2:
                t.wait_ge(eZ.h, lb_eZ + k - 1)
            pt = ptA if k % 2 == 0 else ptB
            ins = t.transpose(pt[:, 0:128],
                              bsbbf[:, (k % 2) * 128:(k % 2 + 1) * 128],
                              ident[:])
            tE.inc(ins)

    @blk.scalar
    def _(sc):
        for k in range(64):
            oc, ch = k // 2, k % 2
            sc.wait_ge(tE.h, lb_tE + k + 1)
            pt = ptA if k % 2 == 0 else ptB
            if ch == 0:
                ins = sc.copy(lbTsb[:, oc * 128:(oc + 1) * 128], pt[:, 0:128])
            else:
                ins = sc.copy(lbTsb[0:96, V + oc * 128: V + (oc + 1) * 128],
                              pt[0:96, 0:128])
            sc.drain()
            eZ.inc(ins)

    @blk.vector
    def _(v):
        vE.inc(v.memset(hAgbf[:], 0.0))

    barrier()

    dYW_D0 = dYW.idx
    wsl_base = {}
    lbT_base = {}

    for half in range(2):
        zd_base = dCk.idx

        @blk.sync
        def _(s, half=half):
            for dc in range(32):
                dCk.issue(s, lambda dc=dc, half=half: s.dma_start(
                    out=zdT[:, dc * HT:(dc + 1) * HT],
                    in_=z2bf_dram[half * HT: (half + 1) * HT,
                                  dc * 128:(dc + 1) * 128], transpose=True))
            if half == 0:
                s.wait_ge(dw_base.h, 16)
                wsl_base[(0, 0)] = dT.idx
                for dc in range(32):
                    dT.issue(s, lambda dc=dc: s.dma_start(
                        out=wsl[:, dc * 512:(dc + 1) * 512],
                        in_=basebf_dram[0:512, dc * 128:(dc + 1) * 128],
                        transpose=True))
            else:
                s.wait_ge(tE.h, tE.n)

        mm_done = {}
        tr_done = {}
        v1_done = {}
        v2_done = {}
        ez_done = {-1: eZ.n, -2: eZ.n}

        def emit_transpose(t, tt):
            t.wait_ge(vE.h, v2_done[tt])
            if tt >= 2:
                t.wait_ge(eZ.h, ez_done[tt - 2])
            for k in range(2):
                pt = ptA if k == 0 else ptB
                ins = t.transpose(pt[:, 0:128],
                                  hAgbf[:, k * 128:(k + 1) * 128], ident[:])
                tE.inc(ins)
            tr_done[tt] = tE.n

        def emit_evac(sc, tt):
            sc.wait_ge(tE.h, tr_done[tt] - 1)
            sc.copy(hAgT[:, tt * 256: tt * 256 + 128], ptA[:, 0:128])
            sc.wait_ge(tE.h, tr_done[tt])
            ins = sc.copy(hAgT[0:96, tt * 256 + 128: tt * 256 + 256],
                          ptB[0:96, 0:128])
            sc.drain()
            eZ.inc(ins)
            ez_done[tt] = eZ.n

        for tt in range(NTT):
            @blk.tensor
            def _(t, half=half, tt=tt, zd_base=zd_base):
                if tt == 0:
                    if half == 0:
                        dPr.wait_all(t)
                    dCk.wait_issue(t, zd_base + 31)
                if tt >= 2:
                    t.wait_ge(vE.h, v1_done[tt - 2])
                ps = pmA if tt % 2 == 0 else pmB
                for dc in range(32):
                    ins = t.matmul(ps[:, 0:256],
                                   zdT[:, dc * HT + tt * 128: dc * HT + (tt + 1) * 128],
                                   gla[:, dc * 256:(dc + 1) * 256],
                                   start=(dc == 0), stop=(dc == 31))
                tE.inc(ins)
                mm_done[tt] = tE.n
                if tt >= 1:
                    emit_transpose(t, tt - 1)

            @blk.vector
            def _(v, half=half, tt=tt):
                v.wait_ge(tE.h, mm_done[tt])
                ps = pmA if tt % 2 == 0 else pmB
                v.tensor_copy(gha[:], ps[:, 0:256])
                v.drain()
                v.memset(gha[:, NE:32], -1e30)
                v.drain()
                v.reduce_max(gsc[:, 0:1], gha[:, 0:32], axis=AX.X)
                v.drain()
                v.tensor_scalar(msk[:, 0:NE], gha[:, 0:NE], gsc[:, 0:1], None,
                                op0=ALU.is_ge)
                v.drain()
                v.scalar_tensor_tensor(msk[:, NE:2 * NE], msk[:, 0:NE], -1e30,
                                       gha[:, 0:NE], op0=ALU.mult, op1=ALU.add)
                v.drain()
                v.reduce_max(gsc[:, 1:2], msk[:, NE:2 * NE], axis=AX.X)
                v.drain()
                v.tensor_scalar(msk[:, NE:2 * NE], msk[:, NE:2 * NE],
                                gsc[:, 1:2], None, op0=ALU.is_ge)
                v.tensor_sub(gsc[:, 2:3], gsc[:, 0:1], gsc[:, 1:2])
                v.tensor_sub(gsc[:, 3:4], gsc[:, 1:2], gsc[:, 0:1])
                v.drain()
                vE.inc(v.tensor_copy(gsc[:, 6:7], gsc[:, 0:1]))
            v1_done[tt] = vE.n

            @blk.scalar
            def _(sc, half=half, tt=tt):
                sc.wait_ge(vE.h, v1_done[tt])
                sc.activation(gsc[:, 4:5], gsc[:, 2:3], AF.Sigmoid)
                ins = sc.activation(gsc[:, 5:6], gsc[:, 3:4], AF.Sigmoid)
                sc.drain()
                aE.inc(ins)
                if tt >= 1:
                    emit_evac(sc, tt - 1)

            @blk.vector
            def _(v, half=half, tt=tt):
                v.wait_ge(aE.h, aE.n)
                if tt >= 1:
                    v.wait_ge(tE.h, tr_done[tt - 1])
                v.tensor_scalar(gt[:], msk[:, 0:NE], gsc[:, 4:5], None,
                                op0=ALU.mult)
                v.tensor_scalar(msk[:, NE:2 * NE], msk[:, NE:2 * NE],
                                gsc[:, 5:6], None, op0=ALU.mult)
                v.drain()
                v.tensor_add(gt[:], gt[:], msk[:, NE:2 * NE])
                v.drain()
                v.tensor_scalar(gt[:], gt[:], SCALING, None, op0=ALU.mult)
                v.drain()
                gv = dataclasses.replace(
                    gt[:], ap=[list(gt[:].ap[0]), [1, NE], [0, RK]])
                v.tensor_mul(hAgbf[:, 0:ER], gha[:, 32:256], gv)
                v.drain()
                vE.inc(v.tensor_copy(gsc[:, 7:8], gsc[:, 0:1]))
            v2_done[tt] = vE.n

        @blk.tensor
        def _(t, half=half):
            emit_transpose(t, NTT - 1)

        @blk.scalar
        def _(sc, half=half):
            emit_evac(sc, NTT - 1)

        for og in range(8):
            par = og % 2
            o_tE = tE.n
            o_eC = eC.n

            @blk.tensor
            def _(t, half=half, og=og, o_eC=o_eC,
                  wb=None, lb=None):
                wb = wsl_base[(half, og)]
                if og == 0:
                    t.wait_ge(eZ.h, eZ.n)
                dT.wait_issue(t, wb + 31)
                for tt in range(NTT):
                    if tt >= 2:
                        t.wait_ge(eC.h, o_eC + tt - 1)
                    ps = pmC if tt % 2 == 0 else pmD
                    for dc in range(32):
                        t.matmul(ps[:, 0:512],
                                 zdT[:, dc * HT + tt * 128: dc * HT + (tt + 1) * 128],
                                 wsl[:, par * 32 * 512 + dc * 512:
                                     par * 32 * 512 + (dc + 1) * 512],
                                 start=(dc == 0), stop=False)
                    t.matmul(ps[:, 0:512], hAgT[:, tt * 256: tt * 256 + 128],
                             lbTsb[:, og * 512:(og + 1) * 512],
                             start=False, stop=False)
                    ins = t.matmul(ps[:, 0:512],
                                   hAgT[0:96, tt * 256 + 128: tt * 256 + 256],
                                   lbTsb[0:96, V + og * 512: V + (og + 1) * 512],
                                   start=False, stop=True)
                    tE.inc(ins)

            @blk.scalar
            def _(sc, half=half, og=og, o_tE=o_tE):
                for tt in range(NTT):
                    sc.wait_ge(tE.h, o_tE + tt + 1)
                    gidx = (half * 8 + og) * NTT + tt
                    if gidx >= 2:
                        dYW.wait_issue(sc, dYW_D0 + gidx - 2)
                    ps = pmC if tt % 2 == 0 else pmD
                    ins = sc.copy(fevac[:, (tt % 2) * 512:((tt % 2) + 1) * 512],
                                  ps[:, 0:512])
                    sc.drain()
                    eC.inc(ins)

            @blk.sync
            def _(s, half=half, og=og, o_eC=o_eC, o_tE=o_tE):
                for tt in range(NTT):
                    s.wait_ge(eC.h, o_eC + tt + 1)
                    dYW.issue(s, lambda tt=tt, half=half, og=og: s.dma_start(
                        out=ffn_dram[half * HT + tt * 128: half * HT + (tt + 1) * 128,
                                     og * 512:(og + 1) * 512],
                        in_=fevac[:, (tt % 2) * 512:((tt % 2) + 1) * 512]))
                nx = half * 8 + og + 1
                if nx < 16:
                    og2 = nx % 8
                    h2 = nx // 8
                    s.wait_ge(tE.h, o_tE)
                    wsl_base[(h2, og2)] = dT.idx
                    for dc in range(32):
                        dT.issue(s, lambda dc=dc, og2=og2: s.dma_start(
                            out=wsl[:, ((og + 1) % 2) * 32 * 512 + dc * 512:
                                    ((og + 1) % 2) * 32 * 512 + (dc + 1) * 512],
                            in_=basebf_dram[og2 * 512:(og2 + 1) * 512,
                                            dc * 128:(dc + 1) * 128],
                            transpose=True))


    @blk.sync
    def _(s):
        dYW.wait_all(s)
    barrier()
    esD1.close()

    if "ffn" in dumps:
        @blk.gpsimd
        def _(g_):
            dw_base.dma(g_.dma_start(out=dump_t["ffn"], in_=ffn_dram))
            g_.wait_ge(dw_base.h, dw_base.n)
        barrier()

    esD2 = contextlib.ExitStack()
    fbufbf = esD2.enter_context(nc.sbuf_tensor("fbufbf", [128, 2 * V], BF16))
    x1t = esD2.enter_context(nc.sbuf_tensor("x1t", [128, 2 * V], F32))
    obuf = esD2.enter_context(nc.sbuf_tensor("obuf", [128, V], F32))

    f_dHB = dHB.idx
    f_dXO = dXO.idx
    f_dY = dY.idx

    @blk.sync
    def _(s):
        for fi in range(2):
            dHB.issue(s, lambda fi=fi: s.dma_start(
                out=fbufbf[:, (fi % 2) * V:((fi % 2) + 1) * V],
                in_=ffn_dram[fi * 128:(fi + 1) * 128]))
            dXO.issue(s, lambda fi=fi: s.dma_start(
                out=x1t[:, (fi % 2) * V:((fi % 2) + 1) * V],
                in_=x1_dram[fi * 128:(fi + 1) * 128]))

    for fi in range(NCT):
        par = fi % 2
        sb_ = 16 + par * 40

        @blk.vector
        def _(v, fi=fi, par=par, sb_=sb_):
            dHB.wait_issue(v, f_dHB + fi)
            fv = fbufbf[:, par * V:(par + 1) * V]
            if fi >= 1:
                dY.wait_issue(v, f_dY + fi - 1)
            v.reduce_sum(stat[:, sb_:sb_ + 1], fv, axis=AX.X)
            v.scalar_tensor_tensor(obuf[:], fv, 1.0, fv, op0=ALU.mult,
                                   op1=ALU.mult, accum_out=stat[:, sb_ + 1:sb_ + 2])
            v.drain()
            v.tensor_scalar(stat[:, sb_ + 2:sb_ + 3], stat[:, sb_:sb_ + 1],
                            1.0 / V, None, op0=ALU.mult)
            v.drain()
            v.scalar_tensor_tensor(stat[:, sb_ + 3:sb_ + 4],
                                   stat[:, sb_ + 2:sb_ + 3], -1.0,
                                   stat[:, sb_ + 2:sb_ + 3],
                                   op0=ALU.mult, op1=ALU.mult)
            v.drain()
            vE.inc(v.scalar_tensor_tensor(stat[:, sb_ + 3:sb_ + 4],
                                          stat[:, sb_ + 1:sb_ + 2], 1.0 / V,
                                          stat[:, sb_ + 3:sb_ + 4],
                                          op0=ALU.mult, op1=ALU.add))

        @blk.scalar
        def _(sc, fi=fi, sb_=sb_):
            sc.wait_ge(vE.h, vE.n)
            sc.activation(stat[:, sb_ + 4:sb_ + 5], stat[:, sb_ + 3:sb_ + 4],
                          AF.Ln, bias=epscol[:, 0:1])
            sc.drain()
            ins = sc.activation(stat[:, sb_ + 5:sb_ + 6], stat[:, sb_ + 4:sb_ + 5],
                                AF.Exp, scale=-0.5)
            sc.drain()
            aE.inc(ins)

        @blk.vector
        def _(v, fi=fi, par=par, sb_=sb_):
            v.wait_ge(aE.h, aE.n)
            dXO.wait_issue(v, f_dXO + fi)
            fv = fbufbf[:, par * V:(par + 1) * V]
            nm = stat[:, sb_ + 6:sb_ + 7]
            v.scalar_tensor_tensor(nm, stat[:, sb_ + 2:sb_ + 3], -1.0,
                                   stat[:, sb_ + 5:sb_ + 6],
                                   op0=ALU.mult, op1=ALU.mult)
            v.drain()
            v.tensor_scalar(obuf[:], fv, stat[:, sb_ + 5:sb_ + 6], nm,
                            op0=ALU.mult, op1=ALU.add)
            v.drain()
            v.tensor_add(obuf[:], obuf[:], x1t[:, par * V:(par + 1) * V])
            v.drain()
            vE.inc(v.tensor_copy(stat[:, sb_:sb_ + 1], stat[:, sb_:sb_ + 1]))

        @blk.sync
        def _(s, fi=fi):
            s.wait_ge(vE.h, vE.n)
            dY.issue(s, lambda fi=fi: s.dma_start(
                out=out_t[fi * 128:(fi + 1) * 128], in_=obuf[:]))
            if fi + 2 < NCT:
                dHB.issue(s, lambda fi=fi: s.dma_start(
                    out=fbufbf[:, (fi % 2) * V:((fi % 2) + 1) * V],
                    in_=ffn_dram[(fi + 2) * 128:(fi + 3) * 128]))
                dXO.issue(s, lambda fi=fi: s.dma_start(
                    out=x1t[:, (fi % 2) * V:((fi % 2) + 1) * V],
                    in_=x1_dram[(fi + 2) * 128:(fi + 3) * 128]))

    @blk.sync
    def _(s):
        dY.wait_all(s)
    barrier()
    esD2.close()
    es.close()
    return nc


def _emit_dummy_out(blk, nc, out_t, scratch, dm, vE):
    @blk.vector
    def _(v):
        v.memset(scratch[:, 0:DIM], 0.0)
        v.drain()
        vE.inc(v.tensor_copy(scratch[:, 0:1], scratch[:, 0:1]))

    @blk.sync
    def _(s):
        s.wait_ge(vE.h, vE.n)
        for i in range(NOUT // 128):
            for j in range(V // DIM):
                dm.dma(s.dma_start(out=out_t[i * 128:(i + 1) * 128,
                                             j * DIM:(j + 1) * DIM],
                                   in_=scratch[:, 0:DIM]))
                s.wait_ge(dm.h, dm.n)


def _prep_inputs(inputs, nonce=None):
    x = np.ascontiguousarray(np.asarray(inputs["L0_attn_x"])[0])
    gate_pad = np.zeros((32, V), np.float32)
    gate_pad[:NE] = np.asarray(inputs["L0_ffn_gate_W"])
    loraA_flat = np.ascontiguousarray(
        np.asarray(inputs["L0_ffn_lora_A"]).reshape(ER, V))
    if nonce is None:
        nonce = int(np.random.randint(1 << 20, 1 << 30))
    common = dict(
        Win_i=inputs["L0_attn_Win"],
        bin_i=np.asarray(inputs["L0_attn_bin"])[None, :],
        ln1w_i=inputs["L0_attn_ln1_w"], ln1b_i=inputs["L0_attn_ln1_b"],
        phzr_i=inputs["L0_attn_phz_re"], phzi_i=inputs["L0_attn_phz_im"],
        lcr_i=inputs["L0_attn_lc_re"], lci_i=inputs["L0_attn_lc_im"],
        scl_i=inputs["L0_attn_sc_lin"],
        ln2w_i=inputs["L0_attn_ln2_w"], ln2b_i=inputs["L0_attn_ln2_b"],
        W1_i=inputs["L0_attn_W1"], b1_i=inputs["L0_attn_b1"],
        W2_i=inputs["L0_attn_W2"], b2_i=inputs["L0_attn_b2"],
        lnlw_i=np.asarray(inputs["L0_attn_lnl_w"])[None, :],
        lnlb_i=np.asarray(inputs["L0_attn_lnl_b"])[None, :],
        Wout_i=inputs["L0_attn_Wout"],
        bout_i=np.asarray(inputs["L0_attn_bout"])[None, :],
        baseW_i=inputs["L0_ffn_base_W"],
        gateW_i=gate_pad,
        loraA_i=loraA_flat,
        loraB_i=inputs["L0_ffn_lora_B"],
    )
    common = {k: np.ascontiguousarray(np.asarray(v, np.float32))
              for k, v in common.items()}
    in_maps = []
    for r in range(8):
        erole = (r % 2 == 0)
        inner_lo = 0 if erole else NIN
        outer_lo = NOUT * r
        own = inner_lo <= outer_lo and outer_lo + NOUT <= inner_lo + NIN
        myslot = 0 if erole else 1
        rolef = np.zeros((1, 8), np.float32)
        rolef[0, 0] = 0.0 if erole else 1.0
        rolef[0, 1] = 0.0 if own else 1.0
        rolef[0, 2] = float(myslot)
        flags = np.zeros((1, 16), np.int32)
        myflag = np.zeros((1, 16), np.int32)
        for l in range(DEPTH):
            flags[0, 2 * l + (1 - myslot)] = nonce + 2 * l + (1 - myslot)
            myflag[0, 2 * l + myslot] = nonce + 2 * l + myslot
        flags[0, 8 + (1 - myslot)] = nonce + 8 + (1 - myslot)
        flags[0, 8 + myslot] = nonce + 8 + myslot
        myflag[0, 8 + myslot] = nonce + 8 + myslot
        pollcol = (8 + myslot) if own else (8 + (1 - myslot))
        flags[0, 10] = 8 + myslot
        flags[0, 11] = pollcol
        flags[0, 12] = myslot
        flags[0, 13] = 0 if own else 1
        flags[0, 14] = NOUT * r
        flags[0, 15] = nonce + pollcol
        myflag[0, 14] = nonce + 8 + myslot
        myflag[0, 15] = NIN * myslot
        m = dict(common)
        m["x_inner"] = np.ascontiguousarray(x[inner_lo:inner_lo + NIN])
        m["x_outer"] = np.ascontiguousarray(x[outer_lo:outer_lo + NOUT])
        m["rolef_i"] = rolef
        m["flags_i"] = flags.copy()
        m["myflag_i"] = myflag.copy()
        in_maps.append(m)
    return in_maps


def kernel(**inputs):
    in_maps = _prep_inputs(inputs)
    nc = build_program()
    res = run_bass_kernel_spmd(nc, in_maps, core_ids=list(range(8)))
    out = np.concatenate([np.asarray(res.results[r]["out"]) for r in range(8)],
                         axis=0)
    return out[None, :, :].astype(np.float32)
